# version 26
# baseline (speedup 1.0000x reference)
from contextlib import ExitStack

import numpy as np
import ml_dtypes
import concourse.bass as bass
import concourse.mybir as mybir
from concourse.bass_utils import run_bass_kernel_spmd

F32 = mybir.dt.float32
BF16 = mybir.dt.bfloat16
AF = mybir.ActivationFunctionType
ALU = mybir.AluOpType
AX = mybir.AxisListType

L = 2048
D = 2048
H = 16
HK = 8
HD = 128
C = 64
NC = 32
KC = 16
NT = 4
TP = 16
DFF = 8192
FB = 64
NI = 30
RIDGE = 0.02
NCORES = 8

DEBUG = False


class Flow:
    OPENGS = ("tensor", "vector", "scalar", "gpsimd")
    DMAENGS = ("sync", "gpsimd", "scalar")
    SERIAL = False

    def __init__(self, nc):
        self.nc = nc
        self.steps = []
        self.counts = {}
        self.track = {}
        self.barrier_deps = []

    @staticmethod
    def _key(h):
        return h if isinstance(h, str) else id(h)

    def _emit(self, eng, kind, fn, r, w, extra):
        deps = {}

        def add(tok):
            if tok[1] == "dma":
                k = (tok[0], "dma", tok[2] % Flow.DPOOL)
            else:
                k = (tok[0], "op")
            if k not in deps or deps[k][2] < tok[2]:
                deps[k] = tok
        for t in self.barrier_deps:
            add(t)
        for t in (extra or ()):
            add(t)
        if Flow.SERIAL and self.steps:
            add(self.steps[-1][4])
        for h in (r or ()):
            ent = self.track.get(self._key(h))
            if ent:
                for t in ent["w"]:
                    add(t)
        for h in (w or ()):
            ent = self.track.get(self._key(h))
            if ent:
                for t in ent["w"]:
                    add(t)
                for t in ent["r"]:
                    add(t)
        gidx = len(self.steps)
        idx = self.counts.get((eng, kind), 0)
        self.counts[(eng, kind)] = idx + 1
        tok = (eng, kind, idx, gidx)
        dl = [t for t in deps.values() if t[3] < gidx]
        self.steps.append((eng, kind, fn, dl, tok))
        for h in (r or ()):
            self.track.setdefault(self._key(h), {"w": [], "r": []})["r"].append(tok)
        for h in (w or ()):
            ent = self.track.setdefault(self._key(h), {"w": [], "r": []})
            if ent["r"]:
                ent["w"] = [tok]
                ent["r"] = []
            else:
                ent["w"].append(tok)
        return tok

    def op(self, eng, fn, r=(), w=(), deps=()):
        return self._emit(eng, "op", fn, r, w, deps)

    def dma(self, eng, out, in_, r=(), w=(), deps=(), **kw):
        return self._emit(eng, "dma",
                          lambda e: e.dma_start(out, in_, **kw), r, w, deps)

    def barrier(self):
        last = {}
        for (eng, kind, _, _, tok) in self.steps:
            last[(eng, kind)] = tok
        self.barrier_deps = list(last.values())

    DPOOL = 16

    def emit(self):
        nc = self.nc
        steps = self.steps
        sem_keys = sorted(self.counts.keys())
        guards = []
        sems = {}
        finals = {}
        for k in sem_keys:
            if k[1] == "op":
                g = nc.semaphore(f"fl_{k[0]}_op")
                sems[k] = g.__enter__()
                guards.append(g)
                finals[k] = self.counts[k]
            else:
                npool = min(Flow.DPOOL, self.counts[k])
                for s in range(npool):
                    g = nc.semaphore(f"fl_{k[0]}_dma{s}")
                    sems[(k[0], "dma", s)] = g.__enter__()
                    guards.append(g)
                    finals[(k[0], "dma", s)] = \
                        16 * ((self.counts[k] - s + Flow.DPOOL - 1)
                              // Flow.DPOOL)

        def dkey(d):
            if d[1] == "op":
                return (d[0], "op"), d[2] + 1
            return (d[0], "dma", d[2] % Flow.DPOOL), 16 * (d[2] // Flow.DPOOL + 1)

        with nc.Block() as block:
            def body_for(engname):
                def body(eng):
                    hw = {}
                    for (en, kind, fn, deps, tok) in steps:
                        if en != engname:
                            continue
                        for d in deps:
                            if (d[0], d[1]) == (engname, "op") and \
                                    engname == "tensor" and kind == "op":
                                continue
                            sk, need = dkey(d)
                            if hw.get(sk, 0) >= need:
                                continue
                            eng.wait_ge(sems[sk], need)
                            hw[sk] = need
                        if kind == "op":
                            fn(eng).then_inc(sems[(en, "op")], 1)
                        else:
                            idx = tok[2]
                            if idx >= Flow.DPOOL:
                                sk = (en, "dma", idx % Flow.DPOOL)
                                need = 16 * (idx // Flow.DPOOL)
                                if hw.get(sk, 0) < need:
                                    eng.wait_ge(sems[sk], need)
                                    hw[sk] = need
                            fn(eng).then_inc(
                                sems[(en, "dma", idx % Flow.DPOOL)], 16)
                    for k, v in finals.items():
                        eng.wait_ge(sems[k], v)
                return body
            for engname in ("sync", "gpsimd", "vector", "scalar", "tensor"):
                getattr(block, engname)(body_for(engname))
        return guards


def bc(ap, shape):
    return ap.broadcast_to(list(shape))


def build_program():
    nc = bass.Bass("TRN2", target_bir_lowering=False)
    S = Flow(nc)

    def inp(name, shape, dt=F32):
        return nc.dram_tensor(name, list(shape), dt, kind="ExternalInput")

    x_d = inp("x", [L, D])
    Wq_d = inp("Wq", [D, 2048])
    Wk_d = inp("Wk", [D, 1024])
    Wv_d = inp("Wv", [D, 1024])
    cq_d = inp("conv_q", [2048, 4])
    ck_d = inp("conv_k", [1024, 4])
    cv_d = inp("conv_v", [1024, 4])
    Wa_d = inp("Wa", [D, H])
    Alog_d = inp("A_log", [H])
    dtb_d = inp("dt_bias", [H])
    Wb_d = inp("Wb", [D, H])
    bb_d = inp("bb", [H])
    Wal_d = inp("Walpha", [D, H])
    bal_d = inp("balpha", [H])
    Wg_d = inp("Wg", [D, 2048])
    Wo_d = inp("Wo", [2048, D])
    w1_d = inp("w1", [DFF, D])
    b1_d = inp("b1", [DFF])
    w2_d = inp("w2", [D, DFF])
    b2_d = inp("b2", [D])
    w1sl_d = inp("w1sl", [DFF // NCORES, D])
    w2sl_d = inp("w2sl", [D, DFF // NCORES])
    id128b_d = inp("id128b", [128, 128], BF16)
    negid128b_d = inp("negid128b", [128, 128], BF16)
    id64f_d = inp("id64f", [64, 64])
    id128f_d = inp("id128f", [128, 128])
    triu64_d = inp("triu64", [64, 64])
    ones64_d = inp("ones64", [64, 64])
    onesrow_d = inp("onesrow", [1, 128])
    ones128c_d = inp("ones128c", [128, 1])
    maskb64_d = inp("maskb64", [64, 64])
    ridgeid_d = inp("ridgeid", [128, 128])
    normwrep_d = inp("normwrep", [64, 1024])

    out_d = nc.dram_tensor("out", [L, D], F32, kind="ExternalOutput")

    spill_kind = "ExternalOutput" if DEBUG else "Internal"
    qspill = nc.dram_tensor("qspill", [NC, 128, H * C], BF16, kind=spill_kind)
    kTspill = nc.dram_tensor("kTspill", [NC, 128, HK * C], BF16, kind=spill_kind)
    ktmspill = nc.dram_tensor("ktmspill", [NC, C, HK * HD], BF16, kind=spill_kind)
    vtmspill = nc.dram_tensor("vtmspill", [NC, C, HK * HD], BF16, kind=spill_kind)
    gspill = nc.dram_tensor("gspill", [NC, C, H * HD], BF16, kind=spill_kind)
    ofmspill = nc.dram_tensor("ofmspill", [TP, 128, H * 128], BF16, kind=spill_kind)
    attnxspill = nc.dram_tensor("attnxspill", [TP, 128, D], F32, kind=spill_kind)
    hspill = nc.dram_tensor("hspill", [TP, 128, FB, 128], BF16)
    w2qT_dram = nc.dram_tensor("w2qT_dram", [4, 128, 32 * 1024], BF16)
    tabsrows = nc.dram_tensor("tabsrows", [62, 512], F32, kind=spill_kind)
    tabs_dram = nc.dram_tensor("tabs_dram", [512, 62], F32)
    csfm_dram = nc.dram_tensor("csfm_dram", [512, 64], F32, kind=spill_kind)
    ccin = nc.dram_tensor("ccin", [2, 1], F32)
    ccout = nc.dram_tensor("ccout", [2, 1], F32)

    dbg = {}

    def dbg_out(name, shape, dt=F32):
        if not DEBUG:
            return None
        dbg[name] = nc.dram_tensor(name, list(shape), dt, kind="ExternalOutput")
        return dbg[name]

    es_all = ExitStack()

    def sb(name, shape, dt=F32, **kw):
        return es_all.enter_context(nc.sbuf_tensor(name, list(shape), dt, **kw))

    with es_all:
        es_xn = ExitStack()
        xn = es_xn.enter_context(
            nc.sbuf_tensor("xn", [128, KC * L], BF16, side="right"))
        id128b = sb("id128b_s", [128, 128], BF16)
        negid128b = sb("negid128b_s", [128, 128], BF16)
        id64f = sb("id64f_s", [64, 64])
        id128f = sb("id128f_s", [128, 128])
        triu64 = sb("triu64_s", [64, 64])
        ones64 = sb("ones64_s", [64, 64])
        onesrow = sb("onesrow_s", [1, 128])
        ones128c = sb("ones128c_s", [128, 1])
        maskb64 = sb("maskb64_s", [64, 64])
        ridgeid = sb("ridgeid_s", [128, 128])
        normwrep = sb("normwrep_s", [64, 1024])
        vecs = sb("vecs", [128, 8])
        glogctm = sb("glogctm", [64, 512])
        csctm = sb("csctm", [64, 512])
        wendctm = sb("wendctm", [64, 512])
        bscorectm = sb("bscorectm", [64, 512])
        expcsctm = sb("expcsctm", [64, 512])
        alphactm = sb("alphactm", [64, 512])
        nkstage = sb("nkstage", [128, 128])
        nk2 = sb("nk2", [64, 256])
        SA = sb("SA", [128, 2048])
        SB = sb("SB", [128, 2048])
        SC = sb("SC", [128, 1024])

        CONSTS = [id128b, id64f, id128f, triu64, ones64, onesrow, ones128c,
                  maskb64, ridgeid, normwrep]

        for t, d in ((id128b, id128b_d), (negid128b, negid128b_d),
                     (id64f, id64f_d), (id128f, id128f_d),
                     (triu64, triu64_d), (ones64, ones64_d),
                     (onesrow, onesrow_d), (ones128c, ones128c_d),
                     (maskb64, maskb64_d), (ridgeid, ridgeid_d),
                     (normwrep, normwrep_d)):
            S.dma("sync", t[:], d[:], w=[t])
        S.op("vector", lambda e: e.memset(vecs[:], 0.0), w=[vecs])
        S.dma("sync", vecs[0:16, 0:1], dtb_d.ap().unsqueeze(1), r=[vecs], w=[vecs])
        S.dma("sync", vecs[32:48, 1:2], bb_d.ap().unsqueeze(1), r=[vecs], w=[vecs])
        S.dma("sync", vecs[64:80, 2:3], bal_d.ap().unsqueeze(1), r=[vecs], w=[vecs])
        S.dma("sync", vecs[0:16, 3:4], Alog_d.ap().unsqueeze(1), r=[vecs], w=[vecs])
        S.op("scalar", lambda e: e.activation(vecs[0:16, 3:4], vecs[0:16, 3:4],
                                              AF.Exp), r=[vecs], w=[vecs])
        S.op("vector", lambda e: e.tensor_scalar_mul(vecs[0:16, 3:4],
                                                     vecs[0:16, 3:4], -1.0),
             r=[vecs], w=[vecs])

        _st = ExitStack()
        with _st:
            wt0 = [_st.enter_context(nc.sbuf_tensor(f"wtile0_{i}", [128, 2048],
                                                    F32)) for i in range(2)]
            abscols = _st.enter_context(nc.sbuf_tensor("abscols", [128, 28], F32))
            p0a = _st.enter_context(nc.psum_tensor("p0a", [128, 512], F32))
            for j in range(8):
                b = wt0[j % 2]
                S.dma("sync", b[:], w1sl_d[j * 128:(j + 1) * 128, :], w=[b])
                S.op("scalar", lambda e, j=j, b=b: e.activation(
                    SA[:], b[:], AF.Abs, accum_out=abscols[:, j:j + 1]),
                    r=[b], w=[SA, abscols])
            for j in range(16):
                b = wt0[j % 2]
                S.dma("sync", b[:, 0:1024], w2sl_d[j * 128:(j + 1) * 128, :],
                      w=[b])
                S.op("scalar", lambda e, j=j, b=b: e.activation(
                    SA[:, 0:1024], b[:, 0:1024], AF.Abs,
                    accum_out=abscols[:, 8 + j:9 + j]), r=[b], w=[SA, abscols])
            S.op("vector", lambda e: e.tensor_reduce(
                abscols[:, 24:25], abscols[:, 0:8].unsqueeze(1), AX.X, ALU.add),
                r=[abscols], w=[abscols])
            S.op("vector", lambda e: e.tensor_reduce(
                abscols[:, 25:26], abscols[:, 8:24].unsqueeze(1), AX.X, ALU.add),
                r=[abscols], w=[abscols])
            S.op("tensor", lambda e: e.matmul(p0a[0:2, 0:1], abscols[:, 24:26],
                                              ones128c[:], start=True, stop=True),
                 r=[abscols, ones128c], w=[p0a])
            S.op("vector", lambda e: e.tensor_copy(abscols[0:2, 26:27],
                                                   p0a[0:2, 0:1]),
                 r=[p0a], w=[abscols])
            S.dma("sync", ccin[:], abscols[0:2, 26:27], r=[abscols], w=[ccin])
            S.op("gpsimd", lambda e: e.collective_compute(
                "AllReduce", ALU.add, replica_groups=[list(range(NCORES))],
                ins=[ccin.ap().opt()], outs=[ccout.ap().opt()]),
                r=[ccin], w=[ccout])
            S.dma("sync", abscols[0:2, 27:28], ccout[:], r=[ccout, abscols],
                  w=[abscols])
            S.op("vector", lambda e: e.tensor_scalar_mul(
                abscols[0:2, 27:28], abscols[0:2, 27:28], 0.7 / (DFF * D)),
                r=[abscols], w=[abscols])
            p0b = _st.enter_context(nc.psum_tensor("p0b", [128, 2], F32))
            S.op("tensor", lambda e: e.transpose(p0b[0:1, 0:2],
                                                 abscols[0:2, 27:28],
                                                 id64f[0:2, 0:2]),
                 r=[abscols, id64f], w=[p0b])
            S.op("vector", lambda e: e.tensor_copy(abscols[0:1, 4:6],
                                                   p0b[0:1, 0:2]),
                 r=[p0b], w=[abscols])
            S.op("tensor", lambda e: e.matmul(p0b[:, 0:2], onesrow[:],
                                              abscols[0:1, 4:6],
                                              start=True, stop=True),
                 r=[onesrow, abscols], w=[p0b])
            S.op("vector", lambda e: e.tensor_copy(vecs[:, 4:6], p0b[:, 0:2]),
                 r=[p0b, vecs], w=[vecs])
        S.barrier()

        _st = ExitStack()
        with _st:
            xt = [_st.enter_context(nc.sbuf_tensor(f"xtile{i}", [128, 2048],
                                                   F32)) for i in range(2)]
            xnb = [_st.enter_context(nc.sbuf_tensor(f"xnbt{i}", [128, 2048],
                                                    BF16)) for i in range(2)]
            lnc = [_st.enter_context(nc.sbuf_tensor(f"lncols{i}", [128, 8],
                                                    F32)) for i in range(2)]
            pt1 = [_st.enter_context(nc.psum_tensor(f"ptr1_{i}", [128, 128],
                                                    BF16)) for i in range(2)]
            for tt in range(TP):
                xtile, xnbt, lncols = xt[tt % 2], xnb[tt % 2], lnc[tt % 2]
                S.dma("sync", xtile[:], x_d[tt * 128:(tt + 1) * 128, :],
                      w=[xtile])
                S.op("vector", lambda e, xtile=xtile, lncols=lncols:
                     e.tensor_reduce(lncols[:, 0:1], xtile[:].unsqueeze(1),
                                     AX.X, ALU.add),
                     r=[xtile], w=[lncols])
                S.op("scalar", lambda e, xtile=xtile, lncols=lncols:
                     e.activation(SA[:], xtile[:], AF.Square,
                                  accum_out=lncols[:, 1:2]),
                     r=[xtile, lncols], w=[SA, lncols])
                S.op("vector", lambda e, lncols=lncols: e.tensor_scalar(
                    lncols[:, 0:2], lncols[:, 0:2], 1.0 / D, None, ALU.mult),
                    r=[lncols], w=[lncols])
                S.op("vector", lambda e, lncols=lncols: e.tensor_tensor(
                    lncols[:, 2:3], lncols[:, 0:1], lncols[:, 0:1], ALU.mult),
                    r=[lncols], w=[lncols])
                S.op("vector", lambda e, lncols=lncols: e.tensor_tensor(
                    lncols[:, 2:3], lncols[:, 1:2], lncols[:, 2:3],
                    ALU.subtract), r=[lncols], w=[lncols])
                S.op("vector", lambda e, lncols=lncols: e.tensor_scalar_add(
                    lncols[:, 2:3], lncols[:, 2:3], 1e-5), r=[lncols], w=[lncols])
                S.op("scalar", lambda e, lncols=lncols: e.activation(
                    lncols[:, 2:3], lncols[:, 2:3], AF.Sqrt),
                    r=[lncols], w=[lncols])
                S.op("vector", lambda e, lncols=lncols: e.reciprocal(
                    lncols[:, 2:3], lncols[:, 2:3]), r=[lncols], w=[lncols])
                S.op("vector", lambda e, xtile=xtile, xnbt=xnbt, lncols=lncols:
                     e.tensor_scalar(xnbt[:], xtile[:], lncols[:, 0:1],
                                     lncols[:, 2:3], ALU.subtract, ALU.mult),
                     r=[xtile, lncols], w=[xnbt])
                for kc in range(KC):
                    ptr1 = pt1[kc % 2]
                    S.op("tensor", lambda e, kc=kc, xnbt=xnbt, ptr1=ptr1:
                         e.transpose(ptr1[:], xnbt[:, kc * 128:(kc + 1) * 128],
                                     id128b[:]),
                         r=[xnbt, id128b], w=[ptr1])
                    S.op("vector", lambda e, kc=kc, tt=tt, ptr1=ptr1:
                         e.tensor_copy(
                             xn[:, kc * L + tt * 128: kc * L + (tt + 1) * 128],
                             ptr1[:]),
                         r=[ptr1], w=["xn"])
        S.barrier()
        if DEBUG:
            d = dbg_out("xn_dbg", [128, KC * L], BF16)
            S.dma("sync", d[:], xn[:], r=["xn"])

        _st = ExitStack()
        with _st:
            wblk2 = [_st.enter_context(nc.sbuf_tensor(f"wblk{i}", [128, 2048],
                                                      BF16)) for i in range(2)]
            upre2 = [_st.enter_context(nc.sbuf_tensor(f"upre{i}", [128, 2051],
                                                      F32)) for i in range(2)]
            cacc2 = [_st.enter_context(nc.sbuf_tensor(f"cacc{i}", [128, 2048],
                                                      F32)) for i in range(2)]
            cvb2 = [_st.enter_context(nc.sbuf_tensor(f"convsb{i}", [128, 4],
                                                     F32)) for i in range(2)]
            qbf2 = [_st.enter_context(nc.sbuf_tensor(f"qbf{i}", [128, 2048],
                                                     BF16)) for i in range(2)]
            tmsb2 = [_st.enter_context(nc.sbuf_tensor(f"tmsb{i}", [128, 2048],
                                                      BF16)) for i in range(2)]
            wsm = _st.enter_context(nc.sbuf_tensor("wsm", [128, 16 * 80], BF16))
            abafm = _st.enter_context(nc.sbuf_tensor("abafm", [128, 2048], F32))
            ppj = [_st.enter_context(nc.psum_tensor(f"pproj{i}", [128, 512],
                                                    F32)) for i in range(2)]
            pt2 = [_st.enter_context(nc.psum_tensor(f"ptr2_{i}", [128, 128],
                                                    BF16)) for i in range(2)]
            ptr2f = _st.enter_context(nc.psum_tensor("ptr2f", [128, 16], F32))

            blkidx = [0]

            def load_wblock(wd, m):
                wblk = wblk2[blkidx[0] % 2]
                for kcq in range(0, KC, 4):
                    src = wd[kcq * 128:(kcq + 4) * 128,
                             m * 128:(m + 1) * 128].rearrange(
                                 "(a p) c -> p a c", p=128)
                    dstv = wblk[:, kcq * 128:(kcq + 4) * 128].rearrange(
                        "p (a c) -> p a c", a=4)
                    S.dma("gpsimd", dstv, src, w=[wblk])
                return wblk

            def mm_evac(wblk, dst, dst_key, to_upre, par=0):
                upre = upre2[par]
                for nt in range(NT):
                    pproj = ppj[nt % 2]
                    for kc in range(KC):
                        S.op("tensor", lambda e, kc=kc, nt=nt, wblk=wblk,
                             pproj=pproj: e.matmul(
                                 pproj[:], wblk[:, kc * 128:(kc + 1) * 128],
                                 xn[:, kc * L + nt * 512:
                                    kc * L + (nt + 1) * 512],
                                 start=(kc == 0), stop=(kc == KC - 1)),
                             r=[wblk, "xn"], w=[pproj])
                    if to_upre:
                        S.op("vector", lambda e, nt=nt, pproj=pproj,
                             upre=upre: e.tensor_copy(
                                 upre[:, 3 + nt * 512: 3 + (nt + 1) * 512],
                                 pproj[:]),
                             r=[pproj, upre], w=[upre])
                    else:
                        S.op("vector", lambda e, nt=nt, pproj=pproj, dst=dst:
                             e.tensor_copy(dst[:, nt * 512:(nt + 1) * 512],
                                           pproj[:]),
                             r=[pproj], w=[dst_key])

            def conv_silu(cd_d, ch, qbf, par=0):
                upre, cacc, convsb = upre2[par], cacc2[par], cvb2[par]
                S.dma("sync", convsb[:], cd_d[ch * 128:(ch + 1) * 128, :],
                      r=[convsb], w=[convsb])
                S.op("vector", lambda e, upre=upre: e.memset(upre[:, 0:3], 0.0),
                     r=[upre], w=[upre])
                S.op("vector", lambda e, upre=upre, cacc=cacc, convsb=convsb:
                     e.tensor_scalar(
                         cacc[:], upre[:, 0:2048], convsb[:, 0:1], None,
                         ALU.mult),
                     r=[upre, convsb, cacc], w=[cacc])
                for j in range(1, 4):
                    S.op("vector", lambda e, j=j, upre=upre, cacc=cacc,
                         convsb=convsb: e.scalar_tensor_tensor(
                             cacc[:], upre[:, j:j + 2048], convsb[:, j:j + 1],
                             cacc[:], ALU.mult, ALU.add),
                         r=[upre, convsb, cacc], w=[cacc])
                S.op("scalar", lambda e, qbf=qbf, cacc=cacc: e.activation(
                    qbf[:], cacc[:], AF.Silu), r=[cacc], w=[qbf])

            def transpose_tm(qbf, tmsb):
                for tt in range(TP):
                    ptr2 = pt2[tt % 2]
                    S.op("tensor", lambda e, tt=tt, qbf=qbf, ptr2=ptr2:
                         e.transpose(ptr2[:], qbf[:, tt * 128:(tt + 1) * 128],
                                     id128b[:]),
                         r=[qbf, id128b], w=[ptr2])
                    S.op("vector", lambda e, tt=tt, tmsb=tmsb, ptr2=ptr2:
                         e.tensor_copy(tmsb[:, tt * 128:(tt + 1) * 128],
                                       ptr2[:]),
                         r=[ptr2], w=[tmsb])

            def spill_tm(spill_d, colbase, tmsb):
                for par in range(2):
                    src = tmsb[par * 64:(par + 1) * 64, :].rearrange(
                        "p (a c) -> p a c", a=TP)
                    dstv = spill_d.ap().rearrange("(a b) c d -> a b c d", b=2)[
                        :, par, :, colbase:colbase + 128].transpose([1, 0, 2])
                    S.dma("sync", dstv, src, r=[tmsb], w=[spill_d])

            def spill_fm(spill_d, colbase, width, qbf):
                for nq in range(0, NC, 8):
                    src = qbf[:, nq * C:(nq + 8) * C].rearrange(
                        "p (a c) -> p a c", a=8)
                    dstv = spill_d.ap()[nq:nq + 8, :,
                                        colbase:colbase + width] \
                        .transpose([1, 0, 2])
                    S.dma("sync", dstv, src, r=[qbf], w=[spill_d])

            for h in range(H):
                qbf = qbf2[h % 2]
                wblk = load_wblock(Wq_d, h)
                mm_evac(wblk, None, None, True, h % 2)
                blkidx[0] += 1
                conv_silu(cq_d, h, qbf, h % 2)
                spill_fm(qspill, h * C, C, qbf)
            def k_tail(kh):
                qbf, tmsb = qbf2[kh % 2], tmsb2[kh % 2]
                transpose_tm(qbf, tmsb)
                spill_tm(ktmspill, kh * 128, tmsb)
                S.op("vector", lambda e, tmsb=tmsb: e.tensor_tensor(
                    SA[:], tmsb[:], tmsb[:], ALU.mult), r=[tmsb], w=[SA])
                S.op("vector", lambda e, kh=kh: e.tensor_reduce(
                    nkstage[:, kh * 16:(kh + 1) * 16],
                    SA[:].rearrange("p (a c) -> p a c", a=16), AX.X, ALU.add),
                    r=[SA, nkstage], w=[nkstage])

            for kh in range(HK):
                qbf = qbf2[kh % 2]
                wblk = load_wblock(Wk_d, kh)
                mm_evac(wblk, None, None, True, kh % 2)
                blkidx[0] += 1
                if kh > 0:
                    k_tail(kh - 1)
                conv_silu(ck_d, kh, qbf, kh % 2)
                spill_fm(kTspill, kh * C, C, qbf)
            k_tail(HK - 1)
            def v_tail(kh):
                qbf, tmsb = qbf2[kh % 2], tmsb2[kh % 2]
                transpose_tm(qbf, tmsb)
                spill_tm(vtmspill, kh * 128, tmsb)

            for kh in range(HK):
                qbf = qbf2[kh % 2]
                wblk = load_wblock(Wv_d, kh)
                mm_evac(wblk, None, None, True, kh % 2)
                blkidx[0] += 1
                if kh > 0:
                    v_tail(kh - 1)
                conv_silu(cv_d, kh, qbf, kh % 2)
            v_tail(HK - 1)
            for h in range(H):
                qbf, tmsb = qbf2[h % 2], tmsb2[h % 2]
                wblk = load_wblock(Wg_d, h)
                mm_evac(wblk, qbf, qbf, False)
                blkidx[0] += 1
                transpose_tm(qbf, tmsb)
                spill_tm(gspill, h * 128, tmsb)
            S.op("vector", lambda e: e.memset(wsm[:], 0.0), w=[wsm])
            for wd, coff in ((Wa_d, 0), (Wb_d, 32), (Wal_d, 64)):
                src = wd.ap().rearrange("(a p) c -> p a c", p=128)
                dstv = wsm[:].rearrange("p (a c) -> p a c", a=KC)[
                    :, :, coff:coff + 16]
                S.dma("gpsimd", dstv, src, r=[wsm], w=[wsm])
            for nt in range(NT):
                pproj = ppj[nt % 2]
                for kc in range(KC):
                    S.op("tensor", lambda e, kc=kc, nt=nt, pproj=pproj:
                         e.matmul(
                             pproj[0:80, :], wsm[:, kc * 80:(kc + 1) * 80],
                             xn[:, kc * L + nt * 512: kc * L + (nt + 1) * 512],
                             start=(kc == 0), stop=(kc == KC - 1)),
                         r=[wsm, "xn"], w=[pproj])
                S.op("vector", lambda e, nt=nt, pproj=pproj: e.tensor_copy(
                    abafm[0:80, nt * 512:(nt + 1) * 512], pproj[0:80, :]),
                    r=[pproj], w=[abafm])
            S.op("scalar", lambda e: e.activation(
                SA[0:16, :], abafm[0:16, :], AF.Abs, bias=vecs[0:16, 0:1]),
                r=[abafm, vecs], w=[SA])
            S.op("scalar", lambda e: e.activation(
                SA[0:16, :], SA[0:16, :], AF.Exp, scale=-1.0), r=[SA], w=[SA])
            S.op("scalar", lambda e: e.activation(
                SA[0:16, :], SA[0:16, :], AF.Ln, bias=1.0), r=[SA], w=[SA])
            S.op("scalar", lambda e: e.activation(
                SB[0:16, :], abafm[0:16, :], AF.Relu, bias=vecs[0:16, 0:1]),
                r=[abafm, vecs], w=[SB])
            S.op("vector", lambda e: e.tensor_add(SA[0:16, :], SA[0:16, :],
                                                  SB[0:16, :]),
                 r=[SA, SB], w=[SA])
            S.op("vector", lambda e: e.tensor_scalar(
                SA[0:16, :], SA[0:16, :], vecs[0:16, 3:4], None, ALU.mult),
                r=[SA, vecs], w=[SA])
            S.op("scalar", lambda e: e.activation(
                SA[32:48, :], abafm[32:48, :], AF.Sigmoid,
                bias=vecs[32:48, 1:2]), r=[abafm, vecs], w=[SA])
            S.op("scalar", lambda e: e.activation(
                SA[64:80, :], abafm[64:80, :], AF.Sigmoid,
                bias=vecs[64:80, 2:3]), r=[abafm, vecs], w=[SA])
            for (r0, ctm) in ((0, glogctm), (32, bscorectm), (64, alphactm)):
                for tt in range(TP):
                    S.op("tensor", lambda e, r0=r0, tt=tt: e.transpose(
                        ptr2f[:], SA[r0:r0 + 16, tt * 128:(tt + 1) * 128],
                        id128f[r0:r0 + 16, r0:r0 + 16]),
                        r=[SA, id128f], w=[ptr2f])
                    S.op("vector", lambda e, ctm=ctm, tt=tt: e.tensor_copy(
                        ctm[:, (2 * tt) * 16:(2 * tt + 1) * 16],
                        ptr2f[0:64, :]), r=[ptr2f], w=[ctm])
                    S.op("vector", lambda e, ctm=ctm, tt=tt: e.tensor_copy(
                        ctm[:, (2 * tt + 1) * 16:(2 * tt + 2) * 16],
                        ptr2f[64:128, :]), r=[ptr2f, ctm], w=[ctm])
        S.barrier()

        if DEBUG:
            d1 = dbg_out("glogctm_dbg", [64, 512])
            d2 = dbg_out("betactm_dbg", [64, 512])
            d3 = dbg_out("alphactm_dbg", [64, 512])
            d4 = dbg_out("nk_dbg", [128, 128])
            S.dma("sync", d1[:], glogctm[:], r=[glogctm])
            S.dma("sync", d2[:], bscorectm[:], r=[bscorectm])
            S.dma("sync", d3[:], alphactm[:], r=[alphactm])
            S.dma("sync", d4[:], nkstage[:], r=[nkstage])

        _st = ExitStack()
        with _st:
            rowA = _st.enter_context(nc.sbuf_tensor("rowA", [1, 512], F32))
            rowB = _st.enter_context(nc.sbuf_tensor("rowB", [1, 512], F32))
            rowC = _st.enter_context(nc.sbuf_tensor("rowC", [1, 512], F32))
            rowD = _st.enter_context(nc.sbuf_tensor("rowD", [1, 512], F32))
            rowE = _st.enter_context(nc.sbuf_tensor("rowE", [1, 512], F32))
            rowF = _st.enter_context(nc.sbuf_tensor("rowF", [1, 512], F32))
            rowG = _st.enter_context(nc.sbuf_tensor("rowG", [1, 512], F32))
            dgrow = _st.enter_context(nc.sbuf_tensor("dgrow", [1, 512], F32))
            trst = _st.enter_context(nc.sbuf_tensor("trst", [1, 16], F32))
            wnk = _st.enter_context(nc.sbuf_tensor("wnk", [64, 16], F32))
            tabsin = _st.enter_context(nc.sbuf_tensor("tabsin", [62, 512], F32))
            tabsT = _st.enter_context(nc.sbuf_tensor("tabsT", [128, 256], F32))
            p25a = _st.enter_context(nc.psum_tensor("p25a", [64, 512], F32))
            p25b = _st.enter_context(nc.psum_tensor("p25b", [64, 512], F32))
            p25r = _st.enter_context(nc.psum_tensor("p25r", [1, 512], F32))
            p25t = _st.enter_context(nc.psum_tensor("p25t", [128, 64], F32))

            S.op("tensor", lambda e: e.matmul(p25a[:], triu64[:], glogctm[:],
                                              start=True, stop=True),
                 r=[triu64, glogctm], w=[p25a])
            S.op("vector", lambda e: e.tensor_copy(csctm[:], p25a[:]),
                 r=[p25a], w=[csctm])
            S.op("tensor", lambda e: e.matmul(p25b[:], ones64[:], glogctm[:],
                                              start=True, stop=True),
                 r=[ones64, glogctm], w=[p25b])
            S.op("vector", lambda e: e.tensor_tensor(SA[0:64, 0:512], p25b[:],
                                                     csctm[:], ALU.subtract),
                 r=[p25b, csctm], w=[SA])
            S.op("scalar", lambda e: e.activation(SA[0:64, 0:512],
                                                  SA[0:64, 0:512], AF.Exp),
                 r=[SA], w=[SA])
            S.op("vector", lambda e: e.tensor_tensor(wendctm[:], SA[0:64, 0:512],
                                                     bscorectm[:], ALU.mult),
                 r=[SA, bscorectm], w=[wendctm])
            S.op("vector", lambda e: e.memset(vecs[:, 6:7],
                                              float(-0.5 * np.log(HD))),
                 r=[vecs], w=[vecs])
            S.op("scalar", lambda e: e.activation(
                expcsctm[:], csctm[:], AF.Exp, bias=vecs[0:64, 6:7]),
                r=[csctm, vecs], w=[expcsctm])
            S.op("vector", lambda e: e.tensor_scalar_mul(
                bscorectm[:], bscorectm[:], float(HD ** -0.5)),
                r=[bscorectm], w=[bscorectm])
            S.op("tensor", lambda e: e.matmul(p25r[:], ones64[:, 0:1],
                                              glogctm[:],
                                              start=True, stop=True),
                 r=[ones64, glogctm], w=[p25r])
            S.op("scalar", lambda e: e.activation(dgrow[:], p25r[:], AF.Exp),
                 r=[p25r], w=[dgrow])
            for t4 in range(4):
                S.op("tensor", lambda e, t4=t4: e.transpose(
                    p25t[:], csctm[:, t4 * 128:(t4 + 1) * 128], id64f[:]),
                    r=[csctm, id64f], w=[p25t])
                S.op("vector", lambda e: e.tensor_copy(SB[:, 0:64], p25t[:]),
                     r=[p25t], w=[SB])
                S.dma("sync", csfm_dram[t4 * 128:(t4 + 1) * 128, :],
                      SB[:, 0:64], r=[SB], w=[csfm_dram])
            for kh in range(HK):
                S.op("vector", lambda e, kh=kh: e.tensor_copy(
                    nk2[:, kh::16], nkstage[0:64, kh * 16:(kh + 1) * 16]),
                    r=[nkstage, nk2], w=[nk2])
                S.op("vector", lambda e, kh=kh: e.tensor_copy(
                    nk2[:, 8 + kh::16], nkstage[64:128, kh * 16:(kh + 1) * 16]),
                    r=[nkstage, nk2], w=[nk2])
            S.op("vector", lambda e: e.memset(trst[:], 0.0), w=[trst])
            for n in range(NC):
                srcnk = nk2[:, n * 8:(n + 1) * 8].unsqueeze(2)
                S.op("vector", lambda e, n=n, srcnk=srcnk: e.tensor_tensor(
                    wnk[:].rearrange("p (a b) -> p a b", a=8),
                    wendctm[:, n * 16:(n + 1) * 16].rearrange(
                        "p (a b) -> p a b", a=8),
                    bc(srcnk, (64, 8, 2)), ALU.mult),
                    r=[wendctm, nk2, wnk], w=[wnk])
                S.op("tensor", lambda e: e.matmul(p25r[:, 0:16], ones64[:, 0:1],
                                                  wnk[:], start=True, stop=True),
                     r=[ones64, wnk], w=[p25r])
                S.op("vector", lambda e, n=n: e.tensor_scalar_add(
                    rowA[:, n * 16:(n + 1) * 16], trst[:], float(RIDGE * HD)),
                    r=[trst, rowA], w=[rowA])
                S.op("vector", lambda e, n=n: e.tensor_tensor(
                    trst[:], trst[:], dgrow[:, n * 16:(n + 1) * 16], ALU.mult),
                    r=[trst, dgrow], w=[trst])
                S.op("vector", lambda e: e.tensor_tensor(
                    trst[:], trst[:], p25r[:, 0:16], ALU.add),
                    r=[trst, p25r], w=[trst])
            S.op("vector", lambda e: e.tensor_scalar(
                rowB[:], rowA[:], RIDGE, 0.5, ALU.add, ALU.mult),
                r=[rowA], w=[rowB])
            S.op("vector", lambda e: e.tensor_scalar(
                rowC[:], rowA[:], -RIDGE, 0.5, ALU.add, ALU.mult),
                r=[rowA], w=[rowC])
            S.op("vector", lambda e: e.reciprocal(rowG[:], rowC[:]),
                 r=[rowC], w=[rowG])
            S.op("vector", lambda e: e.tensor_tensor(rowD[:], rowB[:], rowG[:],
                                                     ALU.mult),
                 r=[rowB, rowG], w=[rowD])
            S.op("vector", lambda e: e.reciprocal(rowE[:], rowD[:]),
                 r=[rowD], w=[rowE])
            S.op("vector", lambda e: e.tensor_scalar_mul(rowD[:], rowD[:], 2.0),
                 r=[rowD], w=[rowD])
            S.op("vector", lambda e: e.tensor_scalar_mul(rowC[:], rowG[:], 2.0),
                 r=[rowG], w=[rowC])
            S.op("vector", lambda e: e.reciprocal(rowG[:], rowB[:]),
                 r=[rowB, rowG], w=[rowG])
            S.dma("sync", tabsrows[0:1, :], rowG[:], r=[rowG], w=[tabsrows])
            rho_cur, rho_nxt = rowE, rowF
            rowpool = [rowG, rowA, None, None]
            rowpool[2] = rowB
            for k in range(NI):
                rg = rowG if k % 2 == 0 else rowA
                S.op("vector", lambda e, rc=rho_cur, rg=rg: e.tensor_tensor(
                    rg[:], rowD[:], rc[:], ALU.subtract),
                    r=[rowD, rho_cur, rg], w=[rg])
                S.op("vector", lambda e, rn=rho_nxt, rg=rg: e.reciprocal(
                    rn[:], rg[:]),
                     r=[rg, rho_nxt], w=[rho_nxt])
                S.op("vector", lambda e, rc=rho_cur, rn=rho_nxt, rg=rg:
                     e.tensor_tensor(rg[:], rn[:], rc[:], ALU.mult),
                    r=[rho_nxt, rho_cur, rg], w=[rg])
                S.dma("sync", tabsrows[1 + 2 * k:2 + 2 * k, :], rg[:],
                      r=[rg], w=[tabsrows])
                rg2 = rowB if k % 2 == 0 else rowpool[2]
                rg2 = rowB
                S.op("vector", lambda e, rn=rho_nxt, rg2=rg2: e.tensor_tensor(
                    rg2[:], rn[:], rowC[:], ALU.mult),
                    r=[rho_nxt, rowC, rg2], w=[rg2])
                S.dma("sync", tabsrows[2 + 2 * k:3 + 2 * k, :], rg2[:],
                      r=[rg2], w=[tabsrows])
                rho_cur, rho_nxt = rho_nxt, rho_cur
            S.dma("sync", tabsrows[61:62, :], dgrow[:], r=[dgrow], w=[tabsrows])
            S.dma("sync", tabsin[:], tabsrows[:], r=[tabsrows], w=[tabsin])
            for t4 in range(4):
                S.op("tensor", lambda e, t4=t4: e.transpose(
                    p25t[:, 0:62], tabsin[:, t4 * 128:(t4 + 1) * 128],
                    id64f[0:62, 0:62]),
                    r=[tabsin, id64f], w=[p25t])
                S.op("vector", lambda e, t4=t4: e.tensor_copy(
                    tabsT[:, t4 * 62:(t4 + 1) * 62], p25t[:, 0:62]),
                    r=[p25t, tabsT], w=[tabsT])
            for t4 in range(4):
                S.dma("sync", tabs_dram[t4 * 128:(t4 + 1) * 128, :],
                      tabsT[:, t4 * 62:(t4 + 1) * 62], r=[tabsT], w=[tabs_dram])
        S.barrier()

        if DEBUG:
            d1 = dbg_out("csctm_dbg", [64, 512])
            d2 = dbg_out("wendctm_dbg", [64, 512])
            d3 = dbg_out("expcs_dbg", [64, 512])
            S.dma("sync", d1[:], csctm[:], r=[csctm])
            S.dma("sync", d2[:], wendctm[:], r=[wendctm])
            S.dma("sync", d3[:], expcsctm[:], r=[expcsctm])

        _st = ExitStack()
        with _st:
            Hst = _st.enter_context(nc.sbuf_tensor("Hst", [128, H * 128], F32))
            Bst = _st.enter_context(nc.sbuf_tensor("Bst", [128, H * 128], F32))
            Abf = _st.enter_context(nc.sbuf_tensor("Abf", [128, H * 128], BF16))
            Bbf = _st.enter_context(nc.sbuf_tensor("Bbf", [128, H * 128], BF16))
            qn2 = [_st.enter_context(nc.sbuf_tensor(f"qn{i}", [128, 1024],
                                                    BF16)) for i in range(2)]
            kT2 = [_st.enter_context(nc.sbuf_tensor(f"kT{i}", [128, 512],
                                                    BF16)) for i in range(2)]
            ktm2 = [_st.enter_context(nc.sbuf_tensor(f"ktmt{i}", [64, 1024],
                                                     BF16)) for i in range(2)]
            vtm2 = [_st.enter_context(nc.sbuf_tensor(f"vtmt{i}", [64, 1024],
                                                     BF16)) for i in range(2)]
            gt2 = [_st.enter_context(nc.sbuf_tensor(f"gt{i}", [64, 2048],
                                                    BF16)) for i in range(2)]
            crep2 = [_st.enter_context(nc.sbuf_tensor(f"crep{i}", [128, 16 * 62],
                                                      F32)) for i in range(2)]
            csm2 = [_st.enter_context(nc.sbuf_tensor(f"csmega{i}", [1, 1024],
                                                     F32)) for i in range(2)]
            dvec2 = [_st.enter_context(nc.sbuf_tensor(f"dvecbf{i}",
                                                      [128, 1024], BF16))
                     for i in range(2)]
            SCb = _st.enter_context(nc.sbuf_tensor("SCb", [128, 1024], BF16))
            SDb = _st.enter_context(nc.sbuf_tensor("SDb", [128, 1024], BF16))
            tavpre = _st.enter_context(nc.sbuf_tensor("tavpre", [64, 2048], BF16))
            sgn = _st.enter_context(nc.sbuf_tensor("sgn", [64, 2048], BF16))
            xchbf = _st.enter_context(nc.sbuf_tensor("xchbf", [128, 1024], BF16))
            scoresbf = _st.enter_context(
                nc.sbuf_tensor("scoresbf", [64, 1024], BF16))
            wkb = _st.enter_context(nc.sbuf_tensor("wkb", [64, 2048], BF16))
            opair = _st.enter_context(nc.sbuf_tensor("opair", [128, 2048], F32))
            ofmsb = _st.enter_context(nc.sbuf_tensor("ofmsb", [128, 2048], BF16))
            ms = _st.enter_context(nc.sbuf_tensor("ms", [64, 16], F32))
            SE = _st.enter_context(nc.sbuf_tensor("SE", [64, 1024], F32))
            pcheb = _st.enter_context(nc.psum_tensor("pcheb", [128, 1024], F32))
            pA = _st.enter_context(nc.psum_tensor("pA", [128, 1024], F32))
            pB = _st.enter_context(nc.psum_tensor("pB", [128, 1024], F32))
            pacc = _st.enter_context(nc.psum_tensor("pacc", [128, 1024],
                                                    F32))

            S.op("vector", lambda e: e.memset(Hst[:], 0.0), w=[Hst])
            S.op("vector", lambda e: e.memset(Bst[:], 0.0), w=[Bst])
            deferred = []

            for n in range(NC):
                qn, kTt = qn2[n % 2], kT2[n % 2]
                ktmt, vtmt = ktm2[n % 2], vtm2[n % 2]
                gt, crep, csmega = gt2[n % 2], crep2[n % 2], csm2[n % 2]
                crep3 = crep[:].rearrange("p (h r) -> p h r", h=16)
                S.dma("sync", qn[:], qspill[n], r=[qspill], w=[qn])
                S.dma("sync", kTt[:], kTspill[n], r=[kTspill], w=[kTt])
                S.dma("sync", ktmt[:], ktmspill[n], r=[ktmspill], w=[ktmt])
                S.dma("sync", vtmt[:], vtmspill[n], r=[vtmspill], w=[vtmt])
                S.dma("sync", gt[:], gspill[n], r=[gspill], w=[gt])
                S.dma("gpsimd", crep[:].rearrange("p (h r) -> p h r", h=16),
                      tabs_dram[n * 16:(n + 1) * 16, :].unsqueeze(0)
                      .partition_broadcast(128).squeeze(1),
                      r=[tabs_dram], w=[crep])
                S.dma("gpsimd", csmega[:],
                      csfm_dram[n * 16:(n + 1) * 16, :]
                      .rearrange("a b -> (a b)").unsqueeze(0),
                      r=[csfm_dram], w=[csmega])

                S.op("vector", lambda e: e.tensor_tensor(
                    Abf[:].rearrange("p (h f) -> p h f", h=H),
                    Hst[:].rearrange("p (h f) -> p h f", h=H),
                    bc(ridgeid[:].unsqueeze(1), (128, H, 128)), ALU.add),
                    r=[Hst, ridgeid, Abf], w=[Abf])

                qn3 = qn[:].rearrange("p (h c) -> p h c", h=H)
                S.op("vector", lambda e, qn3=qn3, crep3=crep3: e.tensor_tensor(
                    dvec2[0][:].rearrange("p (h c) -> p h c", h=H), qn3,
                    bc(crep3[:, :, 0], (128, H, C)), ALU.mult),
                    r=[qn, crep, dvec2[0]], w=[dvec2[0]])
                for h in range(H):
                    S.op("tensor", lambda e, h=h, qn=qn: e.matmul(
                        pcheb[:, h * C:(h + 1) * C], negid128b[:],
                        qn[:, h * C:(h + 1) * C],
                        start=True, stop=False, skip_group_check=True),
                        r=[negid128b, qn], w=["pcheb0", "pcheb1"])
                def emit_scores(n=n, qn=qn, kTt=kTt, csmega=csmega):
                    S.op("tensor", lambda e: e.matmul(
                        pB[0:64, 0:512], onesrow[0:1, 0:64], csmega[:, 0:512],
                        start=True, stop=True),
                        r=[onesrow, csmega], w=["pB0"])
                    S.op("tensor", lambda e: e.matmul(
                        pB[0:64, 512:1024], onesrow[0:1, 0:64],
                        csmega[:, 512:1024], start=True, stop=True),
                        r=[onesrow, csmega], w=["pB1"])
                    S.op("vector", lambda e: e.tensor_tensor(
                        SE[:].rearrange("p (h c) -> p h c", h=H),
                        pB[0:64, :].rearrange("p (h c) -> p h c", h=H),
                        bc(csctm[:, n * 16:(n + 1) * 16], (64, 16, 64)),
                        ALU.subtract),
                        r=["pB0", "pB1", csctm, SE], w=[SE])
                    S.op("gpsimd", lambda e: e.tensor_tensor(
                        SE[:].rearrange("p (h c) -> p h c", h=H),
                        SE[:].rearrange("p (h c) -> p h c", h=H),
                        bc(maskb64[:].unsqueeze(1), (64, 16, 64)), ALU.add),
                        r=[SE, maskb64], w=[SE])
                    S.op("scalar", lambda e: e.activation(
                        SE[:], SE[:], AF.Exp), r=[SE], w=[SE])
                    S.op("gpsimd", lambda e: e.tensor_tensor(
                        SE[:].rearrange("p (h c) -> p h c", h=H),
                        SE[:].rearrange("p (h c) -> p h c", h=H),
                        bc(bscorectm[:, n * 16:(n + 1) * 16], (64, 16, 64)),
                        ALU.mult),
                        r=[SE, bscorectm], w=[SE])
                    for h in range(H):
                        S.op("tensor", lambda e, h=h: e.matmul(
                            pB[0:64, h * C:(h + 1) * C],
                            kTt[:, (h // 2) * C:(h // 2 + 1) * C],
                            qn[:, h * C:(h + 1) * C], start=True, stop=True),
                            r=[kTt, qn], w=["pB0" if h < 8 else "pB1"])
                    S.op("vector", lambda e: e.tensor_tensor(
                        scoresbf[:], pB[0:64, :], SE[:], ALU.mult),
                        r=["pB0", "pB1", SE, scoresbf], w=[scoresbf])

                def omms_pb(half, vtmt=vtmt):
                    hh0 = half * 8
                    for h in range(hh0, hh0 + 8):
                        pbk = "pB0" if (h - hh0) < 4 else "pB1"
                        S.op("tensor", lambda e, h=h, hh0=hh0, vtmt=vtmt:
                             e.matmul(
                                 pB[0:64, (h - hh0) * 128:(h - hh0 + 1) * 128],
                                 scoresbf[0:64, h * C:(h + 1) * C],
                                 vtmt[:, (h // 2) * 128:(h // 2 + 1) * 128],
                                 start=True, stop=True),
                             r=[scoresbf, vtmt], w=[pbk])

                def emit_tavpre(half, n=n, vtmt=vtmt):
                    h4 = half * 4
                    vv4 = vtmt[:, h4 * 128:(h4 + 4) * 128].rearrange(
                        "p (a f) -> p a f", a=4).unsqueeze(2)
                    S.op("gpsimd", lambda e, vv4=vv4, vtmt=vtmt, h4=h4,
                         half=half: e.tensor_tensor(
                             tavpre[:, half * 1024:(half + 1) * 1024]
                             .rearrange("p (a b f) -> p a b f", a=4, b=2),
                             bc(vv4, (64, 4, 2, 128)),
                             bc(alphactm[:, n * 16 + half * 8:
                                         n * 16 + half * 8 + 8]
                                .rearrange("p (a b) -> p a b", a=4),
                                (64, 4, 2, 128)), ALU.mult),
                         r=[vtmt, alphactm, f"tav{half}"], w=[f"tav{half}"])

                def emit_sa(half):
                    S.op("vector", lambda e, half=half: e.tensor_tensor(
                        SA[0:64, 0:1024], pB[0:64, :],
                        tavpre[:, half * 1024:(half + 1) * 1024], ALU.add),
                        r=["pB0", "pB1", f"tav{half}", SA], w=[SA])

                for k in range(NI - 1):
                    dv_in, dv_out = dvec2[k % 2], dvec2[(k + 1) % 2]
                    S.op("vector", lambda e, k=k, crep3=crep3, dv_in=dv_in:
                         e.tensor_tensor(
                             SDb[:].rearrange("p (h c) -> p h c", h=H),
                             dv_in[:].rearrange("p (h c) -> p h c", h=H),
                             bc(crep3[:, :, 1 + 2 * k], (128, H, C)),
                             ALU.mult),
                         r=[dv_in, crep, SDb], w=[SDb])
                    for h in range(H):
                        S.op("tensor", lambda e, h=h, k=k, dv_in=dv_in:
                             e.matmul(
                                 pcheb[:, h * C:(h + 1) * C],
                                 Abf[:, h * 128:(h + 1) * 128],
                                 dv_in[:, h * C:(h + 1) * C],
                                 start=False, stop=(k == NI - 2),
                                 skip_group_check=True),
                             r=[Abf, dv_in], w=["pcheb0", "pcheb1"])
                    for j in range(2):
                        S.op("tensor", lambda e, j=j, k=k, dv_in=dv_in:
                             e.matmul(
                                 pacc[:, j * 512:(j + 1) * 512], id128b[:],
                                 dv_in[:, j * 512:(j + 1) * 512],
                                 start=(k == 0), stop=False,
                                 skip_group_check=True),
                             r=[id128b, dv_in], w=[f"pacc{j}"])
                    if deferred:
                        deferred.pop(0)()
                    if deferred:
                        deferred.pop(0)()
                    S.op("vector", lambda e, k=k, crep3=crep3: e.tensor_tensor(
                        SCb[:].rearrange("p (h c) -> p h c", h=H),
                        pcheb[:].rearrange("p (h c) -> p h c", h=H),
                        bc(crep3[:, :, 2 + 2 * k], (128, H, C)), ALU.mult),
                        r=["pcheb0", "pcheb1", crep, SCb], w=[SCb])
                    S.op("vector", lambda e, dv_out=dv_out: e.tensor_tensor(
                        dv_out[:], SDb[:], SCb[:], ALU.subtract),
                        r=[SCb, SDb, dv_out], w=[dv_out])
                    if k == 3 and n > 0:
                        omms(1)
                    elif k == 4:
                        S.op("vector", lambda e: e.tensor_copy(
                            Bbf[:], Bst[:]), r=[Bst, Bbf], w=[Bbf])
                    elif k == 5:
                        emit_tavpre(0)
                    elif k == 6:
                        emit_tavpre(1)
                    elif k == 11:
                        emit_scores()
                    elif k == 16:
                        omms_pb(0)
                    elif k == 17:
                        emit_sa(0)
                    elif k == 18:
                        omms_pb(1)
                dlast = dvec2[(NI - 1) % 2]
                for j in range(2):
                    S.op("tensor", lambda e, j=j, dlast=dlast: e.matmul(
                        pacc[:, j * 512:(j + 1) * 512], id128b[:],
                        dlast[:, j * 512:(j + 1) * 512],
                        start=False, stop=True, skip_group_check=True),
                        r=[id128b, dlast], w=[f"pacc{j}"])
                while deferred:
                    deferred.pop(0)()
                S.op("scalar", lambda e: e.activation(
                    xchbf[:], pacc[:], AF.Copy),
                    r=["pacc0", "pacc1", xchbf], w=[xchbf])

                S.op("scalar", lambda e, gt=gt: e.activation(
                    sgn[:], gt[:], AF.Silu), r=[gt, sgn], w=[sgn])
                S.op("gpsimd", lambda e: e.tensor_tensor(
                    sgn[:].rearrange("p (h f) -> p h f", h=16),
                    sgn[:].rearrange("p (h f) -> p h f", h=16),
                    bc(normwrep[:, 0:128].unsqueeze(1), (64, 16, 128)),
                    ALU.mult), r=[sgn, normwrep], w=[sgn])

                def omms(half):
                    hh0 = half * 8
                    for h in range(hh0, hh0 + 8):
                        pak = "pA0" if (h - hh0) < 4 else "pA1"
                        S.op("tensor", lambda e, h=h, hh0=hh0: e.matmul(
                            pA[0:64, (h - hh0) * 128:(h - hh0 + 1) * 128],
                            xchbf[:, h * C:(h + 1) * C],
                            Bbf[:, h * 128:(h + 1) * 128],
                            start=True, stop=True),
                            r=[xchbf, Bbf], w=[pak])


                def assembly_thunks(n, half, with_sa=True):
                    hh0 = half * 8
                    ops = []
                    if with_sa:
                        ops.append(lambda: emit_sa(half))
                    ops.append(lambda: S.op(
                        "vector", lambda e: e.tensor_tensor(
                            SB[0:64, 0:1024].rearrange("p (h f) -> p h f", h=8),
                            pA[0:64, :].rearrange("p (h f) -> p h f", h=8),
                            bc(expcsctm[:, n * 16 + hh0:n * 16 + hh0 + 8],
                               (64, 8, 128)), ALU.mult),
                        r=["pA0", "pA1", expcsctm, SB], w=[SB]))
                    ops.append(lambda: S.op(
                        "vector", lambda e: e.tensor_add(
                            SC[0:64, :], SB[0:64, 0:1024], SA[0:64, 0:1024]),
                        r=[SA, SB, SC], w=[SC]))
                    ops.append(lambda: S.op(
                        "vector", lambda e: e.tensor_tensor(
                            SA[0:64, 0:1024], SC[0:64, :], SC[0:64, :],
                            ALU.mult),
                        r=[SC, SA], w=[SA]))
                    ops.append(lambda: S.op(
                        "vector", lambda e: e.tensor_reduce(
                            ms[:, half * 8:(half + 1) * 8],
                            SA[0:64, 0:1024].rearrange("p (h f) -> p h f", h=8),
                            AX.X, ALU.add),
                        r=[SA, ms], w=[ms]))
                    ops.append(lambda: S.op(
                        "vector", lambda e: e.tensor_scalar(
                            ms[:, half * 8:(half + 1) * 8],
                            ms[:, half * 8:(half + 1) * 8],
                            1.0 / HD, 1e-6, ALU.mult, ALU.add),
                        r=[ms], w=[ms]))
                    ops.append(lambda: S.op(
                        "scalar", lambda e: e.activation(
                            ms[:, half * 8:(half + 1) * 8],
                            ms[:, half * 8:(half + 1) * 8], AF.Sqrt),
                        r=[ms], w=[ms]))
                    ops.append(lambda: S.op(
                        "vector", lambda e: e.reciprocal(
                            ms[:, half * 8:(half + 1) * 8],
                            ms[:, half * 8:(half + 1) * 8]),
                        r=[ms], w=[ms]))
                    ops.append(lambda: S.op(
                        "vector", lambda e: e.tensor_tensor(
                            SC[0:64, :].rearrange("p (h f) -> p h f", h=8),
                            SC[0:64, :].rearrange("p (h f) -> p h f", h=8),
                            bc(ms[:, half * 8:(half + 1) * 8], (64, 8, 128)),
                            ALU.mult),
                        r=[ms, SC], w=[SC]))
                    ops.append(lambda: S.op(
                        "vector", lambda e: e.tensor_tensor(
                            opair[(n % 2) * 64:(n % 2) * 64 + 64,
                                  half * 1024:(half + 1) * 1024],
                            SC[0:64, :],
                            sgn[:, half * 1024:(half + 1) * 1024], ALU.mult),
                        r=[SC, sgn, opair], w=[opair]))
                    return ops

                omms(0)
                dnew = (assembly_thunks(n, 0, with_sa=False)
                        + assembly_thunks(n, 1))
                if n % 2 == 1:
                    tpx = n // 2
                    for h in range(H):
                        po = (h % 2) * 512
                        pk = f"pA{h % 2}"
                        dnew.append(lambda h=h, po=po, pk=pk: S.op(
                            "tensor", lambda e: e.transpose(
                                pA[:, po:po + 128],
                                opair[:, h * 128:(h + 1) * 128],
                                id128f[:]),
                            r=[opair, id128f], w=[pk]))
                        dnew.append(lambda h=h, po=po, pk=pk: S.op(
                            "vector", lambda e: e.tensor_copy(
                                ofmsb[:, h * 128:(h + 1) * 128],
                                pA[:, po:po + 128]),
                            r=[pk, ofmsb], w=[ofmsb]))
                    dnew.append(lambda tpx=tpx: S.dma(
                        "sync", ofmspill[tpx], ofmsb[:], r=[ofmsb],
                        w=[ofmspill]))
                if DEBUG and n == 0:
                    dO = dbg_out("ochunk0_dbg", [64, 2048])
                    dnew.append(lambda dO=dO: S.dma(
                        "sync", dO[:], opair[0:64, :], r=[opair]))

                assert not deferred
                deferred = dnew

                for h in range(H):
                    S.op("vector", lambda e, h=h, n=n, ktmt=ktmt:
                         e.tensor_scalar(
                             wkb[:, h * 128:(h + 1) * 128],
                             ktmt[:, (h // 2) * 128:(h // 2 + 1) * 128],
                             wendctm[:, n * 16 + h:n * 16 + h + 1], None,
                             ALU.mult),
                         r=[ktmt, wendctm, wkb], w=[wkb])
                for h in range(H):
                    s0 = (h % 2) * 512
                    pk = "pcheb0" if h % 2 == 0 else "pcheb1"
                    S.op("tensor", lambda e, h=h, s0=s0, ktmt=ktmt: e.matmul(
                        pcheb[:, s0:s0 + 128],
                        wkb[0:64, h * 128:(h + 1) * 128],
                        ktmt[:, (h // 2) * 128:(h // 2 + 1) * 128],
                        start=True, stop=True),
                        r=[wkb, ktmt], w=[pk])
                    S.op("tensor", lambda e, h=h, s0=s0, vtmt=vtmt: e.matmul(
                        pcheb[:, s0 + 128:s0 + 256],
                        wkb[0:64, h * 128:(h + 1) * 128],
                        vtmt[:, (h // 2) * 128:(h // 2 + 1) * 128],
                        start=True, stop=True),
                        r=[wkb, vtmt], w=[pk])
                    S.op("vector", lambda e, h=h, s0=s0, crep=crep:
                         e.scalar_tensor_tensor(
                             Hst[:, h * 128:(h + 1) * 128],
                             Hst[:, h * 128:(h + 1) * 128],
                             crep[:, h * 62 + 61:h * 62 + 62],
                             pcheb[:, s0:s0 + 128], ALU.mult, ALU.add),
                         r=[Hst, crep, pk], w=[Hst])
                    S.op("vector", lambda e, h=h, s0=s0, crep=crep:
                         e.scalar_tensor_tensor(
                             Bst[:, h * 128:(h + 1) * 128],
                             Bst[:, h * 128:(h + 1) * 128],
                             crep[:, h * 62 + 61:h * 62 + 62],
                             pcheb[:, s0 + 128:s0 + 256], ALU.mult, ALU.add),
                         r=[Bst, crep, pk], w=[Bst])

                if DEBUG and n == 0:
                    dH1 = dbg_out("Hst1_dbg", [128, H * 128])
                    dB1 = dbg_out("Bst1_dbg", [128, H * 128])
                    dY = dbg_out("xch0_dbg", [128, 1024], BF16)
                    dCr = dbg_out("crep0_dbg", [128, 16 * 62])
                    S.dma("sync", dH1[:], Hst[:], r=[Hst])
                    S.dma("sync", dB1[:], Bst[:], r=[Bst])
                    S.dma("sync", dY[:], xchbf[:], r=[xchbf])
                    S.dma("sync", dCr[:], crep[:], r=[crep])
            omms(1)
            while deferred:
                deferred.pop(0)()
        S.barrier()

        _st = ExitStack()
        with _st:
            wosb = _st.enter_context(
                nc.sbuf_tensor("wosb", [128, KC * 2048], BF16))
            ofm2 = [_st.enter_context(nc.sbuf_tensor(f"ofmt{i}", [128, 2048],
                                                     BF16)) for i in range(2)]
            xt42 = [_st.enter_context(nc.sbuf_tensor("xt4_0", [128, 2048],
                                                     F32))] * 2
            ax2 = [_st.enter_context(nc.sbuf_tensor("axsb0", [128, 2048],
                                                    F32))] * 2
            pW2 = [_st.enter_context(nc.psum_tensor(f"pW{i}", [128, 512],
                                                    F32)) for i in range(2)]
            for kc in range(KC):
                S.dma("gpsimd", wosb[:, kc * 2048:(kc + 1) * 2048],
                      Wo_d[kc * 128:(kc + 1) * 128, :], r=[wosb], w=[wosb])
            for tp in range(TP):
                ofmt, xt4, axsb = ofm2[tp % 2], xt42[tp % 2], ax2[tp % 2]
                S.dma("sync", ofmt[:], ofmspill[tp], r=[ofmspill], w=[ofmt])
                S.dma("sync", xt4[:], x_d[tp * 128:(tp + 1) * 128, :], w=[xt4])
                for dm in range(4):
                    pW = pW2[dm % 2]
                    for h in range(H):
                        S.op("tensor", lambda e, dm=dm, h=h, ofmt=ofmt, pW=pW:
                             e.matmul(
                                 pW[:], ofmt[:, h * 128:(h + 1) * 128],
                                 wosb[:, h * 2048 + dm * 512:
                                      h * 2048 + (dm + 1) * 512],
                                 start=(h == 0), stop=(h == H - 1)),
                             r=[ofmt, wosb], w=[pW])
                    S.op("vector", lambda e, dm=dm, xt4=xt4, axsb=axsb, pW=pW:
                         e.tensor_tensor(
                             axsb[:, dm * 512:(dm + 1) * 512], pW[:],
                             xt4[:, dm * 512:(dm + 1) * 512], ALU.add),
                         r=[pW, xt4, axsb], w=[axsb])
                S.dma("sync", attnxspill[tp], axsb[:], r=[axsb],
                      w=[attnxspill])
        S.barrier()

        _st = ExitStack()
        with _st:
            w1t2 = [_st.enter_context(nc.sbuf_tensor(f"w1t{i}", [128, 2048],
                                                     F32)) for i in range(3)]
            w1qb2 = [_st.enter_context(nc.sbuf_tensor(f"w1qb{i}", [128, 2048],
                                                      BF16)) for i in range(2)]
            w1qT2 = [_st.enter_context(nc.sbuf_tensor(f"w1qT{i}", [128, 2048],
                                                      BF16)) for i in range(2)]
            w2t2 = [_st.enter_context(nc.sbuf_tensor(f"w2t{i}", [128, 2048],
                                                     F32)) for i in range(2)]
            w2qb2 = [_st.enter_context(nc.sbuf_tensor(f"w2qb{i}", [128, 2048],
                                                      BF16)) for i in range(2)]
            w2st = _st.enter_context(nc.sbuf_tensor("w2st", [128, 2048], BF16))
            qm1 = _st.enter_context(nc.sbuf_tensor("qm1", [128, 2048], F32))
            qs1 = _st.enter_context(nc.sbuf_tensor("qs1", [128, 2048], BF16))
            qm2 = _st.enter_context(nc.sbuf_tensor("qm2", [128, 2048], F32))
            qs2 = _st.enter_context(nc.sbuf_tensor("qs2", [128, 2048], BF16))
            b1sb = _st.enter_context(nc.sbuf_tensor("b1sb", [128, 64], F32))
            hbf2 = [_st.enter_context(nc.sbuf_tensor(f"hbf{i}", [128, 512],
                                                     BF16)) for i in range(3)]
            ph52 = [_st.enter_context(nc.psum_tensor(f"ph5_{i}", [128, 512],
                                                     F32)) for i in range(3)]
            pwt2 = [_st.enter_context(nc.psum_tensor(f"pwt{i}", [128, 512],
                                                     BF16)) for i in range(2)]
            pb1 = _st.enter_context(nc.psum_tensor("pb1", [128, 64], F32))

            S.dma("sync", SA[0:64, 0:128],
                  b1_d.ap().rearrange("(a b) -> a b", b=128), r=[SA], w=[SA])
            S.op("tensor", lambda e: e.transpose(pb1[:], SA[0:64, 0:128],
                                                 id64f[:]),
                 r=[SA, id64f], w=[pb1])
            S.op("vector", lambda e: e.tensor_copy(b1sb[:], pb1[:]),
                 r=[pb1], w=[b1sb])

            def quant(wt, wqb, tcol, qm, qs):
                S.op("scalar", lambda e: e.activation(qm[:], wt[:], AF.Abs),
                     r=[wt, qm], w=[qm])
                S.op("scalar", lambda e: e.activation(qs[:], wt[:], AF.Sign),
                     r=[wt, qs], w=[qs])
                S.op("vector", lambda e: e.scalar_tensor_tensor(
                    wqb[:], qm[:], tcol, qs[:], ALU.is_ge, ALU.mult),
                    r=[qm, qs, vecs, wqb], w=[wqb])

            def transpose16(wqb, dst, dstkey):
                for g in range(4):
                    pwt = pwt2[g % 2]
                    for j in range(4):
                        S.op("tensor", lambda e, g=g, j=j, wqb=wqb, pwt=pwt:
                             e.transpose(
                                 pwt[:, j * 128:(j + 1) * 128],
                                 wqb[:, (g * 4 + j) * 128:
                                     (g * 4 + j + 1) * 128], id128b[:]),
                             r=[wqb, id128b], w=[pwt])
                    S.op("vector", lambda e, g=g, dst=dst, pwt=pwt:
                         e.tensor_copy(dst[:, g * 512:(g + 1) * 512], pwt[:]),
                         r=[pwt, dstkey], w=[dstkey])

            def w1_mms(fb):
                w1qT = w1qT2[fb % 2]
                for t5 in range(NT):
                    ph5 = ph52[(fb * NT + t5) % 3]
                    hbf = hbf2[(fb * NT + t5) % 3]
                    for kc in range(KC):
                        S.op("tensor", lambda e, kc=kc, t5=t5, w1qT=w1qT,
                             ph5=ph5: e.matmul(
                                 ph5[:], w1qT[:, kc * 128:(kc + 1) * 128],
                                 xn[:, kc * L + t5 * 512:
                                    kc * L + (t5 + 1) * 512],
                                 start=(kc == 0), stop=(kc == KC - 1)),
                             r=[w1qT, "xn"], w=[ph5])
                    S.op("scalar", lambda e, fb=fb, ph5=ph5, hbf=hbf:
                         e.activation(hbf[:], ph5[:], AF.Gelu,
                                      bias=b1sb[:, fb:fb + 1]),
                         r=[ph5, b1sb, hbf], w=[hbf])
                    dstv = hspill.ap()[t5 * 4:(t5 + 1) * 4, :, fb, :] \
                        .transpose([1, 0, 2])
                    S.dma("gpsimd", dstv,
                          hbf[:].rearrange("p (a c) -> p a c", a=4),
                          r=[hbf], w=[hspill])

            for fb in range(FB):
                w1t, w1qb, w1qT = w1t2[fb % 3], w1qb2[fb % 2], w1qT2[fb % 2]
                S.dma("sync", w1t[:], w1_d[fb * 128:(fb + 1) * 128, :],
                      w=[w1t])
                quant(w1t, w1qb, vecs[:, 4:5], qm1, qs1)
                r_, cc = fb // 4, fb % 4
                qd, dmb = r_ // 4, r_ % 4
                w2t, w2qb = w2t2[fb % 2], w2qb2[fb % 2]
                S.dma("scalar", w2t[:],
                      w2_d[r_ * 128:(r_ + 1) * 128,
                           cc * 2048:(cc + 1) * 2048], w=[w2t])
                if fb > 0:
                    w1_mms(fb - 1)
                quant(w2t, w2qb, vecs[:, 5:6], qm2, qs2)
                transpose16(w1qb, w1qT, w1qT)
                transpose16(w2qb, w2st, w2st)
                dstv2 = w2qT_dram.ap()[qd].rearrange(
                    "p (f c) -> p f c", f=64)[:, cc * 16:(cc + 1) * 16,
                                              dmb * 128:(dmb + 1) * 128]
                S.dma("scalar", dstv2,
                      w2st[:].rearrange("p (f c) -> p f c", f=16),
                      r=[w2st], w=[w2qT_dram])
            w1_mms(FB - 1)
        S.barrier()
        es_xn.close()

        _st = ExitStack()
        with _st:
            w2qT = _st.enter_context(
                nc.sbuf_tensor("w2qT", [128, 32 * 1024], BF16))
            ht2 = [_st.enter_context(nc.sbuf_tensor(f"htile{i}", [128, 8192],
                                                    BF16)) for i in range(2)]
            axt2 = [_st.enter_context(nc.sbuf_tensor(f"axt{i}", [128, 512],
                                                     F32)) for i in range(2)]
            b2rep = _st.enter_context(nc.sbuf_tensor("b2rep", [128, 2048], F32))
            outs2 = [_st.enter_context(nc.sbuf_tensor(f"outsb{i}", [128, 512],
                                                      F32)) for i in range(2)]
            po2 = [_st.enter_context(nc.psum_tensor(f"pout{i}", [128, 512],
                                                    F32)) for i in range(2)]
            S.dma("gpsimd", b2rep[:],
                  b2_d.ap().unsqueeze(0).partition_broadcast(128).squeeze(1),
                  w=[b2rep])
            for qd in range(4):
                S.dma("sync", w2qT[:].rearrange("p (a c) -> p a c", a=32),
                      w2qT_dram.ap()[qd], r=[w2qT_dram, w2qT], w=[w2qT])
                for tp in range(TP):
                    htile, axt = ht2[tp % 2], axt2[tp % 2]
                    pout, outsb = po2[tp % 2], outs2[tp % 2]
                    S.dma("scalar", htile[:].rearrange("p (a c) -> p a c",
                                                       a=FB),
                          hspill.ap()[tp], r=[hspill], w=[htile])
                    S.dma("sync", axt[:],
                          attnxspill.ap()[tp, :, qd * 512:(qd + 1) * 512],
                          r=[attnxspill], w=[axt])
                    for fb in range(FB):
                        S.op("tensor", lambda e, fb=fb, htile=htile, pout=pout:
                             e.matmul(
                                 pout[:],
                                 htile[:, fb * 128:(fb + 1) * 128],
                                 w2qT[:, fb * 512:(fb + 1) * 512],
                                 start=(fb == 0), stop=(fb == FB - 1)),
                             r=[htile, w2qT], w=[pout])
                    S.op("vector", lambda e, pout=pout, axt=axt, outsb=outsb:
                         e.tensor_tensor(outsb[:], pout[:], axt[:], ALU.add),
                         r=[pout, axt, outsb], w=[outsb])
                    S.op("vector", lambda e, qd=qd, outsb=outsb:
                         e.tensor_tensor(
                             outsb[:], outsb[:],
                             b2rep[:, qd * 512:(qd + 1) * 512], ALU.add),
                         r=[outsb, b2rep], w=[outsb])
                    S.dma("sync",
                          out_d[tp * 128:(tp + 1) * 128,
                                qd * 512:(qd + 1) * 512], outsb[:],
                          r=[outsb], w=[out_d])

        guards = S.emit()
        for g in guards:
            pass

    return nc


def _consts():
    j = np.arange(64)
    c = np.arange(64)
    maskb = np.where(j[:, None] > c[None, :], np.float32(-1e30),
                     np.float32(0.0)).astype(np.float32)
    triu = (j[:, None] <= c[None, :]).astype(np.float32)
    return dict(
        id128b=np.eye(128, dtype=ml_dtypes.bfloat16),
        negid128b=(-np.eye(128)).astype(ml_dtypes.bfloat16),
        id64f=np.eye(64, dtype=np.float32),
        id128f=np.eye(128, dtype=np.float32),
        triu64=triu,
        ones64=np.ones((64, 64), np.float32),
        onesrow=np.ones((1, 128), np.float32),
        ones128c=np.ones((128, 1), np.float32),
        maskb64=maskb,
        ridgeid=(RIDGE * np.eye(128)).astype(np.float32),
    )


def kernel(L0_attn_hidden_states, L0_attn_Wq, L0_attn_Wk, L0_attn_Wv,
           L0_attn_conv_q, L0_attn_conv_k, L0_attn_conv_v, L0_attn_Wa,
           L0_attn_A_log, L0_attn_dt_bias, L0_attn_Wb, L0_attn_bb,
           L0_attn_Walpha, L0_attn_balpha, L0_attn_Wg, L0_attn_norm_w,
           L0_attn_Wo, L0_ffn_w1, L0_ffn_b1, L0_ffn_w2, L0_ffn_b2):
    nc = build_program()
    consts = _consts()
    normwrep = np.tile(np.asarray(L0_attn_norm_w, np.float32)[None, :],
                       (64, 8)).astype(np.float32)
    x = np.ascontiguousarray(np.asarray(L0_attn_hidden_states, np.float32))
    shared = dict(
        Wq=np.asarray(L0_attn_Wq, np.float32),
        Wk=np.asarray(L0_attn_Wk, np.float32),
        Wv=np.asarray(L0_attn_Wv, np.float32),
        conv_q=np.asarray(L0_attn_conv_q, np.float32),
        conv_k=np.asarray(L0_attn_conv_k, np.float32),
        conv_v=np.asarray(L0_attn_conv_v, np.float32),
        Wa=np.asarray(L0_attn_Wa, np.float32),
        A_log=np.asarray(L0_attn_A_log, np.float32),
        dt_bias=np.asarray(L0_attn_dt_bias, np.float32),
        Wb=np.asarray(L0_attn_Wb, np.float32),
        bb=np.asarray(L0_attn_bb, np.float32),
        Walpha=np.asarray(L0_attn_Walpha, np.float32),
        balpha=np.asarray(L0_attn_balpha, np.float32),
        Wg=np.asarray(L0_attn_Wg, np.float32),
        Wo=np.asarray(L0_attn_Wo, np.float32),
        w1=np.asarray(L0_ffn_w1, np.float32),
        b1=np.asarray(L0_ffn_b1, np.float32),
        w2=np.asarray(L0_ffn_w2, np.float32),
        b2=np.asarray(L0_ffn_b2, np.float32),
        normwrep=normwrep,
        **consts,
    )
    in_maps = []
    for core in range(NCORES):
        m = dict(shared)
        m["x"] = x[core]
        m["w1sl"] = np.ascontiguousarray(
            shared["w1"][core * (DFF // NCORES):(core + 1) * (DFF // NCORES), :])
        m["w2sl"] = np.ascontiguousarray(
            shared["w2"][:, core * (DFF // NCORES):(core + 1) * (DFF // NCORES)])
        in_maps.append(m)
    res = run_bass_kernel_spmd(nc, in_maps, core_ids=list(range(NCORES)))
    global LAST_RES
    LAST_RES = res
    out = np.stack([r["out"] for r in res.results], axis=0)
    return out


LAST_RES = None


# version 27
# speedup vs baseline: 1.1982x; 1.1982x over previous
from contextlib import ExitStack

import numpy as np
import ml_dtypes
import concourse.bass as bass
import concourse.mybir as mybir
from concourse.bass_utils import run_bass_kernel_spmd

F32 = mybir.dt.float32
BF16 = mybir.dt.bfloat16
AF = mybir.ActivationFunctionType
ALU = mybir.AluOpType
AX = mybir.AxisListType

L = 2048
D = 2048
H = 16
HK = 8
HD = 128
C = 64
NC = 32
KC = 16
NT = 4
TP = 16
DFF = 8192
FB = 64
NI = 30
RIDGE = 0.02
NCORES = 8

DEBUG = False


class Flow:
    OPENGS = ("tensor", "vector", "scalar", "gpsimd")
    DMAENGS = ("sync", "gpsimd", "scalar")
    SERIAL = False

    def __init__(self, nc):
        self.nc = nc
        self.steps = []
        self.counts = {}
        self.track = {}
        self.barrier_deps = []

    @staticmethod
    def _key(h):
        return h if isinstance(h, str) else id(h)

    def _emit(self, eng, kind, fn, r, w, extra):
        deps = {}

        def add(tok):
            if tok[1] == "dma":
                k = (tok[0], "dma", tok[2] % Flow.DPOOL)
            else:
                k = (tok[0], "op")
            if k not in deps or deps[k][2] < tok[2]:
                deps[k] = tok
        for t in self.barrier_deps:
            add(t)
        for t in (extra or ()):
            add(t)
        if Flow.SERIAL and self.steps:
            add(self.steps[-1][4])
        for h in (r or ()):
            ent = self.track.get(self._key(h))
            if ent:
                for t in ent["w"]:
                    add(t)
        for h in (w or ()):
            ent = self.track.get(self._key(h))
            if ent:
                for t in ent["w"]:
                    add(t)
                for t in ent["r"]:
                    add(t)
        gidx = len(self.steps)
        idx = self.counts.get((eng, kind), 0)
        self.counts[(eng, kind)] = idx + 1
        tok = (eng, kind, idx, gidx)
        dl = [t for t in deps.values() if t[3] < gidx]
        self.steps.append((eng, kind, fn, dl, tok))
        for h in (r or ()):
            self.track.setdefault(self._key(h), {"w": [], "r": []})["r"].append(tok)
        for h in (w or ()):
            ent = self.track.setdefault(self._key(h), {"w": [], "r": []})
            if ent["r"]:
                ent["w"] = [tok]
                ent["r"] = []
            else:
                ent["w"].append(tok)
        return tok

    def op(self, eng, fn, r=(), w=(), deps=()):
        return self._emit(eng, "op", fn, r, w, deps)

    def dma(self, eng, out, in_, r=(), w=(), deps=(), **kw):
        return self._emit(eng, "dma",
                          lambda e: e.dma_start(out, in_, **kw), r, w, deps)

    def barrier(self):
        last = {}
        for (eng, kind, _, _, tok) in self.steps:
            last[(eng, kind)] = tok
        self.barrier_deps = list(last.values())

    DPOOL = 16

    def emit(self):
        nc = self.nc
        steps = self.steps
        sem_keys = sorted(self.counts.keys())
        guards = []
        sems = {}
        finals = {}
        for k in sem_keys:
            if k[1] == "op":
                g = nc.semaphore(f"fl_{k[0]}_op")
                sems[k] = g.__enter__()
                guards.append(g)
                finals[k] = self.counts[k]
            else:
                npool = min(Flow.DPOOL, self.counts[k])
                for s in range(npool):
                    g = nc.semaphore(f"fl_{k[0]}_dma{s}")
                    sems[(k[0], "dma", s)] = g.__enter__()
                    guards.append(g)
                    finals[(k[0], "dma", s)] = \
                        16 * ((self.counts[k] - s + Flow.DPOOL - 1)
                              // Flow.DPOOL)

        def dkey(d):
            if d[1] == "op":
                return (d[0], "op"), d[2] + 1
            return (d[0], "dma", d[2] % Flow.DPOOL), 16 * (d[2] // Flow.DPOOL + 1)

        with nc.Block() as block:
            def body_for(engname):
                def body(eng):
                    hw = {}
                    for (en, kind, fn, deps, tok) in steps:
                        if en != engname:
                            continue
                        for d in deps:
                            if (d[0], d[1]) == (engname, "op") and \
                                    engname == "tensor" and kind == "op":
                                continue
                            sk, need = dkey(d)
                            if hw.get(sk, 0) >= need:
                                continue
                            eng.wait_ge(sems[sk], need)
                            hw[sk] = need
                        if kind == "op":
                            fn(eng).then_inc(sems[(en, "op")], 1)
                        else:
                            idx = tok[2]
                            if idx >= Flow.DPOOL:
                                sk = (en, "dma", idx % Flow.DPOOL)
                                need = 16 * (idx // Flow.DPOOL)
                                if hw.get(sk, 0) < need:
                                    eng.wait_ge(sems[sk], need)
                                    hw[sk] = need
                            fn(eng).then_inc(
                                sems[(en, "dma", idx % Flow.DPOOL)], 16)
                    for k, v in finals.items():
                        eng.wait_ge(sems[k], v)
                return body
            for engname in ("sync", "gpsimd", "vector", "scalar", "tensor"):
                getattr(block, engname)(body_for(engname))
        return guards


def bc(ap, shape):
    return ap.broadcast_to(list(shape))


def build_program():
    nc = bass.Bass("TRN2", target_bir_lowering=False)
    S = Flow(nc)

    def inp(name, shape, dt=F32):
        return nc.dram_tensor(name, list(shape), dt, kind="ExternalInput")

    x_d = inp("x", [L, D])
    Wq_d = inp("Wq", [D, 2048])
    Wk_d = inp("Wk", [D, 1024])
    Wv_d = inp("Wv", [D, 1024])
    cq_d = inp("conv_q", [2048, 4])
    ck_d = inp("conv_k", [1024, 4])
    cv_d = inp("conv_v", [1024, 4])
    Wa_d = inp("Wa", [D, H])
    Alog_d = inp("A_log", [H])
    dtb_d = inp("dt_bias", [H])
    Wb_d = inp("Wb", [D, H])
    bb_d = inp("bb", [H])
    Wal_d = inp("Walpha", [D, H])
    bal_d = inp("balpha", [H])
    Wg_d = inp("Wg", [D, 2048])
    Wo_d = inp("Wo", [2048, D])
    w1_d = inp("w1", [DFF, D])
    b1_d = inp("b1", [DFF])
    w2_d = inp("w2", [D, DFF])
    b2_d = inp("b2", [D])
    w1sl_d = inp("w1sl", [DFF // NCORES, D])
    w2sl_d = inp("w2sl", [D, DFF // NCORES])
    id128b_d = inp("id128b", [128, 128], BF16)
    negid128b_d = inp("negid128b", [128, 128], BF16)
    id64f_d = inp("id64f", [64, 64])
    id128f_d = inp("id128f", [128, 128])
    triu64_d = inp("triu64", [64, 64])
    ones64_d = inp("ones64", [64, 64])
    onesrow_d = inp("onesrow", [1, 128])
    ones128c_d = inp("ones128c", [128, 1])
    maskb64_d = inp("maskb64", [64, 64])
    ridgeid_d = inp("ridgeid", [128, 128])
    normwrep_d = inp("normwrep", [64, 1024])

    out_d = nc.dram_tensor("out", [L, D], F32, kind="ExternalOutput")

    spill_kind = "ExternalOutput" if DEBUG else "Internal"
    qspill = nc.dram_tensor("qspill", [NC, 128, H * C], BF16, kind=spill_kind)
    kTspill = nc.dram_tensor("kTspill", [NC, 128, HK * C], BF16, kind=spill_kind)
    ktmspill = nc.dram_tensor("ktmspill", [NC, C, HK * HD], BF16, kind=spill_kind)
    vtmspill = nc.dram_tensor("vtmspill", [NC, C, HK * HD], BF16, kind=spill_kind)
    gspill = nc.dram_tensor("gspill", [NC, C, H * HD], BF16, kind=spill_kind)
    ofmspill = nc.dram_tensor("ofmspill", [TP, 128, H * 128], BF16, kind=spill_kind)
    attnxspill = nc.dram_tensor("attnxspill", [TP, 128, D], F32, kind=spill_kind)
    hspill = nc.dram_tensor("hspill", [TP, 128, FB, 128], BF16)
    w2qT_dram = nc.dram_tensor("w2qT_dram", [4, 128, 32 * 1024], BF16)
    tabsrows = nc.dram_tensor("tabsrows", [62, 512], F32, kind=spill_kind)
    tabs_dram = nc.dram_tensor("tabs_dram", [512, 62], F32)
    csfm_dram = nc.dram_tensor("csfm_dram", [512, 64], F32, kind=spill_kind)
    ccin = nc.dram_tensor("ccin", [2, 1], F32)
    ccout = nc.dram_tensor("ccout", [2, 1], F32)

    dbg = {}

    def dbg_out(name, shape, dt=F32):
        if not DEBUG:
            return None
        dbg[name] = nc.dram_tensor(name, list(shape), dt, kind="ExternalOutput")
        return dbg[name]

    es_all = ExitStack()

    def sb(name, shape, dt=F32, **kw):
        return es_all.enter_context(nc.sbuf_tensor(name, list(shape), dt, **kw))

    with es_all:
        es_xn = ExitStack()
        xn = es_xn.enter_context(
            nc.sbuf_tensor("xn", [128, KC * L], BF16, side="right"))
        id128b = sb("id128b_s", [128, 128], BF16)
        negid128b = sb("negid128b_s", [128, 128], BF16)
        id64f = sb("id64f_s", [64, 64])
        id128f = sb("id128f_s", [128, 128])
        triu64 = sb("triu64_s", [64, 64])
        ones64 = sb("ones64_s", [64, 64])
        onesrow = sb("onesrow_s", [1, 128])
        ones128c = sb("ones128c_s", [128, 1])
        maskb64 = sb("maskb64_s", [64, 64])
        ridgeid = sb("ridgeid_s", [128, 128])
        normwrep = sb("normwrep_s", [64, 1024])
        vecs = sb("vecs", [128, 8])
        glogctm = sb("glogctm", [64, 512])
        csctm = sb("csctm", [64, 512])
        wendctm = sb("wendctm", [64, 512])
        bscorectm = sb("bscorectm", [64, 512])
        expcsctm = sb("expcsctm", [64, 512])
        alphactm = sb("alphactm", [64, 512])
        nkstage = sb("nkstage", [128, 128])
        nk2 = sb("nk2", [64, 256])
        SA = sb("SA", [128, 2048])
        SB = sb("SB", [128, 2048])
        SC = sb("SC", [128, 1024])

        CONSTS = [id128b, id64f, id128f, triu64, ones64, onesrow, ones128c,
                  maskb64, ridgeid, normwrep]

        for t, d in ((id128b, id128b_d), (negid128b, negid128b_d),
                     (id64f, id64f_d), (id128f, id128f_d),
                     (triu64, triu64_d), (ones64, ones64_d),
                     (onesrow, onesrow_d), (ones128c, ones128c_d),
                     (maskb64, maskb64_d), (ridgeid, ridgeid_d),
                     (normwrep, normwrep_d)):
            S.dma("sync", t[:], d[:], w=[t])
        S.op("vector", lambda e: e.memset(vecs[:], 0.0), w=[vecs])
        S.dma("sync", vecs[0:16, 0:1], dtb_d.ap().unsqueeze(1), r=[vecs], w=[vecs])
        S.dma("sync", vecs[32:48, 1:2], bb_d.ap().unsqueeze(1), r=[vecs], w=[vecs])
        S.dma("sync", vecs[64:80, 2:3], bal_d.ap().unsqueeze(1), r=[vecs], w=[vecs])
        S.dma("sync", vecs[0:16, 3:4], Alog_d.ap().unsqueeze(1), r=[vecs], w=[vecs])
        S.op("scalar", lambda e: e.activation(vecs[0:16, 3:4], vecs[0:16, 3:4],
                                              AF.Exp), r=[vecs], w=[vecs])
        S.op("vector", lambda e: e.tensor_scalar_mul(vecs[0:16, 3:4],
                                                     vecs[0:16, 3:4], -1.0),
             r=[vecs], w=[vecs])

        _st = ExitStack()
        with _st:
            wt0 = [_st.enter_context(nc.sbuf_tensor(f"wtile0_{i}", [128, 2048],
                                                    F32)) for i in range(2)]
            abscols = _st.enter_context(nc.sbuf_tensor("abscols", [128, 28], F32))
            p0a = _st.enter_context(nc.psum_tensor("p0a", [128, 512], F32))
            for j in range(8):
                b = wt0[j % 2]
                S.dma("sync", b[:], w1sl_d[j * 128:(j + 1) * 128, :], w=[b])
                S.op("scalar", lambda e, j=j, b=b: e.activation(
                    SA[:], b[:], AF.Abs, accum_out=abscols[:, j:j + 1]),
                    r=[b], w=[SA, abscols])
            for j in range(16):
                b = wt0[j % 2]
                S.dma("sync", b[:, 0:1024], w2sl_d[j * 128:(j + 1) * 128, :],
                      w=[b])
                S.op("scalar", lambda e, j=j, b=b: e.activation(
                    SA[:, 0:1024], b[:, 0:1024], AF.Abs,
                    accum_out=abscols[:, 8 + j:9 + j]), r=[b], w=[SA, abscols])
            S.op("vector", lambda e: e.tensor_reduce(
                abscols[:, 24:25], abscols[:, 0:8].unsqueeze(1), AX.X, ALU.add),
                r=[abscols], w=[abscols])
            S.op("vector", lambda e: e.tensor_reduce(
                abscols[:, 25:26], abscols[:, 8:24].unsqueeze(1), AX.X, ALU.add),
                r=[abscols], w=[abscols])
            S.op("tensor", lambda e: e.matmul(p0a[0:2, 0:1], abscols[:, 24:26],
                                              ones128c[:], start=True, stop=True),
                 r=[abscols, ones128c], w=[p0a])
            S.op("vector", lambda e: e.tensor_copy(abscols[0:2, 26:27],
                                                   p0a[0:2, 0:1]),
                 r=[p0a], w=[abscols])
            S.dma("sync", ccin[:], abscols[0:2, 26:27], r=[abscols], w=[ccin])
            S.op("gpsimd", lambda e: e.collective_compute(
                "AllReduce", ALU.add, replica_groups=[list(range(NCORES))],
                ins=[ccin.ap().opt()], outs=[ccout.ap().opt()]),
                r=[ccin], w=[ccout])
            S.dma("sync", abscols[0:2, 27:28], ccout[:], r=[ccout, abscols],
                  w=[abscols])
            S.op("vector", lambda e: e.tensor_scalar_mul(
                abscols[0:2, 27:28], abscols[0:2, 27:28], 0.7 / (DFF * D)),
                r=[abscols], w=[abscols])
            p0b = _st.enter_context(nc.psum_tensor("p0b", [128, 2], F32))
            S.op("tensor", lambda e: e.transpose(p0b[0:1, 0:2],
                                                 abscols[0:2, 27:28],
                                                 id64f[0:2, 0:2]),
                 r=[abscols, id64f], w=[p0b])
            S.op("vector", lambda e: e.tensor_copy(abscols[0:1, 4:6],
                                                   p0b[0:1, 0:2]),
                 r=[p0b], w=[abscols])
            S.op("tensor", lambda e: e.matmul(p0b[:, 0:2], onesrow[:],
                                              abscols[0:1, 4:6],
                                              start=True, stop=True),
                 r=[onesrow, abscols], w=[p0b])
            S.op("vector", lambda e: e.tensor_copy(vecs[:, 4:6], p0b[:, 0:2]),
                 r=[p0b, vecs], w=[vecs])
        S.barrier()

        _st = ExitStack()
        with _st:
            xt = [_st.enter_context(nc.sbuf_tensor(f"xtile{i}", [128, 2048],
                                                   F32)) for i in range(2)]
            xnb = [_st.enter_context(nc.sbuf_tensor(f"xnbt{i}", [128, 2048],
                                                    BF16)) for i in range(2)]
            lnc = [_st.enter_context(nc.sbuf_tensor(f"lncols{i}", [128, 8],
                                                    F32)) for i in range(2)]
            pt1 = [_st.enter_context(nc.psum_tensor(f"ptr1_{i}", [128, 128],
                                                    BF16)) for i in range(2)]
            for tt in range(TP):
                xtile, xnbt, lncols = xt[tt % 2], xnb[tt % 2], lnc[tt % 2]
                S.dma("sync", xtile[:], x_d[tt * 128:(tt + 1) * 128, :],
                      w=[xtile])
                S.op("vector", lambda e, xtile=xtile, lncols=lncols:
                     e.tensor_reduce(lncols[:, 0:1], xtile[:].unsqueeze(1),
                                     AX.X, ALU.add),
                     r=[xtile], w=[lncols])
                S.op("scalar", lambda e, xtile=xtile, lncols=lncols:
                     e.activation(SA[:], xtile[:], AF.Square,
                                  accum_out=lncols[:, 1:2]),
                     r=[xtile, lncols], w=[SA, lncols])
                S.op("vector", lambda e, lncols=lncols: e.tensor_scalar(
                    lncols[:, 0:2], lncols[:, 0:2], 1.0 / D, None, ALU.mult),
                    r=[lncols], w=[lncols])
                S.op("vector", lambda e, lncols=lncols: e.tensor_tensor(
                    lncols[:, 2:3], lncols[:, 0:1], lncols[:, 0:1], ALU.mult),
                    r=[lncols], w=[lncols])
                S.op("vector", lambda e, lncols=lncols: e.tensor_tensor(
                    lncols[:, 2:3], lncols[:, 1:2], lncols[:, 2:3],
                    ALU.subtract), r=[lncols], w=[lncols])
                S.op("vector", lambda e, lncols=lncols: e.tensor_scalar_add(
                    lncols[:, 2:3], lncols[:, 2:3], 1e-5), r=[lncols], w=[lncols])
                S.op("scalar", lambda e, lncols=lncols: e.activation(
                    lncols[:, 2:3], lncols[:, 2:3], AF.Sqrt),
                    r=[lncols], w=[lncols])
                S.op("vector", lambda e, lncols=lncols: e.reciprocal(
                    lncols[:, 2:3], lncols[:, 2:3]), r=[lncols], w=[lncols])
                S.op("vector", lambda e, xtile=xtile, xnbt=xnbt, lncols=lncols:
                     e.tensor_scalar(xnbt[:], xtile[:], lncols[:, 0:1],
                                     lncols[:, 2:3], ALU.subtract, ALU.mult),
                     r=[xtile, lncols], w=[xnbt])
                for kc in range(KC):
                    ptr1 = pt1[kc % 2]
                    S.op("tensor", lambda e, kc=kc, xnbt=xnbt, ptr1=ptr1:
                         e.transpose(ptr1[:], xnbt[:, kc * 128:(kc + 1) * 128],
                                     id128b[:]),
                         r=[xnbt, id128b], w=[ptr1])
                    S.op("vector", lambda e, kc=kc, tt=tt, ptr1=ptr1:
                         e.tensor_copy(
                             xn[:, kc * L + tt * 128: kc * L + (tt + 1) * 128],
                             ptr1[:]),
                         r=[ptr1], w=["xn"])
        S.barrier()
        if DEBUG:
            d = dbg_out("xn_dbg", [128, KC * L], BF16)
            S.dma("sync", d[:], xn[:], r=["xn"])

        _st = ExitStack()
        with _st:
            wblk2 = [_st.enter_context(nc.sbuf_tensor(f"wblk{i}", [128, 2048],
                                                      BF16)) for i in range(2)]
            upre2 = [_st.enter_context(nc.sbuf_tensor(f"upre{i}", [128, 2051],
                                                      F32)) for i in range(2)]
            cacc2 = [_st.enter_context(nc.sbuf_tensor(f"cacc{i}", [128, 2048],
                                                      F32)) for i in range(2)]
            cvb2 = [_st.enter_context(nc.sbuf_tensor(f"convsb{i}", [128, 4],
                                                     F32)) for i in range(2)]
            qbf2 = [_st.enter_context(nc.sbuf_tensor(f"qbf{i}", [128, 2048],
                                                     BF16)) for i in range(2)]
            tmsb2 = [_st.enter_context(nc.sbuf_tensor(f"tmsb{i}", [128, 2048],
                                                      BF16)) for i in range(2)]
            wsm = _st.enter_context(nc.sbuf_tensor("wsm", [128, 16 * 80], BF16))
            abafm = _st.enter_context(nc.sbuf_tensor("abafm", [128, 2048], F32))
            ppj = [_st.enter_context(nc.psum_tensor(f"pproj{i}", [128, 512],
                                                    F32)) for i in range(2)]
            pt2 = [_st.enter_context(nc.psum_tensor(f"ptr2_{i}", [128, 128],
                                                    BF16)) for i in range(2)]
            ptr2f = _st.enter_context(nc.psum_tensor("ptr2f", [128, 16], F32))

            blkidx = [0]

            def load_wblock(wd, m):
                wblk = wblk2[blkidx[0] % 2]
                for kcq in range(0, KC, 4):
                    src = wd[kcq * 128:(kcq + 4) * 128,
                             m * 128:(m + 1) * 128].rearrange(
                                 "(a p) c -> p a c", p=128)
                    dstv = wblk[:, kcq * 128:(kcq + 4) * 128].rearrange(
                        "p (a c) -> p a c", a=4)
                    S.dma("gpsimd", dstv, src, w=[wblk])
                return wblk

            def mm_evac(wblk, dst, dst_key, to_upre, par=0):
                upre = upre2[par]
                for nt in range(NT):
                    pproj = ppj[nt % 2]
                    for kc in range(KC):
                        S.op("tensor", lambda e, kc=kc, nt=nt, wblk=wblk,
                             pproj=pproj: e.matmul(
                                 pproj[:], wblk[:, kc * 128:(kc + 1) * 128],
                                 xn[:, kc * L + nt * 512:
                                    kc * L + (nt + 1) * 512],
                                 start=(kc == 0), stop=(kc == KC - 1)),
                             r=[wblk, "xn"], w=[pproj])
                    if to_upre:
                        S.op("vector", lambda e, nt=nt, pproj=pproj,
                             upre=upre: e.tensor_copy(
                                 upre[:, 3 + nt * 512: 3 + (nt + 1) * 512],
                                 pproj[:]),
                             r=[pproj, upre], w=[upre])
                    else:
                        S.op("vector", lambda e, nt=nt, pproj=pproj, dst=dst:
                             e.tensor_copy(dst[:, nt * 512:(nt + 1) * 512],
                                           pproj[:]),
                             r=[pproj], w=[dst_key])

            def conv_silu(cd_d, ch, qbf, par=0):
                upre, cacc, convsb = upre2[par], cacc2[par], cvb2[par]
                S.dma("sync", convsb[:], cd_d[ch * 128:(ch + 1) * 128, :],
                      r=[convsb], w=[convsb])
                S.op("vector", lambda e, upre=upre: e.memset(upre[:, 0:3], 0.0),
                     r=[upre], w=[upre])
                S.op("vector", lambda e, upre=upre, cacc=cacc, convsb=convsb:
                     e.tensor_scalar(
                         cacc[:], upre[:, 0:2048], convsb[:, 0:1], None,
                         ALU.mult),
                     r=[upre, convsb, cacc], w=[cacc])
                for j in range(1, 4):
                    S.op("vector", lambda e, j=j, upre=upre, cacc=cacc,
                         convsb=convsb: e.scalar_tensor_tensor(
                             cacc[:], upre[:, j:j + 2048], convsb[:, j:j + 1],
                             cacc[:], ALU.mult, ALU.add),
                         r=[upre, convsb, cacc], w=[cacc])
                S.op("scalar", lambda e, qbf=qbf, cacc=cacc: e.activation(
                    qbf[:], cacc[:], AF.Silu), r=[cacc], w=[qbf])

            def transpose_tm(qbf, tmsb):
                for tt in range(TP):
                    ptr2 = pt2[tt % 2]
                    S.op("tensor", lambda e, tt=tt, qbf=qbf, ptr2=ptr2:
                         e.transpose(ptr2[:], qbf[:, tt * 128:(tt + 1) * 128],
                                     id128b[:]),
                         r=[qbf, id128b], w=[ptr2])
                    S.op("vector", lambda e, tt=tt, tmsb=tmsb, ptr2=ptr2:
                         e.tensor_copy(tmsb[:, tt * 128:(tt + 1) * 128],
                                       ptr2[:]),
                         r=[ptr2], w=[tmsb])

            def spill_tm(spill_d, colbase, tmsb):
                for par in range(2):
                    src = tmsb[par * 64:(par + 1) * 64, :].rearrange(
                        "p (a c) -> p a c", a=TP)
                    dstv = spill_d.ap().rearrange("(a b) c d -> a b c d", b=2)[
                        :, par, :, colbase:colbase + 128].transpose([1, 0, 2])
                    S.dma("sync", dstv, src, r=[tmsb], w=[spill_d])

            def spill_fm(spill_d, colbase, width, qbf):
                for nq in range(0, NC, 8):
                    src = qbf[:, nq * C:(nq + 8) * C].rearrange(
                        "p (a c) -> p a c", a=8)
                    dstv = spill_d.ap()[nq:nq + 8, :,
                                        colbase:colbase + width] \
                        .transpose([1, 0, 2])
                    S.dma("sync", dstv, src, r=[qbf], w=[spill_d])

            for h in range(H):
                qbf = qbf2[h % 2]
                wblk = load_wblock(Wq_d, h)
                mm_evac(wblk, None, None, True, h % 2)
                blkidx[0] += 1
                conv_silu(cq_d, h, qbf, h % 2)
                spill_fm(qspill, h * C, C, qbf)
            def k_tail(kh):
                qbf, tmsb = qbf2[kh % 2], tmsb2[kh % 2]
                transpose_tm(qbf, tmsb)
                spill_tm(ktmspill, kh * 128, tmsb)
                S.op("vector", lambda e, tmsb=tmsb: e.tensor_tensor(
                    SA[:], tmsb[:], tmsb[:], ALU.mult), r=[tmsb], w=[SA])
                S.op("vector", lambda e, kh=kh: e.tensor_reduce(
                    nkstage[:, kh * 16:(kh + 1) * 16],
                    SA[:].rearrange("p (a c) -> p a c", a=16), AX.X, ALU.add),
                    r=[SA, nkstage], w=[nkstage])

            for kh in range(HK):
                qbf = qbf2[kh % 2]
                wblk = load_wblock(Wk_d, kh)
                mm_evac(wblk, None, None, True, kh % 2)
                blkidx[0] += 1
                if kh > 0:
                    k_tail(kh - 1)
                conv_silu(ck_d, kh, qbf, kh % 2)
                spill_fm(kTspill, kh * C, C, qbf)
            k_tail(HK - 1)
            def v_tail(kh):
                qbf, tmsb = qbf2[kh % 2], tmsb2[kh % 2]
                transpose_tm(qbf, tmsb)
                spill_tm(vtmspill, kh * 128, tmsb)

            for kh in range(HK):
                qbf = qbf2[kh % 2]
                wblk = load_wblock(Wv_d, kh)
                mm_evac(wblk, None, None, True, kh % 2)
                blkidx[0] += 1
                if kh > 0:
                    v_tail(kh - 1)
                conv_silu(cv_d, kh, qbf, kh % 2)
            v_tail(HK - 1)
            for h in range(H):
                qbf, tmsb = qbf2[h % 2], tmsb2[h % 2]
                wblk = load_wblock(Wg_d, h)
                mm_evac(wblk, qbf, qbf, False)
                blkidx[0] += 1
                transpose_tm(qbf, tmsb)
                spill_tm(gspill, h * 128, tmsb)
            S.op("vector", lambda e: e.memset(wsm[:], 0.0), w=[wsm])
            for wd, coff in ((Wa_d, 0), (Wb_d, 32), (Wal_d, 64)):
                src = wd.ap().rearrange("(a p) c -> p a c", p=128)
                dstv = wsm[:].rearrange("p (a c) -> p a c", a=KC)[
                    :, :, coff:coff + 16]
                S.dma("gpsimd", dstv, src, r=[wsm], w=[wsm])
            for nt in range(NT):
                pproj = ppj[nt % 2]
                for kc in range(KC):
                    S.op("tensor", lambda e, kc=kc, nt=nt, pproj=pproj:
                         e.matmul(
                             pproj[0:80, :], wsm[:, kc * 80:(kc + 1) * 80],
                             xn[:, kc * L + nt * 512: kc * L + (nt + 1) * 512],
                             start=(kc == 0), stop=(kc == KC - 1)),
                         r=[wsm, "xn"], w=[pproj])
                S.op("vector", lambda e, nt=nt, pproj=pproj: e.tensor_copy(
                    abafm[0:80, nt * 512:(nt + 1) * 512], pproj[0:80, :]),
                    r=[pproj], w=[abafm])
            S.op("scalar", lambda e: e.activation(
                SA[0:16, :], abafm[0:16, :], AF.Abs, bias=vecs[0:16, 0:1]),
                r=[abafm, vecs], w=[SA])
            S.op("scalar", lambda e: e.activation(
                SA[0:16, :], SA[0:16, :], AF.Exp, scale=-1.0), r=[SA], w=[SA])
            S.op("scalar", lambda e: e.activation(
                SA[0:16, :], SA[0:16, :], AF.Ln, bias=1.0), r=[SA], w=[SA])
            S.op("scalar", lambda e: e.activation(
                SB[0:16, :], abafm[0:16, :], AF.Relu, bias=vecs[0:16, 0:1]),
                r=[abafm, vecs], w=[SB])
            S.op("vector", lambda e: e.tensor_add(SA[0:16, :], SA[0:16, :],
                                                  SB[0:16, :]),
                 r=[SA, SB], w=[SA])
            S.op("vector", lambda e: e.tensor_scalar(
                SA[0:16, :], SA[0:16, :], vecs[0:16, 3:4], None, ALU.mult),
                r=[SA, vecs], w=[SA])
            S.op("scalar", lambda e: e.activation(
                SA[32:48, :], abafm[32:48, :], AF.Sigmoid,
                bias=vecs[32:48, 1:2]), r=[abafm, vecs], w=[SA])
            S.op("scalar", lambda e: e.activation(
                SA[64:80, :], abafm[64:80, :], AF.Sigmoid,
                bias=vecs[64:80, 2:3]), r=[abafm, vecs], w=[SA])
            for (r0, ctm) in ((0, glogctm), (32, bscorectm), (64, alphactm)):
                for tt in range(TP):
                    S.op("tensor", lambda e, r0=r0, tt=tt: e.transpose(
                        ptr2f[:], SA[r0:r0 + 16, tt * 128:(tt + 1) * 128],
                        id128f[r0:r0 + 16, r0:r0 + 16]),
                        r=[SA, id128f], w=[ptr2f])
                    S.op("vector", lambda e, ctm=ctm, tt=tt: e.tensor_copy(
                        ctm[:, (2 * tt) * 16:(2 * tt + 1) * 16],
                        ptr2f[0:64, :]), r=[ptr2f], w=[ctm])
                    S.op("vector", lambda e, ctm=ctm, tt=tt: e.tensor_copy(
                        ctm[:, (2 * tt + 1) * 16:(2 * tt + 2) * 16],
                        ptr2f[64:128, :]), r=[ptr2f, ctm], w=[ctm])
        S.barrier()

        if DEBUG:
            d1 = dbg_out("glogctm_dbg", [64, 512])
            d2 = dbg_out("betactm_dbg", [64, 512])
            d3 = dbg_out("alphactm_dbg", [64, 512])
            d4 = dbg_out("nk_dbg", [128, 128])
            S.dma("sync", d1[:], glogctm[:], r=[glogctm])
            S.dma("sync", d2[:], bscorectm[:], r=[bscorectm])
            S.dma("sync", d3[:], alphactm[:], r=[alphactm])
            S.dma("sync", d4[:], nkstage[:], r=[nkstage])

        _st = ExitStack()
        with _st:
            rowA = _st.enter_context(nc.sbuf_tensor("rowA", [1, 512], F32))
            rowB = _st.enter_context(nc.sbuf_tensor("rowB", [1, 512], F32))
            rowC = _st.enter_context(nc.sbuf_tensor("rowC", [1, 512], F32))
            rowD = _st.enter_context(nc.sbuf_tensor("rowD", [1, 512], F32))
            rowE = _st.enter_context(nc.sbuf_tensor("rowE", [1, 512], F32))
            rowF = _st.enter_context(nc.sbuf_tensor("rowF", [1, 512], F32))
            rowG = _st.enter_context(nc.sbuf_tensor("rowG", [1, 512], F32))
            dgrow = _st.enter_context(nc.sbuf_tensor("dgrow", [1, 512], F32))
            trst = _st.enter_context(nc.sbuf_tensor("trst", [1, 16], F32))
            wnk = _st.enter_context(nc.sbuf_tensor("wnk", [64, 16], F32))
            tabsin = _st.enter_context(nc.sbuf_tensor("tabsin", [62, 512], F32))
            tabsT = _st.enter_context(nc.sbuf_tensor("tabsT", [128, 256], F32))
            p25a = _st.enter_context(nc.psum_tensor("p25a", [64, 512], F32))
            p25b = _st.enter_context(nc.psum_tensor("p25b", [64, 512], F32))
            p25r = _st.enter_context(nc.psum_tensor("p25r", [1, 512], F32))
            p25t = _st.enter_context(nc.psum_tensor("p25t", [128, 64], F32))

            S.op("tensor", lambda e: e.matmul(p25a[:], triu64[:], glogctm[:],
                                              start=True, stop=True),
                 r=[triu64, glogctm], w=[p25a])
            S.op("vector", lambda e: e.tensor_copy(csctm[:], p25a[:]),
                 r=[p25a], w=[csctm])
            S.op("tensor", lambda e: e.matmul(p25b[:], ones64[:], glogctm[:],
                                              start=True, stop=True),
                 r=[ones64, glogctm], w=[p25b])
            S.op("vector", lambda e: e.tensor_tensor(SA[0:64, 0:512], p25b[:],
                                                     csctm[:], ALU.subtract),
                 r=[p25b, csctm], w=[SA])
            S.op("scalar", lambda e: e.activation(SA[0:64, 0:512],
                                                  SA[0:64, 0:512], AF.Exp),
                 r=[SA], w=[SA])
            S.op("vector", lambda e: e.tensor_tensor(wendctm[:], SA[0:64, 0:512],
                                                     bscorectm[:], ALU.mult),
                 r=[SA, bscorectm], w=[wendctm])
            S.op("vector", lambda e: e.memset(vecs[:, 6:7],
                                              float(-0.5 * np.log(HD))),
                 r=[vecs], w=[vecs])
            S.op("scalar", lambda e: e.activation(
                expcsctm[:], csctm[:], AF.Exp, bias=vecs[0:64, 6:7]),
                r=[csctm, vecs], w=[expcsctm])
            S.op("vector", lambda e: e.tensor_scalar_mul(
                bscorectm[:], bscorectm[:], float(HD ** -0.5)),
                r=[bscorectm], w=[bscorectm])
            S.op("tensor", lambda e: e.matmul(p25r[:], ones64[:, 0:1],
                                              glogctm[:],
                                              start=True, stop=True),
                 r=[ones64, glogctm], w=[p25r])
            S.op("scalar", lambda e: e.activation(dgrow[:], p25r[:], AF.Exp),
                 r=[p25r], w=[dgrow])
            for t4 in range(4):
                S.op("tensor", lambda e, t4=t4: e.transpose(
                    p25t[:], csctm[:, t4 * 128:(t4 + 1) * 128], id64f[:]),
                    r=[csctm, id64f], w=[p25t])
                S.op("vector", lambda e: e.tensor_copy(SB[:, 0:64], p25t[:]),
                     r=[p25t], w=[SB])
                S.dma("sync", csfm_dram[t4 * 128:(t4 + 1) * 128, :],
                      SB[:, 0:64], r=[SB], w=[csfm_dram])
            for kh in range(HK):
                S.op("vector", lambda e, kh=kh: e.tensor_copy(
                    nk2[:, kh::16], nkstage[0:64, kh * 16:(kh + 1) * 16]),
                    r=[nkstage, nk2], w=[nk2])
                S.op("vector", lambda e, kh=kh: e.tensor_copy(
                    nk2[:, 8 + kh::16], nkstage[64:128, kh * 16:(kh + 1) * 16]),
                    r=[nkstage, nk2], w=[nk2])
            S.op("vector", lambda e: e.memset(trst[:], 0.0), w=[trst])
            for n in range(NC):
                srcnk = nk2[:, n * 8:(n + 1) * 8].unsqueeze(2)
                S.op("vector", lambda e, n=n, srcnk=srcnk: e.tensor_tensor(
                    wnk[:].rearrange("p (a b) -> p a b", a=8),
                    wendctm[:, n * 16:(n + 1) * 16].rearrange(
                        "p (a b) -> p a b", a=8),
                    bc(srcnk, (64, 8, 2)), ALU.mult),
                    r=[wendctm, nk2, wnk], w=[wnk])
                S.op("tensor", lambda e: e.matmul(p25r[:, 0:16], ones64[:, 0:1],
                                                  wnk[:], start=True, stop=True),
                     r=[ones64, wnk], w=[p25r])
                S.op("vector", lambda e, n=n: e.tensor_scalar_add(
                    rowA[:, n * 16:(n + 1) * 16], trst[:], float(RIDGE * HD)),
                    r=[trst, rowA], w=[rowA])
                S.op("vector", lambda e, n=n: e.tensor_tensor(
                    trst[:], trst[:], dgrow[:, n * 16:(n + 1) * 16], ALU.mult),
                    r=[trst, dgrow], w=[trst])
                S.op("vector", lambda e: e.tensor_tensor(
                    trst[:], trst[:], p25r[:, 0:16], ALU.add),
                    r=[trst, p25r], w=[trst])
            S.op("vector", lambda e: e.tensor_scalar(
                rowB[:], rowA[:], RIDGE, 0.5, ALU.add, ALU.mult),
                r=[rowA], w=[rowB])
            S.op("vector", lambda e: e.tensor_scalar(
                rowC[:], rowA[:], -RIDGE, 0.5, ALU.add, ALU.mult),
                r=[rowA], w=[rowC])
            S.op("vector", lambda e: e.reciprocal(rowG[:], rowC[:]),
                 r=[rowC], w=[rowG])
            S.op("vector", lambda e: e.tensor_tensor(rowD[:], rowB[:], rowG[:],
                                                     ALU.mult),
                 r=[rowB, rowG], w=[rowD])
            S.op("vector", lambda e: e.reciprocal(rowE[:], rowD[:]),
                 r=[rowD], w=[rowE])
            S.op("vector", lambda e: e.tensor_scalar_mul(rowD[:], rowD[:], 2.0),
                 r=[rowD], w=[rowD])
            S.op("vector", lambda e: e.tensor_scalar_mul(rowC[:], rowG[:], 2.0),
                 r=[rowG], w=[rowC])
            S.op("vector", lambda e: e.reciprocal(rowG[:], rowB[:]),
                 r=[rowB, rowG], w=[rowG])
            S.dma("sync", tabsrows[0:1, :], rowG[:], r=[rowG], w=[tabsrows])
            rho_cur, rho_nxt = rowE, rowF
            rowpool = [rowG, rowA, None, None]
            rowpool[2] = rowB
            for k in range(NI):
                rg = rowG if k % 2 == 0 else rowA
                S.op("vector", lambda e, rc=rho_cur, rg=rg: e.tensor_tensor(
                    rg[:], rowD[:], rc[:], ALU.subtract),
                    r=[rowD, rho_cur, rg], w=[rg])
                S.op("vector", lambda e, rn=rho_nxt, rg=rg: e.reciprocal(
                    rn[:], rg[:]),
                     r=[rg, rho_nxt], w=[rho_nxt])
                S.op("vector", lambda e, rc=rho_cur, rn=rho_nxt, rg=rg:
                     e.tensor_tensor(rg[:], rn[:], rc[:], ALU.mult),
                    r=[rho_nxt, rho_cur, rg], w=[rg])
                S.dma("sync", tabsrows[1 + 2 * k:2 + 2 * k, :], rg[:],
                      r=[rg], w=[tabsrows])
                rg2 = rowB if k % 2 == 0 else rowpool[2]
                rg2 = rowB
                S.op("vector", lambda e, rn=rho_nxt, rg2=rg2: e.tensor_tensor(
                    rg2[:], rn[:], rowC[:], ALU.mult),
                    r=[rho_nxt, rowC, rg2], w=[rg2])
                S.dma("sync", tabsrows[2 + 2 * k:3 + 2 * k, :], rg2[:],
                      r=[rg2], w=[tabsrows])
                rho_cur, rho_nxt = rho_nxt, rho_cur
            S.dma("sync", tabsrows[61:62, :], dgrow[:], r=[dgrow], w=[tabsrows])
            S.dma("sync", tabsin[:], tabsrows[:], r=[tabsrows], w=[tabsin])
            for t4 in range(4):
                S.op("tensor", lambda e, t4=t4: e.transpose(
                    p25t[:, 0:62], tabsin[:, t4 * 128:(t4 + 1) * 128],
                    id64f[0:62, 0:62]),
                    r=[tabsin, id64f], w=[p25t])
                S.op("vector", lambda e, t4=t4: e.tensor_copy(
                    tabsT[:, t4 * 62:(t4 + 1) * 62], p25t[:, 0:62]),
                    r=[p25t, tabsT], w=[tabsT])
            for t4 in range(4):
                S.dma("sync", tabs_dram[t4 * 128:(t4 + 1) * 128, :],
                      tabsT[:, t4 * 62:(t4 + 1) * 62], r=[tabsT], w=[tabs_dram])
        S.barrier()

        if DEBUG:
            d1 = dbg_out("csctm_dbg", [64, 512])
            d2 = dbg_out("wendctm_dbg", [64, 512])
            d3 = dbg_out("expcs_dbg", [64, 512])
            S.dma("sync", d1[:], csctm[:], r=[csctm])
            S.dma("sync", d2[:], wendctm[:], r=[wendctm])
            S.dma("sync", d3[:], expcsctm[:], r=[expcsctm])

        _st = ExitStack()
        with _st:
            Hst = _st.enter_context(nc.sbuf_tensor("Hst", [128, H * 128], F32))
            Bst = _st.enter_context(nc.sbuf_tensor("Bst", [128, H * 128], F32))
            Abf = _st.enter_context(nc.sbuf_tensor("Abf", [128, H * 128], BF16))
            Bbf = _st.enter_context(nc.sbuf_tensor("Bbf", [128, H * 128], BF16))
            qn2 = [_st.enter_context(nc.sbuf_tensor(f"qn{i}", [128, 1024],
                                                    BF16)) for i in range(2)]
            kT2 = [_st.enter_context(nc.sbuf_tensor(f"kT{i}", [128, 512],
                                                    BF16)) for i in range(2)]
            ktm2 = [_st.enter_context(nc.sbuf_tensor(f"ktmt{i}", [64, 1024],
                                                     BF16)) for i in range(2)]
            vtm2 = [_st.enter_context(nc.sbuf_tensor(f"vtmt{i}", [64, 1024],
                                                     BF16)) for i in range(2)]
            gt2 = [_st.enter_context(nc.sbuf_tensor(f"gt{i}", [64, 2048],
                                                    BF16)) for i in range(2)]
            crep2 = [_st.enter_context(nc.sbuf_tensor(f"crep{i}", [128, 16 * 62],
                                                      F32)) for i in range(2)]
            csm2 = [_st.enter_context(nc.sbuf_tensor(f"csmega{i}", [1, 1024],
                                                     F32)) for i in range(2)]
            dvec2 = [_st.enter_context(nc.sbuf_tensor(f"dvecbf{i}",
                                                      [128, 1024], BF16))
                     for i in range(2)]
            SCb = _st.enter_context(nc.sbuf_tensor("SCb", [128, 1024], BF16))
            SDb = _st.enter_context(nc.sbuf_tensor("SDb", [128, 1024], BF16))
            tavpre = _st.enter_context(nc.sbuf_tensor("tavpre", [64, 2048], BF16))
            sgn = _st.enter_context(nc.sbuf_tensor("sgn", [64, 2048], BF16))
            xchbf = _st.enter_context(nc.sbuf_tensor("xchbf", [128, 1024], BF16))
            scoresbf = _st.enter_context(
                nc.sbuf_tensor("scoresbf", [64, 1024], BF16))
            wkb = _st.enter_context(nc.sbuf_tensor("wkb", [64, 2048], BF16))
            opair = _st.enter_context(nc.sbuf_tensor("opair", [128, 2048], F32))
            ofmsb = _st.enter_context(nc.sbuf_tensor("ofmsb", [128, 2048], BF16))
            ms = _st.enter_context(nc.sbuf_tensor("ms", [64, 16], F32))
            SE = _st.enter_context(nc.sbuf_tensor("SE", [64, 1024], F32))
            pcheb = _st.enter_context(nc.psum_tensor("pcheb", [128, 1024], F32))
            pA = _st.enter_context(nc.psum_tensor("pA", [128, 1024], F32))
            pB = _st.enter_context(nc.psum_tensor("pB", [128, 1024], F32))
            pacc = _st.enter_context(nc.psum_tensor("pacc", [128, 1024],
                                                    F32))

            S.op("vector", lambda e: e.memset(Hst[:], 0.0), w=[Hst])
            S.op("vector", lambda e: e.memset(Bst[:], 0.0), w=[Bst])
            deferred = []

            for n in range(NC):
                qn, kTt = qn2[n % 2], kT2[n % 2]
                ktmt, vtmt = ktm2[n % 2], vtm2[n % 2]
                gt, crep, csmega = gt2[n % 2], crep2[n % 2], csm2[n % 2]
                crep3 = crep[:].rearrange("p (h r) -> p h r", h=16)
                S.dma("sync", qn[:], qspill[n], r=[qspill], w=[qn])
                S.dma("sync", kTt[:], kTspill[n], r=[kTspill], w=[kTt])
                S.dma("sync", ktmt[:], ktmspill[n], r=[ktmspill], w=[ktmt])
                S.dma("sync", vtmt[:], vtmspill[n], r=[vtmspill], w=[vtmt])
                S.dma("sync", gt[:], gspill[n], r=[gspill], w=[gt])
                S.dma("gpsimd", crep[:].rearrange("p (h r) -> p h r", h=16),
                      tabs_dram[n * 16:(n + 1) * 16, :].unsqueeze(0)
                      .partition_broadcast(128).squeeze(1),
                      r=[tabs_dram], w=[crep])
                S.dma("gpsimd", csmega[:],
                      csfm_dram[n * 16:(n + 1) * 16, :]
                      .rearrange("a b -> (a b)").unsqueeze(0),
                      r=[csfm_dram], w=[csmega])

                S.op("vector", lambda e: e.tensor_tensor(
                    Abf[:].rearrange("p (h f) -> p h f", h=H),
                    Hst[:].rearrange("p (h f) -> p h f", h=H),
                    bc(ridgeid[:].unsqueeze(1), (128, H, 128)), ALU.add),
                    r=[Hst, ridgeid, Abf], w=[Abf])

                qn3 = qn[:].rearrange("p (h c) -> p h c", h=H)
                S.op("vector", lambda e, qn3=qn3, crep3=crep3: e.tensor_tensor(
                    dvec2[0][:].rearrange("p (h c) -> p h c", h=H), qn3,
                    bc(crep3[:, :, 0], (128, H, C)), ALU.mult),
                    r=[qn, crep, dvec2[0]], w=[dvec2[0]])
                for h in range(H):
                    S.op("tensor", lambda e, h=h, qn=qn: e.matmul(
                        pcheb[:, h * C:(h + 1) * C], negid128b[:],
                        qn[:, h * C:(h + 1) * C],
                        start=True, stop=False, skip_group_check=True),
                        r=[negid128b, qn], w=["pcheb0", "pcheb1"])
                def emit_scores(n=n, qn=qn, kTt=kTt, csmega=csmega):
                    S.op("tensor", lambda e: e.matmul(
                        pB[0:64, 0:512], onesrow[0:1, 0:64], csmega[:, 0:512],
                        start=True, stop=True),
                        r=[onesrow, csmega], w=["pB0"])
                    S.op("tensor", lambda e: e.matmul(
                        pB[0:64, 512:1024], onesrow[0:1, 0:64],
                        csmega[:, 512:1024], start=True, stop=True),
                        r=[onesrow, csmega], w=["pB1"])
                    S.op("vector", lambda e: e.tensor_tensor(
                        SE[:].rearrange("p (h c) -> p h c", h=H),
                        pB[0:64, :].rearrange("p (h c) -> p h c", h=H),
                        bc(csctm[:, n * 16:(n + 1) * 16], (64, 16, 64)),
                        ALU.subtract),
                        r=["pB0", "pB1", csctm, SE], w=[SE])
                    S.op("gpsimd", lambda e: e.tensor_tensor(
                        SE[:].rearrange("p (h c) -> p h c", h=H),
                        SE[:].rearrange("p (h c) -> p h c", h=H),
                        bc(maskb64[:].unsqueeze(1), (64, 16, 64)), ALU.add),
                        r=[SE, maskb64], w=[SE])
                    S.op("scalar", lambda e: e.activation(
                        SE[:], SE[:], AF.Exp), r=[SE], w=[SE])
                    S.op("gpsimd", lambda e: e.tensor_tensor(
                        SE[:].rearrange("p (h c) -> p h c", h=H),
                        SE[:].rearrange("p (h c) -> p h c", h=H),
                        bc(bscorectm[:, n * 16:(n + 1) * 16], (64, 16, 64)),
                        ALU.mult),
                        r=[SE, bscorectm], w=[SE])
                    for h in range(H):
                        S.op("tensor", lambda e, h=h: e.matmul(
                            pB[0:64, h * C:(h + 1) * C],
                            kTt[:, (h // 2) * C:(h // 2 + 1) * C],
                            qn[:, h * C:(h + 1) * C], start=True, stop=True),
                            r=[kTt, qn], w=["pB0" if h < 8 else "pB1"])
                    S.op("vector", lambda e: e.tensor_tensor(
                        scoresbf[:], pB[0:64, :], SE[:], ALU.mult),
                        r=["pB0", "pB1", SE, scoresbf], w=[scoresbf])

                def omms_pb(half, vtmt=vtmt):
                    hh0 = half * 8
                    for h in range(hh0, hh0 + 8):
                        pbk = "pB0" if (h - hh0) < 4 else "pB1"
                        S.op("tensor", lambda e, h=h, hh0=hh0, vtmt=vtmt:
                             e.matmul(
                                 pB[0:64, (h - hh0) * 128:(h - hh0 + 1) * 128],
                                 scoresbf[0:64, h * C:(h + 1) * C],
                                 vtmt[:, (h // 2) * 128:(h // 2 + 1) * 128],
                                 start=True, stop=True),
                             r=[scoresbf, vtmt], w=[pbk])

                def emit_tavpre(half, n=n, vtmt=vtmt):
                    h4 = half * 4
                    vv4 = vtmt[:, h4 * 128:(h4 + 4) * 128].rearrange(
                        "p (a f) -> p a f", a=4).unsqueeze(2)
                    S.op("gpsimd", lambda e, vv4=vv4, vtmt=vtmt, h4=h4,
                         half=half: e.tensor_tensor(
                             tavpre[:, half * 1024:(half + 1) * 1024]
                             .rearrange("p (a b f) -> p a b f", a=4, b=2),
                             bc(vv4, (64, 4, 2, 128)),
                             bc(alphactm[:, n * 16 + half * 8:
                                         n * 16 + half * 8 + 8]
                                .rearrange("p (a b) -> p a b", a=4),
                                (64, 4, 2, 128)), ALU.mult),
                         r=[vtmt, alphactm, f"tav{half}"], w=[f"tav{half}"])

                def emit_sa(half):
                    S.op("vector", lambda e, half=half: e.tensor_tensor(
                        SA[0:64, 0:1024], pB[0:64, :],
                        tavpre[:, half * 1024:(half + 1) * 1024], ALU.add),
                        r=["pB0", "pB1", f"tav{half}", SA], w=[SA])

                for k in range(NI - 1):
                    dv_in, dv_out = dvec2[k % 2], dvec2[(k + 1) % 2]
                    S.op("vector", lambda e, k=k, crep3=crep3, dv_in=dv_in:
                         e.tensor_tensor(
                             SDb[:].rearrange("p (h c) -> p h c", h=H),
                             dv_in[:].rearrange("p (h c) -> p h c", h=H),
                             bc(crep3[:, :, 1 + 2 * k], (128, H, C)),
                             ALU.mult),
                         r=[dv_in, crep, SDb], w=[SDb])
                    for h in range(H):
                        S.op("tensor", lambda e, h=h, k=k, dv_in=dv_in:
                             e.matmul(
                                 pcheb[:, h * C:(h + 1) * C],
                                 Abf[:, h * 128:(h + 1) * 128],
                                 dv_in[:, h * C:(h + 1) * C],
                                 start=False, stop=(k == NI - 2),
                                 skip_group_check=True),
                             r=[Abf, dv_in], w=["pcheb0", "pcheb1"])
                    for j in range(2):
                        S.op("tensor", lambda e, j=j, k=k, dv_in=dv_in:
                             e.matmul(
                                 pacc[:, j * 512:(j + 1) * 512], id128b[:],
                                 dv_in[:, j * 512:(j + 1) * 512],
                                 start=(k == 0), stop=False,
                                 skip_group_check=True),
                             r=[id128b, dv_in], w=[f"pacc{j}"])
                    if deferred:
                        deferred.pop(0)()
                    if deferred:
                        deferred.pop(0)()
                    S.op("vector", lambda e, k=k, crep3=crep3: e.tensor_tensor(
                        SCb[:].rearrange("p (h c) -> p h c", h=H),
                        pcheb[:].rearrange("p (h c) -> p h c", h=H),
                        bc(crep3[:, :, 2 + 2 * k], (128, H, C)), ALU.mult),
                        r=["pcheb0", "pcheb1", crep, SCb], w=[SCb])
                    S.op("vector", lambda e, dv_out=dv_out: e.tensor_tensor(
                        dv_out[:], SDb[:], SCb[:], ALU.subtract),
                        r=[SCb, SDb, dv_out], w=[dv_out])
                    if k == 3 and n > 0:
                        omms(1)
                    elif k == 4:
                        S.op("vector", lambda e: e.tensor_copy(
                            Bbf[:], Bst[:]), r=[Bst, Bbf], w=[Bbf])
                    elif k == 5:
                        emit_tavpre(0)
                    elif k == 6:
                        emit_tavpre(1)
                    elif k == 11:
                        emit_scores()
                    elif k == 16:
                        omms_pb(0)
                    elif k == 17:
                        emit_sa(0)
                    elif k == 18:
                        omms_pb(1)
                dlast = dvec2[(NI - 1) % 2]
                for j in range(2):
                    S.op("tensor", lambda e, j=j, dlast=dlast: e.matmul(
                        pacc[:, j * 512:(j + 1) * 512], id128b[:],
                        dlast[:, j * 512:(j + 1) * 512],
                        start=False, stop=True, skip_group_check=True),
                        r=[id128b, dlast], w=[f"pacc{j}"])
                while deferred:
                    deferred.pop(0)()
                S.op("scalar", lambda e: e.activation(
                    xchbf[:], pacc[:], AF.Copy),
                    r=["pacc0", "pacc1", xchbf], w=[xchbf])

                S.op("scalar", lambda e, gt=gt: e.activation(
                    sgn[:], gt[:], AF.Silu), r=[gt, sgn], w=[sgn])
                S.op("gpsimd", lambda e: e.tensor_tensor(
                    sgn[:].rearrange("p (h f) -> p h f", h=16),
                    sgn[:].rearrange("p (h f) -> p h f", h=16),
                    bc(normwrep[:, 0:128].unsqueeze(1), (64, 16, 128)),
                    ALU.mult), r=[sgn, normwrep], w=[sgn])

                def omms(half):
                    hh0 = half * 8
                    for h in range(hh0, hh0 + 8):
                        pak = "pA0" if (h - hh0) < 4 else "pA1"
                        S.op("tensor", lambda e, h=h, hh0=hh0: e.matmul(
                            pA[0:64, (h - hh0) * 128:(h - hh0 + 1) * 128],
                            xchbf[:, h * C:(h + 1) * C],
                            Bbf[:, h * 128:(h + 1) * 128],
                            start=True, stop=True),
                            r=[xchbf, Bbf], w=[pak])


                def assembly_thunks(n, half, with_sa=True):
                    hh0 = half * 8
                    ops = []
                    if with_sa:
                        ops.append(lambda: emit_sa(half))
                    ops.append(lambda: S.op(
                        "vector", lambda e: e.tensor_tensor(
                            SB[0:64, 0:1024].rearrange("p (h f) -> p h f", h=8),
                            pA[0:64, :].rearrange("p (h f) -> p h f", h=8),
                            bc(expcsctm[:, n * 16 + hh0:n * 16 + hh0 + 8],
                               (64, 8, 128)), ALU.mult),
                        r=["pA0", "pA1", expcsctm, SB], w=[SB]))
                    ops.append(lambda: S.op(
                        "vector", lambda e: e.tensor_add(
                            SC[0:64, :], SB[0:64, 0:1024], SA[0:64, 0:1024]),
                        r=[SA, SB, SC], w=[SC]))
                    ops.append(lambda: S.op(
                        "vector", lambda e: e.tensor_tensor(
                            SA[0:64, 0:1024], SC[0:64, :], SC[0:64, :],
                            ALU.mult),
                        r=[SC, SA], w=[SA]))
                    ops.append(lambda: S.op(
                        "vector", lambda e: e.tensor_reduce(
                            ms[:, half * 8:(half + 1) * 8],
                            SA[0:64, 0:1024].rearrange("p (h f) -> p h f", h=8),
                            AX.X, ALU.add),
                        r=[SA, ms], w=[ms]))
                    ops.append(lambda: S.op(
                        "vector", lambda e: e.tensor_scalar(
                            ms[:, half * 8:(half + 1) * 8],
                            ms[:, half * 8:(half + 1) * 8],
                            1.0 / HD, 1e-6, ALU.mult, ALU.add),
                        r=[ms], w=[ms]))
                    ops.append(lambda: S.op(
                        "scalar", lambda e: e.activation(
                            ms[:, half * 8:(half + 1) * 8],
                            ms[:, half * 8:(half + 1) * 8], AF.Sqrt),
                        r=[ms], w=[ms]))
                    ops.append(lambda: S.op(
                        "vector", lambda e: e.reciprocal(
                            ms[:, half * 8:(half + 1) * 8],
                            ms[:, half * 8:(half + 1) * 8]),
                        r=[ms], w=[ms]))
                    ops.append(lambda: S.op(
                        "vector", lambda e: e.tensor_tensor(
                            SC[0:64, :].rearrange("p (h f) -> p h f", h=8),
                            SC[0:64, :].rearrange("p (h f) -> p h f", h=8),
                            bc(ms[:, half * 8:(half + 1) * 8], (64, 8, 128)),
                            ALU.mult),
                        r=[ms, SC], w=[SC]))
                    ops.append(lambda: S.op(
                        "vector", lambda e: e.tensor_tensor(
                            opair[(n % 2) * 64:(n % 2) * 64 + 64,
                                  half * 1024:(half + 1) * 1024],
                            SC[0:64, :],
                            sgn[:, half * 1024:(half + 1) * 1024], ALU.mult),
                        r=[SC, sgn, opair], w=[opair]))
                    return ops

                omms(0)
                dnew = (assembly_thunks(n, 0, with_sa=False)
                        + assembly_thunks(n, 1))
                if n % 2 == 1:
                    tpx = n // 2
                    for h in range(H):
                        po = (h % 2) * 512
                        pk = f"pA{h % 2}"
                        dnew.append(lambda h=h, po=po, pk=pk: S.op(
                            "tensor", lambda e: e.transpose(
                                pA[:, po:po + 128],
                                opair[:, h * 128:(h + 1) * 128],
                                id128f[:]),
                            r=[opair, id128f], w=[pk]))
                        dnew.append(lambda h=h, po=po, pk=pk: S.op(
                            "vector", lambda e: e.tensor_copy(
                                ofmsb[:, h * 128:(h + 1) * 128],
                                pA[:, po:po + 128]),
                            r=[pk, ofmsb], w=[ofmsb]))
                    dnew.append(lambda tpx=tpx: S.dma(
                        "sync", ofmspill[tpx], ofmsb[:], r=[ofmsb],
                        w=[ofmspill]))
                if DEBUG and n == 0:
                    dO = dbg_out("ochunk0_dbg", [64, 2048])
                    dnew.append(lambda dO=dO: S.dma(
                        "sync", dO[:], opair[0:64, :], r=[opair]))

                assert not deferred
                deferred = dnew

                for h in range(H):
                    S.op("vector", lambda e, h=h, n=n, ktmt=ktmt:
                         e.tensor_scalar(
                             wkb[:, h * 128:(h + 1) * 128],
                             ktmt[:, (h // 2) * 128:(h // 2 + 1) * 128],
                             wendctm[:, n * 16 + h:n * 16 + h + 1], None,
                             ALU.mult),
                         r=[ktmt, wendctm, wkb], w=[wkb])
                for h in range(H):
                    s0 = (h % 2) * 512
                    pk = "pcheb0" if h % 2 == 0 else "pcheb1"
                    S.op("tensor", lambda e, h=h, s0=s0, ktmt=ktmt: e.matmul(
                        pcheb[:, s0:s0 + 128],
                        wkb[0:64, h * 128:(h + 1) * 128],
                        ktmt[:, (h // 2) * 128:(h // 2 + 1) * 128],
                        start=True, stop=True),
                        r=[wkb, ktmt], w=[pk])
                    S.op("tensor", lambda e, h=h, s0=s0, vtmt=vtmt: e.matmul(
                        pcheb[:, s0 + 128:s0 + 256],
                        wkb[0:64, h * 128:(h + 1) * 128],
                        vtmt[:, (h // 2) * 128:(h // 2 + 1) * 128],
                        start=True, stop=True),
                        r=[wkb, vtmt], w=[pk])
                    S.op("vector", lambda e, h=h, s0=s0, crep=crep:
                         e.scalar_tensor_tensor(
                             Hst[:, h * 128:(h + 1) * 128],
                             Hst[:, h * 128:(h + 1) * 128],
                             crep[:, h * 62 + 61:h * 62 + 62],
                             pcheb[:, s0:s0 + 128], ALU.mult, ALU.add),
                         r=[Hst, crep, pk], w=[Hst])
                    S.op("vector", lambda e, h=h, s0=s0, crep=crep:
                         e.scalar_tensor_tensor(
                             Bst[:, h * 128:(h + 1) * 128],
                             Bst[:, h * 128:(h + 1) * 128],
                             crep[:, h * 62 + 61:h * 62 + 62],
                             pcheb[:, s0 + 128:s0 + 256], ALU.mult, ALU.add),
                         r=[Bst, crep, pk], w=[Bst])

                if DEBUG and n == 0:
                    dH1 = dbg_out("Hst1_dbg", [128, H * 128])
                    dB1 = dbg_out("Bst1_dbg", [128, H * 128])
                    dY = dbg_out("xch0_dbg", [128, 1024], BF16)
                    dCr = dbg_out("crep0_dbg", [128, 16 * 62])
                    S.dma("sync", dH1[:], Hst[:], r=[Hst])
                    S.dma("sync", dB1[:], Bst[:], r=[Bst])
                    S.dma("sync", dY[:], xchbf[:], r=[xchbf])
                    S.dma("sync", dCr[:], crep[:], r=[crep])
            omms(1)
            while deferred:
                deferred.pop(0)()
        S.barrier()

        _st = ExitStack()
        with _st:
            wosb = _st.enter_context(
                nc.sbuf_tensor("wosb", [128, KC * 2048], BF16))
            ofm2 = [_st.enter_context(nc.sbuf_tensor(f"ofmt{i}", [128, 2048],
                                                     BF16)) for i in range(2)]
            xt42 = [_st.enter_context(nc.sbuf_tensor("xt4_0", [128, 2048],
                                                     F32))] * 2
            ax2 = [_st.enter_context(nc.sbuf_tensor("axsb0", [128, 2048],
                                                    F32))] * 2
            pW2 = [_st.enter_context(nc.psum_tensor(f"pW{i}", [128, 512],
                                                    F32)) for i in range(2)]
            for kc in range(KC):
                S.dma("gpsimd", wosb[:, kc * 2048:(kc + 1) * 2048],
                      Wo_d[kc * 128:(kc + 1) * 128, :], r=[f"wosb{kc}"],
                      w=[f"wosb{kc}"])
            for tp in range(TP):
                ofmt, xt4, axsb = ofm2[tp % 2], xt42[tp % 2], ax2[tp % 2]
                S.dma("sync", ofmt[:], ofmspill[tp], r=[ofmspill], w=[ofmt])
                S.dma("sync", xt4[:], x_d[tp * 128:(tp + 1) * 128, :], w=[xt4])
                for dm in range(4):
                    pW = pW2[dm % 2]
                    for h in range(H):
                        S.op("tensor", lambda e, dm=dm, h=h, ofmt=ofmt, pW=pW:
                             e.matmul(
                                 pW[:], ofmt[:, h * 128:(h + 1) * 128],
                                 wosb[:, h * 2048 + dm * 512:
                                      h * 2048 + (dm + 1) * 512],
                                 start=(h == 0), stop=(h == H - 1)),
                             r=[ofmt, f"wosb{h}"], w=[pW])
                    S.op("vector", lambda e, dm=dm, xt4=xt4, axsb=axsb, pW=pW:
                         e.tensor_tensor(
                             axsb[:, dm * 512:(dm + 1) * 512], pW[:],
                             xt4[:, dm * 512:(dm + 1) * 512], ALU.add),
                         r=[pW, xt4, axsb], w=[axsb])
                S.dma("sync", attnxspill[tp], axsb[:], r=[axsb],
                      w=[attnxspill])
        S.barrier()

        _st = ExitStack()
        with _st:
            w1t2 = [_st.enter_context(nc.sbuf_tensor(f"w1t{i}", [128, 2048],
                                                     F32)) for i in range(3)]
            w1qb2 = [_st.enter_context(nc.sbuf_tensor(f"w1qb{i}", [128, 2048],
                                                      BF16)) for i in range(2)]
            w1qT2 = [_st.enter_context(nc.sbuf_tensor(f"w1qT{i}", [128, 2048],
                                                      BF16)) for i in range(2)]
            w2t2 = [_st.enter_context(nc.sbuf_tensor(f"w2t{i}", [128, 2048],
                                                     F32)) for i in range(2)]
            w2qb2 = [_st.enter_context(nc.sbuf_tensor(f"w2qb{i}", [128, 2048],
                                                      BF16)) for i in range(2)]
            w2st = _st.enter_context(nc.sbuf_tensor("w2st", [128, 2048], BF16))
            qm1 = _st.enter_context(nc.sbuf_tensor("qm1", [128, 2048], F32))
            qs1 = _st.enter_context(nc.sbuf_tensor("qs1", [128, 2048], BF16))
            qm2 = _st.enter_context(nc.sbuf_tensor("qm2", [128, 2048], F32))
            qs2 = _st.enter_context(nc.sbuf_tensor("qs2", [128, 2048], BF16))
            b1sb = _st.enter_context(nc.sbuf_tensor("b1sb", [128, 64], F32))
            hbf2 = [_st.enter_context(nc.sbuf_tensor(f"hbf{i}", [128, 512],
                                                     BF16)) for i in range(3)]
            ph52 = [_st.enter_context(nc.psum_tensor(f"ph5_{i}", [128, 512],
                                                     F32)) for i in range(3)]
            pwt2 = [_st.enter_context(nc.psum_tensor(f"pwt{i}", [128, 512],
                                                     BF16)) for i in range(2)]
            pb1 = _st.enter_context(nc.psum_tensor("pb1", [128, 64], F32))

            S.dma("sync", SA[0:64, 0:128],
                  b1_d.ap().rearrange("(a b) -> a b", b=128), r=[SA], w=[SA])
            S.op("tensor", lambda e: e.transpose(pb1[:], SA[0:64, 0:128],
                                                 id64f[:]),
                 r=[SA, id64f], w=[pb1])
            S.op("vector", lambda e: e.tensor_copy(b1sb[:], pb1[:]),
                 r=[pb1], w=[b1sb])

            def quant(wt, wqb, tcol, qm, qs):
                S.op("scalar", lambda e: e.activation(qm[:], wt[:], AF.Abs),
                     r=[wt, qm], w=[qm])
                S.op("scalar", lambda e: e.activation(qs[:], wt[:], AF.Sign),
                     r=[wt, qs], w=[qs])
                S.op("vector", lambda e: e.scalar_tensor_tensor(
                    wqb[:], qm[:], tcol, qs[:], ALU.is_ge, ALU.mult),
                    r=[qm, qs, vecs, wqb], w=[wqb])

            def transpose16(wqb, dst, dstkey):
                for g in range(4):
                    pwt = pwt2[g % 2]
                    for j in range(4):
                        S.op("tensor", lambda e, g=g, j=j, wqb=wqb, pwt=pwt:
                             e.transpose(
                                 pwt[:, j * 128:(j + 1) * 128],
                                 wqb[:, (g * 4 + j) * 128:
                                     (g * 4 + j + 1) * 128], id128b[:]),
                             r=[wqb, id128b], w=[pwt])
                    S.op("vector", lambda e, g=g, dst=dst, pwt=pwt:
                         e.tensor_copy(dst[:, g * 512:(g + 1) * 512], pwt[:]),
                         r=[pwt, dstkey], w=[dstkey])

            def w1_mms(fb):
                w1qT = w1qT2[fb % 2]
                for t5 in range(NT):
                    ph5 = ph52[(fb * NT + t5) % 3]
                    hbf = hbf2[(fb * NT + t5) % 3]
                    for kc in range(KC):
                        S.op("tensor", lambda e, kc=kc, t5=t5, w1qT=w1qT,
                             ph5=ph5: e.matmul(
                                 ph5[:], w1qT[:, kc * 128:(kc + 1) * 128],
                                 xn[:, kc * L + t5 * 512:
                                    kc * L + (t5 + 1) * 512],
                                 start=(kc == 0), stop=(kc == KC - 1)),
                             r=[w1qT, "xn"], w=[ph5])
                    S.op("scalar", lambda e, fb=fb, ph5=ph5, hbf=hbf:
                         e.activation(hbf[:], ph5[:], AF.Gelu,
                                      bias=b1sb[:, fb:fb + 1]),
                         r=[ph5, b1sb, hbf], w=[hbf])
                    dstv = hspill.ap()[t5 * 4:(t5 + 1) * 4, :, fb, :] \
                        .transpose([1, 0, 2])
                    S.dma("gpsimd", dstv,
                          hbf[:].rearrange("p (a c) -> p a c", a=4),
                          r=[hbf], w=[hspill])

            for fb in range(FB):
                w1t, w1qb, w1qT = w1t2[fb % 3], w1qb2[fb % 2], w1qT2[fb % 2]
                S.dma("sync", w1t[:], w1_d[fb * 128:(fb + 1) * 128, :],
                      w=[w1t])
                quant(w1t, w1qb, vecs[:, 4:5], qm1, qs1)
                r_, cc = fb // 4, fb % 4
                qd, dmb = r_ // 4, r_ % 4
                w2t, w2qb = w2t2[fb % 2], w2qb2[fb % 2]
                S.dma("scalar", w2t[:],
                      w2_d[r_ * 128:(r_ + 1) * 128,
                           cc * 2048:(cc + 1) * 2048], w=[w2t])
                if fb > 0:
                    w1_mms(fb - 1)
                quant(w2t, w2qb, vecs[:, 5:6], qm2, qs2)
                transpose16(w1qb, w1qT, w1qT)
                transpose16(w2qb, w2st, w2st)
                dstv2 = w2qT_dram.ap()[qd].rearrange(
                    "p (f c) -> p f c", f=64)[:, cc * 16:(cc + 1) * 16,
                                              dmb * 128:(dmb + 1) * 128]
                S.dma("scalar", dstv2,
                      w2st[:].rearrange("p (f c) -> p f c", f=16),
                      r=[w2st], w=[w2qT_dram])
            w1_mms(FB - 1)
        S.barrier()
        es_xn.close()

        _st = ExitStack()
        with _st:
            w2qT = _st.enter_context(
                nc.sbuf_tensor("w2qT", [128, 32 * 1024], BF16))
            ht2 = [_st.enter_context(nc.sbuf_tensor(f"htile{i}", [128, 8192],
                                                    BF16)) for i in range(2)]
            axt2 = [_st.enter_context(nc.sbuf_tensor(f"axt{i}", [128, 512],
                                                     F32)) for i in range(2)]
            b2rep = _st.enter_context(nc.sbuf_tensor("b2rep", [128, 2048], F32))
            outs2 = [_st.enter_context(nc.sbuf_tensor(f"outsb{i}", [128, 512],
                                                      F32)) for i in range(2)]
            po2 = [_st.enter_context(nc.psum_tensor(f"pout{i}", [128, 512],
                                                    F32)) for i in range(2)]
            S.dma("gpsimd", b2rep[:],
                  b2_d.ap().unsqueeze(0).partition_broadcast(128).squeeze(1),
                  w=[b2rep])
            for qd in range(4):
                for j in range(4):
                    S.dma("sync",
                          w2qT[:, j * 8192:(j + 1) * 8192].rearrange(
                              "p (a c) -> p a c", a=8),
                          w2qT_dram.ap()[qd, :, j * 8192:(j + 1) * 8192]
                          .rearrange("p (a c) -> p a c", a=8),
                          r=[w2qT_dram, f"w2q{j}"], w=[f"w2q{j}"])
                for tp in range(TP):
                    htile, axt = ht2[tp % 2], axt2[tp % 2]
                    pout, outsb = po2[tp % 2], outs2[tp % 2]
                    S.dma("scalar", htile[:].rearrange("p (a c) -> p a c",
                                                       a=FB),
                          hspill.ap()[tp], r=[hspill], w=[htile])
                    S.dma("sync", axt[:],
                          attnxspill.ap()[tp, :, qd * 512:(qd + 1) * 512],
                          r=[attnxspill], w=[axt])
                    for fb in range(FB):
                        S.op("tensor", lambda e, fb=fb, htile=htile, pout=pout:
                             e.matmul(
                                 pout[:],
                                 htile[:, fb * 128:(fb + 1) * 128],
                                 w2qT[:, fb * 512:(fb + 1) * 512],
                                 start=(fb == 0), stop=(fb == FB - 1)),
                             r=[htile, f"w2q{fb // 16}"], w=[pout])
                    S.op("vector", lambda e, pout=pout, axt=axt, outsb=outsb:
                         e.tensor_tensor(outsb[:], pout[:], axt[:], ALU.add),
                         r=[pout, axt, outsb], w=[outsb])
                    S.op("vector", lambda e, qd=qd, outsb=outsb:
                         e.tensor_tensor(
                             outsb[:], outsb[:],
                             b2rep[:, qd * 512:(qd + 1) * 512], ALU.add),
                         r=[outsb, b2rep], w=[outsb])
                    S.dma("sync",
                          out_d[tp * 128:(tp + 1) * 128,
                                qd * 512:(qd + 1) * 512], outsb[:],
                          r=[outsb], w=[out_d])

        guards = S.emit()
        for g in guards:
            pass

    return nc


def _consts():
    j = np.arange(64)
    c = np.arange(64)
    maskb = np.where(j[:, None] > c[None, :], np.float32(-1e30),
                     np.float32(0.0)).astype(np.float32)
    triu = (j[:, None] <= c[None, :]).astype(np.float32)
    return dict(
        id128b=np.eye(128, dtype=ml_dtypes.bfloat16),
        negid128b=(-np.eye(128)).astype(ml_dtypes.bfloat16),
        id64f=np.eye(64, dtype=np.float32),
        id128f=np.eye(128, dtype=np.float32),
        triu64=triu,
        ones64=np.ones((64, 64), np.float32),
        onesrow=np.ones((1, 128), np.float32),
        ones128c=np.ones((128, 1), np.float32),
        maskb64=maskb,
        ridgeid=(RIDGE * np.eye(128)).astype(np.float32),
    )


def kernel(L0_attn_hidden_states, L0_attn_Wq, L0_attn_Wk, L0_attn_Wv,
           L0_attn_conv_q, L0_attn_conv_k, L0_attn_conv_v, L0_attn_Wa,
           L0_attn_A_log, L0_attn_dt_bias, L0_attn_Wb, L0_attn_bb,
           L0_attn_Walpha, L0_attn_balpha, L0_attn_Wg, L0_attn_norm_w,
           L0_attn_Wo, L0_ffn_w1, L0_ffn_b1, L0_ffn_w2, L0_ffn_b2):
    nc = build_program()
    consts = _consts()
    normwrep = np.tile(np.asarray(L0_attn_norm_w, np.float32)[None, :],
                       (64, 8)).astype(np.float32)
    x = np.ascontiguousarray(np.asarray(L0_attn_hidden_states, np.float32))
    shared = dict(
        Wq=np.asarray(L0_attn_Wq, np.float32),
        Wk=np.asarray(L0_attn_Wk, np.float32),
        Wv=np.asarray(L0_attn_Wv, np.float32),
        conv_q=np.asarray(L0_attn_conv_q, np.float32),
        conv_k=np.asarray(L0_attn_conv_k, np.float32),
        conv_v=np.asarray(L0_attn_conv_v, np.float32),
        Wa=np.asarray(L0_attn_Wa, np.float32),
        A_log=np.asarray(L0_attn_A_log, np.float32),
        dt_bias=np.asarray(L0_attn_dt_bias, np.float32),
        Wb=np.asarray(L0_attn_Wb, np.float32),
        bb=np.asarray(L0_attn_bb, np.float32),
        Walpha=np.asarray(L0_attn_Walpha, np.float32),
        balpha=np.asarray(L0_attn_balpha, np.float32),
        Wg=np.asarray(L0_attn_Wg, np.float32),
        Wo=np.asarray(L0_attn_Wo, np.float32),
        w1=np.asarray(L0_ffn_w1, np.float32),
        b1=np.asarray(L0_ffn_b1, np.float32),
        w2=np.asarray(L0_ffn_w2, np.float32),
        b2=np.asarray(L0_ffn_b2, np.float32),
        normwrep=normwrep,
        **consts,
    )
    in_maps = []
    for core in range(NCORES):
        m = dict(shared)
        m["x"] = x[core]
        m["w1sl"] = np.ascontiguousarray(
            shared["w1"][core * (DFF // NCORES):(core + 1) * (DFF // NCORES), :])
        m["w2sl"] = np.ascontiguousarray(
            shared["w2"][:, core * (DFF // NCORES):(core + 1) * (DFF // NCORES)])
        in_maps.append(m)
    res = run_bass_kernel_spmd(nc, in_maps, core_ids=list(range(NCORES)))
    global LAST_RES
    LAST_RES = res
    out = np.stack([r["out"] for r in res.results], axis=0)
    return out


LAST_RES = None


# version 28
# speedup vs baseline: 1.1999x; 1.0014x over previous
from contextlib import ExitStack

import numpy as np
import ml_dtypes
import concourse.bass as bass
import concourse.mybir as mybir
from concourse.bass_utils import run_bass_kernel_spmd

F32 = mybir.dt.float32
BF16 = mybir.dt.bfloat16
AF = mybir.ActivationFunctionType
ALU = mybir.AluOpType
AX = mybir.AxisListType

L = 2048
D = 2048
H = 16
HK = 8
HD = 128
C = 64
NC = 32
KC = 16
NT = 4
TP = 16
DFF = 8192
FB = 64
NI = 30
RIDGE = 0.02
NCORES = 8

DEBUG = False


class Flow:
    OPENGS = ("tensor", "vector", "scalar", "gpsimd")
    DMAENGS = ("sync", "gpsimd", "scalar")
    SERIAL = False

    def __init__(self, nc):
        self.nc = nc
        self.steps = []
        self.counts = {}
        self.track = {}
        self.barrier_deps = []

    @staticmethod
    def _key(h):
        return h if isinstance(h, str) else id(h)

    def _emit(self, eng, kind, fn, r, w, extra):
        deps = {}

        def add(tok):
            if tok[1] == "dma":
                k = (tok[0], "dma", tok[2] % Flow.DPOOL)
            else:
                k = (tok[0], "op")
            if k not in deps or deps[k][2] < tok[2]:
                deps[k] = tok
        for t in self.barrier_deps:
            add(t)
        for t in (extra or ()):
            add(t)
        if Flow.SERIAL and self.steps:
            add(self.steps[-1][4])
        for h in (r or ()):
            ent = self.track.get(self._key(h))
            if ent:
                for t in ent["w"]:
                    add(t)
        for h in (w or ()):
            ent = self.track.get(self._key(h))
            if ent:
                for t in ent["w"]:
                    add(t)
                for t in ent["r"]:
                    add(t)
        gidx = len(self.steps)
        idx = self.counts.get((eng, kind), 0)
        self.counts[(eng, kind)] = idx + 1
        tok = (eng, kind, idx, gidx)
        dl = [t for t in deps.values() if t[3] < gidx]
        self.steps.append((eng, kind, fn, dl, tok))
        for h in (r or ()):
            self.track.setdefault(self._key(h), {"w": [], "r": []})["r"].append(tok)
        for h in (w or ()):
            ent = self.track.setdefault(self._key(h), {"w": [], "r": []})
            if ent["r"]:
                ent["w"] = [tok]
                ent["r"] = []
            else:
                ent["w"].append(tok)
        return tok

    def op(self, eng, fn, r=(), w=(), deps=()):
        return self._emit(eng, "op", fn, r, w, deps)

    def dma(self, eng, out, in_, r=(), w=(), deps=(), **kw):
        return self._emit(eng, "dma",
                          lambda e: e.dma_start(out, in_, **kw), r, w, deps)

    def barrier(self):
        last = {}
        for (eng, kind, _, _, tok) in self.steps:
            last[(eng, kind)] = tok
        self.barrier_deps = list(last.values())

    DPOOL = 16

    def emit(self):
        nc = self.nc
        steps = self.steps
        sem_keys = sorted(self.counts.keys())
        guards = []
        sems = {}
        finals = {}
        for k in sem_keys:
            if k[1] == "op":
                g = nc.semaphore(f"fl_{k[0]}_op")
                sems[k] = g.__enter__()
                guards.append(g)
                finals[k] = self.counts[k]
            else:
                npool = min(Flow.DPOOL, self.counts[k])
                for s in range(npool):
                    g = nc.semaphore(f"fl_{k[0]}_dma{s}")
                    sems[(k[0], "dma", s)] = g.__enter__()
                    guards.append(g)
                    finals[(k[0], "dma", s)] = \
                        16 * ((self.counts[k] - s + Flow.DPOOL - 1)
                              // Flow.DPOOL)

        def dkey(d):
            if d[1] == "op":
                return (d[0], "op"), d[2] + 1
            return (d[0], "dma", d[2] % Flow.DPOOL), 16 * (d[2] // Flow.DPOOL + 1)

        with nc.Block() as block:
            def body_for(engname):
                def body(eng):
                    hw = {}
                    for (en, kind, fn, deps, tok) in steps:
                        if en != engname:
                            continue
                        for d in deps:
                            if (d[0], d[1]) == (engname, "op") and \
                                    engname == "tensor" and kind == "op":
                                continue
                            sk, need = dkey(d)
                            if hw.get(sk, 0) >= need:
                                continue
                            eng.wait_ge(sems[sk], need)
                            hw[sk] = need
                        if kind == "op":
                            fn(eng).then_inc(sems[(en, "op")], 1)
                        else:
                            idx = tok[2]
                            if idx >= Flow.DPOOL:
                                sk = (en, "dma", idx % Flow.DPOOL)
                                need = 16 * (idx // Flow.DPOOL)
                                if hw.get(sk, 0) < need:
                                    eng.wait_ge(sems[sk], need)
                                    hw[sk] = need
                            fn(eng).then_inc(
                                sems[(en, "dma", idx % Flow.DPOOL)], 16)
                    for k, v in finals.items():
                        eng.wait_ge(sems[k], v)
                return body
            for engname in ("sync", "gpsimd", "vector", "scalar", "tensor"):
                getattr(block, engname)(body_for(engname))
        return guards


def bc(ap, shape):
    return ap.broadcast_to(list(shape))


def build_program():
    nc = bass.Bass("TRN2", target_bir_lowering=False)
    S = Flow(nc)

    def inp(name, shape, dt=F32):
        return nc.dram_tensor(name, list(shape), dt, kind="ExternalInput")

    x_d = inp("x", [L, D])
    Wq_d = inp("Wq", [D, 2048])
    Wk_d = inp("Wk", [D, 1024])
    Wv_d = inp("Wv", [D, 1024])
    cq_d = inp("conv_q", [2048, 4])
    ck_d = inp("conv_k", [1024, 4])
    cv_d = inp("conv_v", [1024, 4])
    Wa_d = inp("Wa", [D, H])
    Alog_d = inp("A_log", [H])
    dtb_d = inp("dt_bias", [H])
    Wb_d = inp("Wb", [D, H])
    bb_d = inp("bb", [H])
    Wal_d = inp("Walpha", [D, H])
    bal_d = inp("balpha", [H])
    Wg_d = inp("Wg", [D, 2048])
    Wo_d = inp("Wo", [2048, D])
    w1_d = inp("w1", [DFF, D])
    b1_d = inp("b1", [DFF])
    w2_d = inp("w2", [D, DFF])
    b2_d = inp("b2", [D])
    w1sl_d = inp("w1sl", [DFF // NCORES, D])
    w2sl_d = inp("w2sl", [D, DFF // NCORES])
    id128b_d = inp("id128b", [128, 128], BF16)
    negid128b_d = inp("negid128b", [128, 128], BF16)
    id64f_d = inp("id64f", [64, 64])
    id128f_d = inp("id128f", [128, 128])
    triu64_d = inp("triu64", [64, 64])
    ones64_d = inp("ones64", [64, 64])
    onesrow_d = inp("onesrow", [1, 128])
    ones128c_d = inp("ones128c", [128, 1])
    maskb64_d = inp("maskb64", [64, 64])
    ridgeid_d = inp("ridgeid", [128, 128])
    normwrep_d = inp("normwrep", [64, 1024])

    out_d = nc.dram_tensor("out", [L, D], F32, kind="ExternalOutput")

    spill_kind = "ExternalOutput" if DEBUG else "Internal"
    qspill = nc.dram_tensor("qspill", [NC, 128, H * C], BF16, kind=spill_kind)
    kTspill = nc.dram_tensor("kTspill", [NC, 128, HK * C], BF16, kind=spill_kind)
    ktmspill = nc.dram_tensor("ktmspill", [NC, C, HK * HD], BF16, kind=spill_kind)
    vtmspill = nc.dram_tensor("vtmspill", [NC, C, HK * HD], BF16, kind=spill_kind)
    gspill = nc.dram_tensor("gspill", [NC, C, H * HD], BF16, kind=spill_kind)
    ofmspill = nc.dram_tensor("ofmspill", [TP, 128, H * 128], BF16, kind=spill_kind)
    attnxspill = nc.dram_tensor("attnxspill", [TP, 128, D], F32, kind=spill_kind)
    hspill = nc.dram_tensor("hspill", [TP, 128, FB, 128], BF16)
    w2qT_dram = nc.dram_tensor("w2qT_dram", [4, 128, 32 * 1024], BF16)
    tabsrows = nc.dram_tensor("tabsrows", [62, 512], F32, kind=spill_kind)
    tabs_dram = nc.dram_tensor("tabs_dram", [512, 62], F32)
    csfm_dram = nc.dram_tensor("csfm_dram", [512, 64], F32, kind=spill_kind)
    ccin = nc.dram_tensor("ccin", [2, 1], F32)
    ccout = nc.dram_tensor("ccout", [2, 1], F32)

    dbg = {}

    def dbg_out(name, shape, dt=F32):
        if not DEBUG:
            return None
        dbg[name] = nc.dram_tensor(name, list(shape), dt, kind="ExternalOutput")
        return dbg[name]

    es_all = ExitStack()

    def sb(name, shape, dt=F32, **kw):
        return es_all.enter_context(nc.sbuf_tensor(name, list(shape), dt, **kw))

    with es_all:
        es_xn = ExitStack()
        xn = es_xn.enter_context(
            nc.sbuf_tensor("xn", [128, KC * L], BF16, side="right"))
        id128b = sb("id128b_s", [128, 128], BF16)
        negid128b = sb("negid128b_s", [128, 128], BF16)
        id64f = sb("id64f_s", [64, 64])
        id128f = sb("id128f_s", [128, 128])
        triu64 = sb("triu64_s", [64, 64])
        ones64 = sb("ones64_s", [64, 64])
        onesrow = sb("onesrow_s", [1, 128])
        ones128c = sb("ones128c_s", [128, 1])
        maskb64 = sb("maskb64_s", [64, 64])
        ridgeid = sb("ridgeid_s", [128, 128])
        normwrep = sb("normwrep_s", [64, 1024])
        vecs = sb("vecs", [128, 8])
        glogctm = sb("glogctm", [64, 512])
        csctm = sb("csctm", [64, 512])
        wendctm = sb("wendctm", [64, 512])
        bscorectm = sb("bscorectm", [64, 512])
        expcsctm = sb("expcsctm", [64, 512])
        alphactm = sb("alphactm", [64, 512])
        nkstage = sb("nkstage", [128, 128])
        nk2 = sb("nk2", [64, 256])
        SA = sb("SA", [128, 2048])
        SB = sb("SB", [128, 2048])
        SC = sb("SC", [128, 1024])

        CONSTS = [id128b, id64f, id128f, triu64, ones64, onesrow, ones128c,
                  maskb64, ridgeid, normwrep]

        for t, d in ((id128b, id128b_d), (negid128b, negid128b_d),
                     (id64f, id64f_d), (id128f, id128f_d),
                     (triu64, triu64_d), (ones64, ones64_d),
                     (onesrow, onesrow_d), (ones128c, ones128c_d),
                     (maskb64, maskb64_d), (ridgeid, ridgeid_d),
                     (normwrep, normwrep_d)):
            S.dma("sync", t[:], d[:], w=[t])
        S.op("vector", lambda e: e.memset(vecs[:], 0.0), w=[vecs])
        S.dma("sync", vecs[0:16, 0:1], dtb_d.ap().unsqueeze(1), r=[vecs], w=[vecs])
        S.dma("sync", vecs[32:48, 1:2], bb_d.ap().unsqueeze(1), r=[vecs], w=[vecs])
        S.dma("sync", vecs[64:80, 2:3], bal_d.ap().unsqueeze(1), r=[vecs], w=[vecs])
        S.dma("sync", vecs[0:16, 3:4], Alog_d.ap().unsqueeze(1), r=[vecs], w=[vecs])
        S.op("scalar", lambda e: e.activation(vecs[0:16, 3:4], vecs[0:16, 3:4],
                                              AF.Exp), r=[vecs], w=[vecs])
        S.op("vector", lambda e: e.tensor_scalar_mul(vecs[0:16, 3:4],
                                                     vecs[0:16, 3:4], -1.0),
             r=[vecs], w=[vecs])

        _st = ExitStack()
        with _st:
            wt0 = [_st.enter_context(nc.sbuf_tensor(f"wtile0_{i}", [128, 2048],
                                                    F32)) for i in range(2)]
            abscols = _st.enter_context(nc.sbuf_tensor("abscols", [128, 28], F32))
            p0a = _st.enter_context(nc.psum_tensor("p0a", [128, 512], F32))
            for j in range(8):
                b = wt0[j % 2]
                S.dma("sync", b[:], w1sl_d[j * 128:(j + 1) * 128, :], w=[b])
                S.op("scalar", lambda e, j=j, b=b: e.activation(
                    SA[:], b[:], AF.Abs, accum_out=abscols[:, j:j + 1]),
                    r=[b], w=[SA, abscols])
            for j in range(16):
                b = wt0[j % 2]
                S.dma("sync", b[:, 0:1024], w2sl_d[j * 128:(j + 1) * 128, :],
                      w=[b])
                S.op("scalar", lambda e, j=j, b=b: e.activation(
                    SA[:, 0:1024], b[:, 0:1024], AF.Abs,
                    accum_out=abscols[:, 8 + j:9 + j]), r=[b], w=[SA, abscols])
            S.op("vector", lambda e: e.tensor_reduce(
                abscols[:, 24:25], abscols[:, 0:8].unsqueeze(1), AX.X, ALU.add),
                r=[abscols], w=[abscols])
            S.op("vector", lambda e: e.tensor_reduce(
                abscols[:, 25:26], abscols[:, 8:24].unsqueeze(1), AX.X, ALU.add),
                r=[abscols], w=[abscols])
            S.op("tensor", lambda e: e.matmul(p0a[0:2, 0:1], abscols[:, 24:26],
                                              ones128c[:], start=True, stop=True),
                 r=[abscols, ones128c], w=[p0a])
            S.op("vector", lambda e: e.tensor_copy(abscols[0:2, 26:27],
                                                   p0a[0:2, 0:1]),
                 r=[p0a], w=[abscols])
            S.dma("sync", ccin[:], abscols[0:2, 26:27], r=[abscols], w=[ccin])
            S.op("gpsimd", lambda e: e.collective_compute(
                "AllReduce", ALU.add, replica_groups=[list(range(NCORES))],
                ins=[ccin.ap().opt()], outs=[ccout.ap().opt()]),
                r=[ccin], w=[ccout])
            S.dma("sync", abscols[0:2, 27:28], ccout[:], r=[ccout, abscols],
                  w=[abscols])
            S.op("vector", lambda e: e.tensor_scalar_mul(
                abscols[0:2, 27:28], abscols[0:2, 27:28], 0.7 / (DFF * D)),
                r=[abscols], w=[abscols])
            p0b = _st.enter_context(nc.psum_tensor("p0b", [128, 2], F32))
            S.op("tensor", lambda e: e.transpose(p0b[0:1, 0:2],
                                                 abscols[0:2, 27:28],
                                                 id64f[0:2, 0:2]),
                 r=[abscols, id64f], w=[p0b])
            S.op("vector", lambda e: e.tensor_copy(abscols[0:1, 4:6],
                                                   p0b[0:1, 0:2]),
                 r=[p0b], w=[abscols])
            S.op("tensor", lambda e: e.matmul(p0b[:, 0:2], onesrow[:],
                                              abscols[0:1, 4:6],
                                              start=True, stop=True),
                 r=[onesrow, abscols], w=[p0b])
            S.op("vector", lambda e: e.tensor_copy(vecs[:, 4:6], p0b[:, 0:2]),
                 r=[p0b, vecs], w=[vecs])
        S.barrier()

        _st = ExitStack()
        with _st:
            xt = [_st.enter_context(nc.sbuf_tensor(f"xtile{i}", [128, 2048],
                                                   F32)) for i in range(2)]
            xnb = [_st.enter_context(nc.sbuf_tensor(f"xnbt{i}", [128, 2048],
                                                    BF16)) for i in range(2)]
            lnc = [_st.enter_context(nc.sbuf_tensor(f"lncols{i}", [128, 8],
                                                    F32)) for i in range(2)]
            pt1 = [_st.enter_context(nc.psum_tensor(f"ptr1_{i}", [128, 128],
                                                    BF16)) for i in range(2)]
            for tt in range(TP):
                xtile, xnbt, lncols = xt[tt % 2], xnb[tt % 2], lnc[tt % 2]
                S.dma("sync", xtile[:], x_d[tt * 128:(tt + 1) * 128, :],
                      w=[xtile])
                S.op("vector", lambda e, xtile=xtile, lncols=lncols:
                     e.tensor_reduce(lncols[:, 0:1], xtile[:].unsqueeze(1),
                                     AX.X, ALU.add),
                     r=[xtile], w=[lncols])
                S.op("scalar", lambda e, xtile=xtile, lncols=lncols:
                     e.activation(SA[:], xtile[:], AF.Square,
                                  accum_out=lncols[:, 1:2]),
                     r=[xtile, lncols], w=[SA, lncols])
                S.op("vector", lambda e, lncols=lncols: e.tensor_scalar(
                    lncols[:, 0:2], lncols[:, 0:2], 1.0 / D, None, ALU.mult),
                    r=[lncols], w=[lncols])
                S.op("vector", lambda e, lncols=lncols: e.tensor_tensor(
                    lncols[:, 2:3], lncols[:, 0:1], lncols[:, 0:1], ALU.mult),
                    r=[lncols], w=[lncols])
                S.op("vector", lambda e, lncols=lncols: e.tensor_tensor(
                    lncols[:, 2:3], lncols[:, 1:2], lncols[:, 2:3],
                    ALU.subtract), r=[lncols], w=[lncols])
                S.op("vector", lambda e, lncols=lncols: e.tensor_scalar_add(
                    lncols[:, 2:3], lncols[:, 2:3], 1e-5), r=[lncols], w=[lncols])
                S.op("scalar", lambda e, lncols=lncols: e.activation(
                    lncols[:, 2:3], lncols[:, 2:3], AF.Sqrt),
                    r=[lncols], w=[lncols])
                S.op("vector", lambda e, lncols=lncols: e.reciprocal(
                    lncols[:, 2:3], lncols[:, 2:3]), r=[lncols], w=[lncols])
                S.op("vector", lambda e, xtile=xtile, xnbt=xnbt, lncols=lncols:
                     e.tensor_scalar(xnbt[:], xtile[:], lncols[:, 0:1],
                                     lncols[:, 2:3], ALU.subtract, ALU.mult),
                     r=[xtile, lncols], w=[xnbt])
                for kc in range(KC):
                    ptr1 = pt1[kc % 2]
                    S.op("tensor", lambda e, kc=kc, xnbt=xnbt, ptr1=ptr1:
                         e.transpose(ptr1[:], xnbt[:, kc * 128:(kc + 1) * 128],
                                     id128b[:]),
                         r=[xnbt, id128b], w=[ptr1])
                    S.op("vector", lambda e, kc=kc, tt=tt, ptr1=ptr1:
                         e.tensor_copy(
                             xn[:, kc * L + tt * 128: kc * L + (tt + 1) * 128],
                             ptr1[:]),
                         r=[ptr1], w=["xn"])
        S.barrier()
        if DEBUG:
            d = dbg_out("xn_dbg", [128, KC * L], BF16)
            S.dma("sync", d[:], xn[:], r=["xn"])

        _st = ExitStack()
        with _st:
            wblk2 = [_st.enter_context(nc.sbuf_tensor(f"wblk{i}", [128, 2048],
                                                      BF16)) for i in range(2)]
            upre2 = [_st.enter_context(nc.sbuf_tensor(f"upre{i}", [128, 2051],
                                                      F32)) for i in range(2)]
            cacc2 = [_st.enter_context(nc.sbuf_tensor(f"cacc{i}", [128, 2048],
                                                      F32)) for i in range(2)]
            cvb2 = [_st.enter_context(nc.sbuf_tensor(f"convsb{i}", [128, 4],
                                                     F32)) for i in range(2)]
            qbf2 = [_st.enter_context(nc.sbuf_tensor(f"qbf{i}", [128, 2048],
                                                     BF16)) for i in range(2)]
            tmsb2 = [_st.enter_context(nc.sbuf_tensor(f"tmsb{i}", [128, 2048],
                                                      BF16)) for i in range(2)]
            wsm = _st.enter_context(nc.sbuf_tensor("wsm", [128, 16 * 80], BF16))
            abafm = _st.enter_context(nc.sbuf_tensor("abafm", [128, 2048], F32))
            ppj = [_st.enter_context(nc.psum_tensor(f"pproj{i}", [128, 512],
                                                    F32)) for i in range(2)]
            pt2 = [_st.enter_context(nc.psum_tensor(f"ptr2_{i}", [128, 128],
                                                    BF16)) for i in range(2)]
            ptr2f = _st.enter_context(nc.psum_tensor("ptr2f", [128, 16], F32))

            blkidx = [0]

            def load_wblock(wd, m):
                wblk = wblk2[blkidx[0] % 2]
                for kcq in range(0, KC, 4):
                    src = wd[kcq * 128:(kcq + 4) * 128,
                             m * 128:(m + 1) * 128].rearrange(
                                 "(a p) c -> p a c", p=128)
                    dstv = wblk[:, kcq * 128:(kcq + 4) * 128].rearrange(
                        "p (a c) -> p a c", a=4)
                    S.dma("gpsimd", dstv, src, w=[wblk])
                return wblk

            def mm_evac(wblk, dst, dst_key, to_upre, par=0):
                upre = upre2[par]
                for nt in range(NT):
                    pproj = ppj[nt % 2]
                    for kc in range(KC):
                        S.op("tensor", lambda e, kc=kc, nt=nt, wblk=wblk,
                             pproj=pproj: e.matmul(
                                 pproj[:], wblk[:, kc * 128:(kc + 1) * 128],
                                 xn[:, kc * L + nt * 512:
                                    kc * L + (nt + 1) * 512],
                                 start=(kc == 0), stop=(kc == KC - 1)),
                             r=[wblk, "xn"], w=[pproj])
                    if to_upre:
                        S.op("vector", lambda e, nt=nt, pproj=pproj,
                             upre=upre: e.tensor_copy(
                                 upre[:, 3 + nt * 512: 3 + (nt + 1) * 512],
                                 pproj[:]),
                             r=[pproj, upre], w=[upre])
                    else:
                        S.op("vector", lambda e, nt=nt, pproj=pproj, dst=dst:
                             e.tensor_copy(dst[:, nt * 512:(nt + 1) * 512],
                                           pproj[:]),
                             r=[pproj], w=[dst_key])

            def conv_silu(cd_d, ch, qbf, par=0):
                upre, cacc, convsb = upre2[par], cacc2[par], cvb2[par]
                S.dma("sync", convsb[:], cd_d[ch * 128:(ch + 1) * 128, :],
                      r=[convsb], w=[convsb])
                S.op("vector", lambda e, upre=upre: e.memset(upre[:, 0:3], 0.0),
                     r=[upre], w=[upre])
                S.op("vector", lambda e, upre=upre, cacc=cacc, convsb=convsb:
                     e.tensor_scalar(
                         cacc[:], upre[:, 0:2048], convsb[:, 0:1], None,
                         ALU.mult),
                     r=[upre, convsb, cacc], w=[cacc])
                for j in range(1, 4):
                    S.op("vector", lambda e, j=j, upre=upre, cacc=cacc,
                         convsb=convsb: e.scalar_tensor_tensor(
                             cacc[:], upre[:, j:j + 2048], convsb[:, j:j + 1],
                             cacc[:], ALU.mult, ALU.add),
                         r=[upre, convsb, cacc], w=[cacc])
                S.op("scalar", lambda e, qbf=qbf, cacc=cacc: e.activation(
                    qbf[:], cacc[:], AF.Silu), r=[cacc], w=[qbf])

            def transpose_tm(qbf, tmsb):
                for tt in range(TP):
                    ptr2 = pt2[tt % 2]
                    S.op("tensor", lambda e, tt=tt, qbf=qbf, ptr2=ptr2:
                         e.transpose(ptr2[:], qbf[:, tt * 128:(tt + 1) * 128],
                                     id128b[:]),
                         r=[qbf, id128b], w=[ptr2])
                    S.op("vector", lambda e, tt=tt, tmsb=tmsb, ptr2=ptr2:
                         e.tensor_copy(tmsb[:, tt * 128:(tt + 1) * 128],
                                       ptr2[:]),
                         r=[ptr2], w=[tmsb])

            def spill_tm(spill_d, colbase, tmsb):
                for par in range(2):
                    src = tmsb[par * 64:(par + 1) * 64, :].rearrange(
                        "p (a c) -> p a c", a=TP)
                    dstv = spill_d.ap().rearrange("(a b) c d -> a b c d", b=2)[
                        :, par, :, colbase:colbase + 128].transpose([1, 0, 2])
                    S.dma("sync", dstv, src, r=[tmsb], w=[spill_d])

            def spill_fm(spill_d, colbase, width, qbf):
                for nq in range(0, NC, 8):
                    src = qbf[:, nq * C:(nq + 8) * C].rearrange(
                        "p (a c) -> p a c", a=8)
                    dstv = spill_d.ap()[nq:nq + 8, :,
                                        colbase:colbase + width] \
                        .transpose([1, 0, 2])
                    S.dma("sync", dstv, src, r=[qbf], w=[spill_d])

            for h in range(H):
                qbf = qbf2[h % 2]
                wblk = load_wblock(Wq_d, h)
                mm_evac(wblk, None, None, True, h % 2)
                blkidx[0] += 1
                conv_silu(cq_d, h, qbf, h % 2)
                spill_fm(qspill, h * C, C, qbf)
            def k_tail(kh):
                qbf, tmsb = qbf2[kh % 2], tmsb2[kh % 2]
                transpose_tm(qbf, tmsb)
                spill_tm(ktmspill, kh * 128, tmsb)
                S.op("vector", lambda e, tmsb=tmsb: e.tensor_tensor(
                    SA[:], tmsb[:], tmsb[:], ALU.mult), r=[tmsb], w=[SA])
                S.op("vector", lambda e, kh=kh: e.tensor_reduce(
                    nkstage[:, kh * 16:(kh + 1) * 16],
                    SA[:].rearrange("p (a c) -> p a c", a=16), AX.X, ALU.add),
                    r=[SA, nkstage], w=[nkstage])

            for kh in range(HK):
                qbf = qbf2[kh % 2]
                wblk = load_wblock(Wk_d, kh)
                mm_evac(wblk, None, None, True, kh % 2)
                blkidx[0] += 1
                if kh > 0:
                    k_tail(kh - 1)
                conv_silu(ck_d, kh, qbf, kh % 2)
                spill_fm(kTspill, kh * C, C, qbf)
            k_tail(HK - 1)
            def v_tail(kh):
                qbf, tmsb = qbf2[kh % 2], tmsb2[kh % 2]
                transpose_tm(qbf, tmsb)
                spill_tm(vtmspill, kh * 128, tmsb)

            for kh in range(HK):
                qbf = qbf2[kh % 2]
                wblk = load_wblock(Wv_d, kh)
                mm_evac(wblk, None, None, True, kh % 2)
                blkidx[0] += 1
                if kh > 0:
                    v_tail(kh - 1)
                conv_silu(cv_d, kh, qbf, kh % 2)
            v_tail(HK - 1)
            for h in range(H):
                qbf, tmsb = qbf2[h % 2], tmsb2[h % 2]
                wblk = load_wblock(Wg_d, h)
                mm_evac(wblk, qbf, qbf, False)
                blkidx[0] += 1
                transpose_tm(qbf, tmsb)
                spill_tm(gspill, h * 128, tmsb)
            S.op("vector", lambda e: e.memset(wsm[:], 0.0), w=[wsm])
            for wd, coff in ((Wa_d, 0), (Wb_d, 32), (Wal_d, 64)):
                src = wd.ap().rearrange("(a p) c -> p a c", p=128)
                dstv = wsm[:].rearrange("p (a c) -> p a c", a=KC)[
                    :, :, coff:coff + 16]
                S.dma("gpsimd", dstv, src, r=[wsm], w=[wsm])
            for nt in range(NT):
                pproj = ppj[nt % 2]
                for kc in range(KC):
                    S.op("tensor", lambda e, kc=kc, nt=nt, pproj=pproj:
                         e.matmul(
                             pproj[0:80, :], wsm[:, kc * 80:(kc + 1) * 80],
                             xn[:, kc * L + nt * 512: kc * L + (nt + 1) * 512],
                             start=(kc == 0), stop=(kc == KC - 1)),
                         r=[wsm, "xn"], w=[pproj])
                S.op("vector", lambda e, nt=nt, pproj=pproj: e.tensor_copy(
                    abafm[0:80, nt * 512:(nt + 1) * 512], pproj[0:80, :]),
                    r=[pproj], w=[abafm])
            S.op("scalar", lambda e: e.activation(
                SA[0:16, :], abafm[0:16, :], AF.Abs, bias=vecs[0:16, 0:1]),
                r=[abafm, vecs], w=[SA])
            S.op("scalar", lambda e: e.activation(
                SA[0:16, :], SA[0:16, :], AF.Exp, scale=-1.0), r=[SA], w=[SA])
            S.op("scalar", lambda e: e.activation(
                SA[0:16, :], SA[0:16, :], AF.Ln, bias=1.0), r=[SA], w=[SA])
            S.op("scalar", lambda e: e.activation(
                SB[0:16, :], abafm[0:16, :], AF.Relu, bias=vecs[0:16, 0:1]),
                r=[abafm, vecs], w=[SB])
            S.op("vector", lambda e: e.tensor_add(SA[0:16, :], SA[0:16, :],
                                                  SB[0:16, :]),
                 r=[SA, SB], w=[SA])
            S.op("vector", lambda e: e.tensor_scalar(
                SA[0:16, :], SA[0:16, :], vecs[0:16, 3:4], None, ALU.mult),
                r=[SA, vecs], w=[SA])
            S.op("scalar", lambda e: e.activation(
                SA[32:48, :], abafm[32:48, :], AF.Sigmoid,
                bias=vecs[32:48, 1:2]), r=[abafm, vecs], w=[SA])
            S.op("scalar", lambda e: e.activation(
                SA[64:80, :], abafm[64:80, :], AF.Sigmoid,
                bias=vecs[64:80, 2:3]), r=[abafm, vecs], w=[SA])
            for (r0, ctm) in ((0, glogctm), (32, bscorectm), (64, alphactm)):
                for tt in range(TP):
                    S.op("tensor", lambda e, r0=r0, tt=tt: e.transpose(
                        ptr2f[:], SA[r0:r0 + 16, tt * 128:(tt + 1) * 128],
                        id128f[r0:r0 + 16, r0:r0 + 16]),
                        r=[SA, id128f], w=[ptr2f])
                    S.op("vector", lambda e, ctm=ctm, tt=tt: e.tensor_copy(
                        ctm[:, (2 * tt) * 16:(2 * tt + 1) * 16],
                        ptr2f[0:64, :]), r=[ptr2f], w=[ctm])
                    S.op("vector", lambda e, ctm=ctm, tt=tt: e.tensor_copy(
                        ctm[:, (2 * tt + 1) * 16:(2 * tt + 2) * 16],
                        ptr2f[64:128, :]), r=[ptr2f, ctm], w=[ctm])
        S.barrier()

        if DEBUG:
            d1 = dbg_out("glogctm_dbg", [64, 512])
            d2 = dbg_out("betactm_dbg", [64, 512])
            d3 = dbg_out("alphactm_dbg", [64, 512])
            d4 = dbg_out("nk_dbg", [128, 128])
            S.dma("sync", d1[:], glogctm[:], r=[glogctm])
            S.dma("sync", d2[:], bscorectm[:], r=[bscorectm])
            S.dma("sync", d3[:], alphactm[:], r=[alphactm])
            S.dma("sync", d4[:], nkstage[:], r=[nkstage])

        _st = ExitStack()
        with _st:
            rowA = _st.enter_context(nc.sbuf_tensor("rowA", [1, 512], F32))
            rowB = _st.enter_context(nc.sbuf_tensor("rowB", [1, 512], F32))
            rowC = _st.enter_context(nc.sbuf_tensor("rowC", [1, 512], F32))
            rowD = _st.enter_context(nc.sbuf_tensor("rowD", [1, 512], F32))
            rowE = _st.enter_context(nc.sbuf_tensor("rowE", [1, 512], F32))
            rowF = _st.enter_context(nc.sbuf_tensor("rowF", [1, 512], F32))
            rowG = _st.enter_context(nc.sbuf_tensor("rowG", [1, 512], F32))
            dgrow = _st.enter_context(nc.sbuf_tensor("dgrow", [1, 512], F32))
            trst = _st.enter_context(nc.sbuf_tensor("trst", [1, 16], F32))
            wnk = _st.enter_context(nc.sbuf_tensor("wnk", [64, 16], F32))
            tabsin = _st.enter_context(nc.sbuf_tensor("tabsin", [62, 512], F32))
            tabsT = _st.enter_context(nc.sbuf_tensor("tabsT", [128, 256], F32))
            p25a = _st.enter_context(nc.psum_tensor("p25a", [64, 512], F32))
            p25b = _st.enter_context(nc.psum_tensor("p25b", [64, 512], F32))
            p25r = _st.enter_context(nc.psum_tensor("p25r", [1, 512], F32))
            p25t = _st.enter_context(nc.psum_tensor("p25t", [128, 64], F32))

            S.op("tensor", lambda e: e.matmul(p25a[:], triu64[:], glogctm[:],
                                              start=True, stop=True),
                 r=[triu64, glogctm], w=[p25a])
            S.op("vector", lambda e: e.tensor_copy(csctm[:], p25a[:]),
                 r=[p25a], w=[csctm])
            S.op("tensor", lambda e: e.matmul(p25b[:], ones64[:], glogctm[:],
                                              start=True, stop=True),
                 r=[ones64, glogctm], w=[p25b])
            S.op("vector", lambda e: e.tensor_tensor(SA[0:64, 0:512], p25b[:],
                                                     csctm[:], ALU.subtract),
                 r=[p25b, csctm], w=[SA])
            S.op("scalar", lambda e: e.activation(SA[0:64, 0:512],
                                                  SA[0:64, 0:512], AF.Exp),
                 r=[SA], w=[SA])
            S.op("vector", lambda e: e.tensor_tensor(wendctm[:], SA[0:64, 0:512],
                                                     bscorectm[:], ALU.mult),
                 r=[SA, bscorectm], w=[wendctm])
            S.op("vector", lambda e: e.memset(vecs[:, 6:7],
                                              float(-0.5 * np.log(HD))),
                 r=[vecs], w=[vecs])
            S.op("scalar", lambda e: e.activation(
                expcsctm[:], csctm[:], AF.Exp, bias=vecs[0:64, 6:7]),
                r=[csctm, vecs], w=[expcsctm])
            S.op("vector", lambda e: e.tensor_scalar_mul(
                bscorectm[:], bscorectm[:], float(HD ** -0.5)),
                r=[bscorectm], w=[bscorectm])
            S.op("tensor", lambda e: e.matmul(p25r[:], ones64[:, 0:1],
                                              glogctm[:],
                                              start=True, stop=True),
                 r=[ones64, glogctm], w=[p25r])
            S.op("scalar", lambda e: e.activation(dgrow[:], p25r[:], AF.Exp),
                 r=[p25r], w=[dgrow])
            for t4 in range(4):
                S.op("tensor", lambda e, t4=t4: e.transpose(
                    p25t[:], csctm[:, t4 * 128:(t4 + 1) * 128], id64f[:]),
                    r=[csctm, id64f], w=[p25t])
                S.op("vector", lambda e: e.tensor_copy(SB[:, 0:64], p25t[:]),
                     r=[p25t], w=[SB])
                S.dma("sync", csfm_dram[t4 * 128:(t4 + 1) * 128, :],
                      SB[:, 0:64], r=[SB], w=[csfm_dram])
            for kh in range(HK):
                S.op("vector", lambda e, kh=kh: e.tensor_copy(
                    nk2[:, kh::16], nkstage[0:64, kh * 16:(kh + 1) * 16]),
                    r=[nkstage, nk2], w=[nk2])
                S.op("vector", lambda e, kh=kh: e.tensor_copy(
                    nk2[:, 8 + kh::16], nkstage[64:128, kh * 16:(kh + 1) * 16]),
                    r=[nkstage, nk2], w=[nk2])
            S.op("vector", lambda e: e.memset(trst[:], 0.0), w=[trst])
            for n in range(NC):
                srcnk = nk2[:, n * 8:(n + 1) * 8].unsqueeze(2)
                S.op("vector", lambda e, n=n, srcnk=srcnk: e.tensor_tensor(
                    wnk[:].rearrange("p (a b) -> p a b", a=8),
                    wendctm[:, n * 16:(n + 1) * 16].rearrange(
                        "p (a b) -> p a b", a=8),
                    bc(srcnk, (64, 8, 2)), ALU.mult),
                    r=[wendctm, nk2, wnk], w=[wnk])
                S.op("tensor", lambda e: e.matmul(p25r[:, 0:16], ones64[:, 0:1],
                                                  wnk[:], start=True, stop=True),
                     r=[ones64, wnk], w=[p25r])
                S.op("vector", lambda e, n=n: e.tensor_scalar_add(
                    rowA[:, n * 16:(n + 1) * 16], trst[:], float(RIDGE * HD)),
                    r=[trst, rowA], w=[rowA])
                S.op("vector", lambda e, n=n: e.tensor_tensor(
                    trst[:], trst[:], dgrow[:, n * 16:(n + 1) * 16], ALU.mult),
                    r=[trst, dgrow], w=[trst])
                S.op("vector", lambda e: e.tensor_tensor(
                    trst[:], trst[:], p25r[:, 0:16], ALU.add),
                    r=[trst, p25r], w=[trst])
            S.op("vector", lambda e: e.tensor_scalar(
                rowB[:], rowA[:], RIDGE, 0.5, ALU.add, ALU.mult),
                r=[rowA], w=[rowB])
            S.op("vector", lambda e: e.tensor_scalar(
                rowC[:], rowA[:], -RIDGE, 0.5, ALU.add, ALU.mult),
                r=[rowA], w=[rowC])
            S.op("vector", lambda e: e.reciprocal(rowG[:], rowC[:]),
                 r=[rowC], w=[rowG])
            S.op("vector", lambda e: e.tensor_tensor(rowD[:], rowB[:], rowG[:],
                                                     ALU.mult),
                 r=[rowB, rowG], w=[rowD])
            S.op("vector", lambda e: e.reciprocal(rowE[:], rowD[:]),
                 r=[rowD], w=[rowE])
            S.op("vector", lambda e: e.tensor_scalar_mul(rowD[:], rowD[:], 2.0),
                 r=[rowD], w=[rowD])
            S.op("vector", lambda e: e.tensor_scalar_mul(rowC[:], rowG[:], 2.0),
                 r=[rowG], w=[rowC])
            S.op("vector", lambda e: e.reciprocal(rowG[:], rowB[:]),
                 r=[rowB, rowG], w=[rowG])
            S.dma("sync", tabsrows[0:1, :], rowG[:], r=[rowG], w=[tabsrows])
            rho_cur, rho_nxt = rowE, rowF
            rowpool = [rowG, rowA, None, None]
            rowpool[2] = rowB
            for k in range(NI):
                rg = rowG if k % 2 == 0 else rowA
                S.op("vector", lambda e, rc=rho_cur, rg=rg: e.tensor_tensor(
                    rg[:], rowD[:], rc[:], ALU.subtract),
                    r=[rowD, rho_cur, rg], w=[rg])
                S.op("vector", lambda e, rn=rho_nxt, rg=rg: e.reciprocal(
                    rn[:], rg[:]),
                     r=[rg, rho_nxt], w=[rho_nxt])
                S.op("vector", lambda e, rc=rho_cur, rn=rho_nxt, rg=rg:
                     e.tensor_tensor(rg[:], rn[:], rc[:], ALU.mult),
                    r=[rho_nxt, rho_cur, rg], w=[rg])
                S.dma("sync", tabsrows[1 + 2 * k:2 + 2 * k, :], rg[:],
                      r=[rg], w=[tabsrows])
                rg2 = rowB if k % 2 == 0 else rowpool[2]
                rg2 = rowB
                S.op("vector", lambda e, rn=rho_nxt, rg2=rg2: e.tensor_tensor(
                    rg2[:], rn[:], rowC[:], ALU.mult),
                    r=[rho_nxt, rowC, rg2], w=[rg2])
                S.dma("sync", tabsrows[2 + 2 * k:3 + 2 * k, :], rg2[:],
                      r=[rg2], w=[tabsrows])
                rho_cur, rho_nxt = rho_nxt, rho_cur
            S.dma("sync", tabsrows[61:62, :], dgrow[:], r=[dgrow], w=[tabsrows])
            S.dma("sync", tabsin[:], tabsrows[:], r=[tabsrows], w=[tabsin])
            for t4 in range(4):
                S.op("tensor", lambda e, t4=t4: e.transpose(
                    p25t[:, 0:62], tabsin[:, t4 * 128:(t4 + 1) * 128],
                    id64f[0:62, 0:62]),
                    r=[tabsin, id64f], w=[p25t])
                S.op("vector", lambda e, t4=t4: e.tensor_copy(
                    tabsT[:, t4 * 62:(t4 + 1) * 62], p25t[:, 0:62]),
                    r=[p25t, tabsT], w=[tabsT])
            for t4 in range(4):
                S.dma("sync", tabs_dram[t4 * 128:(t4 + 1) * 128, :],
                      tabsT[:, t4 * 62:(t4 + 1) * 62], r=[tabsT], w=[tabs_dram])
        S.barrier()

        if DEBUG:
            d1 = dbg_out("csctm_dbg", [64, 512])
            d2 = dbg_out("wendctm_dbg", [64, 512])
            d3 = dbg_out("expcs_dbg", [64, 512])
            S.dma("sync", d1[:], csctm[:], r=[csctm])
            S.dma("sync", d2[:], wendctm[:], r=[wendctm])
            S.dma("sync", d3[:], expcsctm[:], r=[expcsctm])

        _st = ExitStack()
        with _st:
            Hst = _st.enter_context(nc.sbuf_tensor("Hst", [128, H * 128], F32))
            Bst = _st.enter_context(nc.sbuf_tensor("Bst", [128, H * 128], F32))
            Abf = _st.enter_context(nc.sbuf_tensor("Abf", [128, H * 128], BF16))
            Bbf = _st.enter_context(nc.sbuf_tensor("Bbf", [128, H * 128], BF16))
            qn2 = [_st.enter_context(nc.sbuf_tensor(f"qn{i}", [128, 1024],
                                                    BF16)) for i in range(2)]
            kT2 = [_st.enter_context(nc.sbuf_tensor(f"kT{i}", [128, 512],
                                                    BF16)) for i in range(2)]
            ktm2 = [_st.enter_context(nc.sbuf_tensor(f"ktmt{i}", [64, 1024],
                                                     BF16)) for i in range(2)]
            vtm2 = [_st.enter_context(nc.sbuf_tensor(f"vtmt{i}", [64, 1024],
                                                     BF16)) for i in range(2)]
            gt2 = [_st.enter_context(nc.sbuf_tensor(f"gt{i}", [64, 2048],
                                                    BF16)) for i in range(2)]
            crep2 = [_st.enter_context(nc.sbuf_tensor(f"crep{i}", [128, 16 * 62],
                                                      F32)) for i in range(2)]
            csm2 = [_st.enter_context(nc.sbuf_tensor(f"csmega{i}", [1, 1024],
                                                     F32)) for i in range(2)]
            dvec2 = [_st.enter_context(nc.sbuf_tensor(f"dvecbf{i}",
                                                      [128, 1024], BF16))
                     for i in range(2)]
            SCb = _st.enter_context(nc.sbuf_tensor("SCb", [128, 1024], BF16))
            SDb = _st.enter_context(nc.sbuf_tensor("SDb", [128, 1024], BF16))
            tavpre = _st.enter_context(nc.sbuf_tensor("tavpre", [64, 2048], BF16))
            sgn = _st.enter_context(nc.sbuf_tensor("sgn", [64, 2048], BF16))
            xchbf = _st.enter_context(nc.sbuf_tensor("xchbf", [128, 1024], BF16))
            scoresbf = _st.enter_context(
                nc.sbuf_tensor("scoresbf", [64, 1024], BF16))
            wkb = _st.enter_context(nc.sbuf_tensor("wkb", [64, 2048], BF16))
            opair = _st.enter_context(nc.sbuf_tensor("opair", [128, 2048], F32))
            ofmsb = _st.enter_context(nc.sbuf_tensor("ofmsb", [128, 2048], BF16))
            ms = _st.enter_context(nc.sbuf_tensor("ms", [64, 16], F32))
            SE = _st.enter_context(nc.sbuf_tensor("SE", [64, 1024], F32))
            pcheb = _st.enter_context(nc.psum_tensor("pcheb", [128, 1024], F32))
            pA = _st.enter_context(nc.psum_tensor("pA", [128, 1024], F32))
            pB = _st.enter_context(nc.psum_tensor("pB", [128, 1024], F32))
            pacc = _st.enter_context(nc.psum_tensor("pacc", [128, 1024],
                                                    F32))

            S.op("vector", lambda e: e.memset(Hst[:], 0.0), w=[Hst])
            S.op("vector", lambda e: e.memset(Bst[:], 0.0), w=[Bst])
            deferred = []

            for n in range(NC):
                qn, kTt = qn2[n % 2], kT2[n % 2]
                ktmt, vtmt = ktm2[n % 2], vtm2[n % 2]
                gt, crep, csmega = gt2[n % 2], crep2[n % 2], csm2[n % 2]
                crep3 = crep[:].rearrange("p (h r) -> p h r", h=16)
                S.dma("sync", qn[:], qspill[n], r=[qspill], w=[qn])
                S.dma("sync", kTt[:], kTspill[n], r=[kTspill], w=[kTt])
                S.dma("sync", ktmt[:], ktmspill[n], r=[ktmspill], w=[ktmt])
                S.dma("sync", vtmt[:], vtmspill[n], r=[vtmspill], w=[vtmt])
                S.dma("sync", gt[:], gspill[n], r=[gspill], w=[gt])
                S.dma("gpsimd", crep[:].rearrange("p (h r) -> p h r", h=16),
                      tabs_dram[n * 16:(n + 1) * 16, :].unsqueeze(0)
                      .partition_broadcast(128).squeeze(1),
                      r=[tabs_dram], w=[crep])
                S.dma("gpsimd", csmega[:],
                      csfm_dram[n * 16:(n + 1) * 16, :]
                      .rearrange("a b -> (a b)").unsqueeze(0),
                      r=[csfm_dram], w=[csmega])

                S.op("vector", lambda e: e.tensor_tensor(
                    Abf[:].rearrange("p (h f) -> p h f", h=H),
                    Hst[:].rearrange("p (h f) -> p h f", h=H),
                    bc(ridgeid[:].unsqueeze(1), (128, H, 128)), ALU.add),
                    r=[Hst, ridgeid, Abf], w=[Abf])

                qn3 = qn[:].rearrange("p (h c) -> p h c", h=H)
                S.op("vector", lambda e, qn3=qn3, crep3=crep3: e.tensor_tensor(
                    dvec2[0][:].rearrange("p (h c) -> p h c", h=H), qn3,
                    bc(crep3[:, :, 0], (128, H, C)), ALU.mult),
                    r=[qn, crep, dvec2[0]], w=[dvec2[0]])
                for h in range(H):
                    S.op("tensor", lambda e, h=h, qn=qn: e.matmul(
                        pcheb[:, h * C:(h + 1) * C], negid128b[:],
                        qn[:, h * C:(h + 1) * C],
                        start=True, stop=False, skip_group_check=True),
                        r=[negid128b, qn], w=["pcheb0", "pcheb1"])
                def emit_scores(n=n, qn=qn, kTt=kTt, csmega=csmega):
                    S.op("tensor", lambda e: e.matmul(
                        pB[0:64, 0:512], onesrow[0:1, 0:64], csmega[:, 0:512],
                        start=True, stop=True),
                        r=[onesrow, csmega], w=["pB0"])
                    S.op("tensor", lambda e: e.matmul(
                        pB[0:64, 512:1024], onesrow[0:1, 0:64],
                        csmega[:, 512:1024], start=True, stop=True),
                        r=[onesrow, csmega], w=["pB1"])
                    S.op("vector", lambda e: e.tensor_tensor(
                        SE[:].rearrange("p (h c) -> p h c", h=H),
                        pB[0:64, :].rearrange("p (h c) -> p h c", h=H),
                        bc(csctm[:, n * 16:(n + 1) * 16], (64, 16, 64)),
                        ALU.subtract),
                        r=["pB0", "pB1", csctm, SE], w=[SE])
                    S.op("gpsimd", lambda e: e.tensor_tensor(
                        SE[:].rearrange("p (h c) -> p h c", h=H),
                        SE[:].rearrange("p (h c) -> p h c", h=H),
                        bc(maskb64[:].unsqueeze(1), (64, 16, 64)), ALU.add),
                        r=[SE, maskb64], w=[SE])
                    S.op("scalar", lambda e: e.activation(
                        SE[:], SE[:], AF.Exp), r=[SE], w=[SE])
                    S.op("gpsimd", lambda e: e.tensor_tensor(
                        SE[:].rearrange("p (h c) -> p h c", h=H),
                        SE[:].rearrange("p (h c) -> p h c", h=H),
                        bc(bscorectm[:, n * 16:(n + 1) * 16], (64, 16, 64)),
                        ALU.mult),
                        r=[SE, bscorectm], w=[SE])
                    for h in range(H):
                        S.op("tensor", lambda e, h=h: e.matmul(
                            pB[0:64, h * C:(h + 1) * C],
                            kTt[:, (h // 2) * C:(h // 2 + 1) * C],
                            qn[:, h * C:(h + 1) * C], start=True, stop=True),
                            r=[kTt, qn], w=["pB0" if h < 8 else "pB1"])
                    S.op("vector", lambda e: e.tensor_tensor(
                        scoresbf[:], pB[0:64, :], SE[:], ALU.mult),
                        r=["pB0", "pB1", SE, scoresbf], w=[scoresbf])

                def omms_pb(half, vtmt=vtmt):
                    hh0 = half * 8
                    for h in range(hh0, hh0 + 8):
                        pbk = "pB0" if (h - hh0) < 4 else "pB1"
                        S.op("tensor", lambda e, h=h, hh0=hh0, vtmt=vtmt:
                             e.matmul(
                                 pB[0:64, (h - hh0) * 128:(h - hh0 + 1) * 128],
                                 scoresbf[0:64, h * C:(h + 1) * C],
                                 vtmt[:, (h // 2) * 128:(h // 2 + 1) * 128],
                                 start=True, stop=True),
                             r=[scoresbf, vtmt], w=[pbk])

                def emit_tavpre(half, n=n, vtmt=vtmt):
                    h4 = half * 4
                    vv4 = vtmt[:, h4 * 128:(h4 + 4) * 128].rearrange(
                        "p (a f) -> p a f", a=4).unsqueeze(2)
                    S.op("gpsimd", lambda e, vv4=vv4, vtmt=vtmt, h4=h4,
                         half=half: e.tensor_tensor(
                             tavpre[:, half * 1024:(half + 1) * 1024]
                             .rearrange("p (a b f) -> p a b f", a=4, b=2),
                             bc(vv4, (64, 4, 2, 128)),
                             bc(alphactm[:, n * 16 + half * 8:
                                         n * 16 + half * 8 + 8]
                                .rearrange("p (a b) -> p a b", a=4),
                                (64, 4, 2, 128)), ALU.mult),
                         r=[vtmt, alphactm, f"tav{half}"], w=[f"tav{half}"])

                def emit_sa(half):
                    S.op("vector", lambda e, half=half: e.tensor_tensor(
                        SA[0:64, 0:1024], pB[0:64, :],
                        tavpre[:, half * 1024:(half + 1) * 1024], ALU.add),
                        r=["pB0", "pB1", f"tav{half}", SA], w=[SA])

                for k in range(NI - 1):
                    dv_in, dv_out = dvec2[k % 2], dvec2[(k + 1) % 2]
                    S.op("vector", lambda e, k=k, crep3=crep3, dv_in=dv_in:
                         e.tensor_tensor(
                             SDb[:].rearrange("p (h c) -> p h c", h=H),
                             dv_in[:].rearrange("p (h c) -> p h c", h=H),
                             bc(crep3[:, :, 1 + 2 * k], (128, H, C)),
                             ALU.mult),
                         r=[dv_in, crep, SDb], w=[SDb])
                    for h in range(H):
                        S.op("tensor", lambda e, h=h, k=k, dv_in=dv_in:
                             e.matmul(
                                 pcheb[:, h * C:(h + 1) * C],
                                 Abf[:, h * 128:(h + 1) * 128],
                                 dv_in[:, h * C:(h + 1) * C],
                                 start=False, stop=(k == NI - 2),
                                 skip_group_check=True),
                             r=[Abf, dv_in], w=["pcheb0", "pcheb1"])
                    for j in range(2):
                        S.op("tensor", lambda e, j=j, k=k, dv_in=dv_in:
                             e.matmul(
                                 pacc[:, j * 512:(j + 1) * 512], id128b[:],
                                 dv_in[:, j * 512:(j + 1) * 512],
                                 start=(k == 0), stop=False,
                                 skip_group_check=True),
                             r=[id128b, dv_in], w=[f"pacc{j}"])
                    if deferred:
                        deferred.pop(0)()
                    if deferred:
                        deferred.pop(0)()
                    S.op("vector", lambda e, k=k, crep3=crep3: e.tensor_tensor(
                        SCb[:].rearrange("p (h c) -> p h c", h=H),
                        pcheb[:].rearrange("p (h c) -> p h c", h=H),
                        bc(crep3[:, :, 2 + 2 * k], (128, H, C)), ALU.mult),
                        r=["pcheb0", "pcheb1", crep, SCb], w=[SCb])
                    S.op("vector", lambda e, dv_out=dv_out: e.tensor_tensor(
                        dv_out[:], SDb[:], SCb[:], ALU.subtract),
                        r=[SCb, SDb, dv_out], w=[dv_out])
                    if k == 3 and n > 0:
                        omms(1)
                    elif k == 4:
                        S.op("vector", lambda e: e.tensor_copy(
                            Bbf[:], Bst[:]), r=[Bst, Bbf], w=[Bbf])
                    elif k == 5:
                        emit_tavpre(0)
                    elif k == 6:
                        emit_tavpre(1)
                    elif k == 11:
                        emit_scores()
                    elif k == 16:
                        omms_pb(0)
                    elif k == 17:
                        emit_sa(0)
                    elif k == 18:
                        omms_pb(1)
                dlast = dvec2[(NI - 1) % 2]
                for j in range(2):
                    S.op("tensor", lambda e, j=j, dlast=dlast: e.matmul(
                        pacc[:, j * 512:(j + 1) * 512], id128b[:],
                        dlast[:, j * 512:(j + 1) * 512],
                        start=False, stop=True, skip_group_check=True),
                        r=[id128b, dlast], w=[f"pacc{j}"])
                while deferred:
                    deferred.pop(0)()
                S.op("scalar", lambda e: e.activation(
                    xchbf[:], pacc[:], AF.Copy),
                    r=["pacc0", "pacc1", xchbf], w=[xchbf])

                S.op("scalar", lambda e, gt=gt: e.activation(
                    sgn[:], gt[:], AF.Silu), r=[gt, sgn], w=[sgn])
                S.op("gpsimd", lambda e: e.tensor_tensor(
                    sgn[:].rearrange("p (h f) -> p h f", h=16),
                    sgn[:].rearrange("p (h f) -> p h f", h=16),
                    bc(normwrep[:, 0:128].unsqueeze(1), (64, 16, 128)),
                    ALU.mult), r=[sgn, normwrep], w=[sgn])

                def omms(half):
                    hh0 = half * 8
                    for h in range(hh0, hh0 + 8):
                        pak = "pA0" if (h - hh0) < 4 else "pA1"
                        S.op("tensor", lambda e, h=h, hh0=hh0: e.matmul(
                            pA[0:64, (h - hh0) * 128:(h - hh0 + 1) * 128],
                            xchbf[:, h * C:(h + 1) * C],
                            Bbf[:, h * 128:(h + 1) * 128],
                            start=True, stop=True),
                            r=[xchbf, Bbf], w=[pak])


                def assembly_thunks(n, half, with_sa=True):
                    hh0 = half * 8
                    ops = []
                    if with_sa:
                        ops.append(lambda: emit_sa(half))
                    ops.append(lambda: S.op(
                        "vector", lambda e: e.tensor_tensor(
                            SB[0:64, 0:1024].rearrange("p (h f) -> p h f", h=8),
                            pA[0:64, :].rearrange("p (h f) -> p h f", h=8),
                            bc(expcsctm[:, n * 16 + hh0:n * 16 + hh0 + 8],
                               (64, 8, 128)), ALU.mult),
                        r=["pA0", "pA1", expcsctm, SB], w=[SB]))
                    ops.append(lambda: S.op(
                        "vector", lambda e: e.tensor_add(
                            SC[0:64, :], SB[0:64, 0:1024], SA[0:64, 0:1024]),
                        r=[SA, SB, SC], w=[SC]))
                    ops.append(lambda: S.op(
                        "vector", lambda e: e.tensor_tensor(
                            SA[0:64, 0:1024], SC[0:64, :], SC[0:64, :],
                            ALU.mult),
                        r=[SC, SA], w=[SA]))
                    ops.append(lambda: S.op(
                        "vector", lambda e: e.tensor_reduce(
                            ms[:, half * 8:(half + 1) * 8],
                            SA[0:64, 0:1024].rearrange("p (h f) -> p h f", h=8),
                            AX.X, ALU.add),
                        r=[SA, ms], w=[ms]))
                    ops.append(lambda: S.op(
                        "vector", lambda e: e.tensor_scalar(
                            ms[:, half * 8:(half + 1) * 8],
                            ms[:, half * 8:(half + 1) * 8],
                            1.0 / HD, 1e-6, ALU.mult, ALU.add),
                        r=[ms], w=[ms]))
                    ops.append(lambda: S.op(
                        "scalar", lambda e: e.activation(
                            ms[:, half * 8:(half + 1) * 8],
                            ms[:, half * 8:(half + 1) * 8], AF.Sqrt),
                        r=[ms], w=[ms]))
                    ops.append(lambda: S.op(
                        "vector", lambda e: e.reciprocal(
                            ms[:, half * 8:(half + 1) * 8],
                            ms[:, half * 8:(half + 1) * 8]),
                        r=[ms], w=[ms]))
                    ops.append(lambda: S.op(
                        "vector", lambda e: e.tensor_tensor(
                            SC[0:64, :].rearrange("p (h f) -> p h f", h=8),
                            SC[0:64, :].rearrange("p (h f) -> p h f", h=8),
                            bc(ms[:, half * 8:(half + 1) * 8], (64, 8, 128)),
                            ALU.mult),
                        r=[ms, SC], w=[SC]))
                    ops.append(lambda: S.op(
                        "vector", lambda e: e.tensor_tensor(
                            opair[(n % 2) * 64:(n % 2) * 64 + 64,
                                  half * 1024:(half + 1) * 1024],
                            SC[0:64, :],
                            sgn[:, half * 1024:(half + 1) * 1024], ALU.mult),
                        r=[SC, sgn, opair], w=[opair]))
                    return ops

                omms(0)
                dnew = (assembly_thunks(n, 0, with_sa=False)
                        + assembly_thunks(n, 1))
                if n % 2 == 1:
                    tpx = n // 2
                    for h in range(H):
                        po = (h % 2) * 512
                        pk = f"pA{h % 2}"
                        dnew.append(lambda h=h, po=po, pk=pk: S.op(
                            "tensor", lambda e: e.transpose(
                                pA[:, po:po + 128],
                                opair[:, h * 128:(h + 1) * 128],
                                id128f[:]),
                            r=[opair, id128f], w=[pk]))
                        dnew.append(lambda h=h, po=po, pk=pk: S.op(
                            "vector", lambda e: e.tensor_copy(
                                ofmsb[:, h * 128:(h + 1) * 128],
                                pA[:, po:po + 128]),
                            r=[pk, ofmsb], w=[ofmsb]))
                    dnew.append(lambda tpx=tpx: S.dma(
                        "sync", ofmspill[tpx], ofmsb[:], r=[ofmsb],
                        w=[ofmspill]))
                if DEBUG and n == 0:
                    dO = dbg_out("ochunk0_dbg", [64, 2048])
                    dnew.append(lambda dO=dO: S.dma(
                        "sync", dO[:], opair[0:64, :], r=[opair]))

                assert not deferred
                deferred = dnew

                for h in range(H):
                    S.op("vector", lambda e, h=h, n=n, ktmt=ktmt:
                         e.tensor_scalar(
                             wkb[:, h * 128:(h + 1) * 128],
                             ktmt[:, (h // 2) * 128:(h // 2 + 1) * 128],
                             wendctm[:, n * 16 + h:n * 16 + h + 1], None,
                             ALU.mult),
                         r=[ktmt, wendctm, wkb], w=[wkb])
                for h in range(H):
                    s0 = (h % 2) * 512
                    pk = "pcheb0" if h % 2 == 0 else "pcheb1"
                    S.op("tensor", lambda e, h=h, s0=s0, ktmt=ktmt: e.matmul(
                        pcheb[:, s0:s0 + 128],
                        wkb[0:64, h * 128:(h + 1) * 128],
                        ktmt[:, (h // 2) * 128:(h // 2 + 1) * 128],
                        start=True, stop=True),
                        r=[wkb, ktmt], w=[pk])
                    S.op("tensor", lambda e, h=h, s0=s0, vtmt=vtmt: e.matmul(
                        pcheb[:, s0 + 128:s0 + 256],
                        wkb[0:64, h * 128:(h + 1) * 128],
                        vtmt[:, (h // 2) * 128:(h // 2 + 1) * 128],
                        start=True, stop=True),
                        r=[wkb, vtmt], w=[pk])
                    S.op("vector", lambda e, h=h, s0=s0, crep=crep:
                         e.scalar_tensor_tensor(
                             Hst[:, h * 128:(h + 1) * 128],
                             Hst[:, h * 128:(h + 1) * 128],
                             crep[:, h * 62 + 61:h * 62 + 62],
                             pcheb[:, s0:s0 + 128], ALU.mult, ALU.add),
                         r=[Hst, crep, pk], w=[Hst])
                    S.op("vector", lambda e, h=h, s0=s0, crep=crep:
                         e.scalar_tensor_tensor(
                             Bst[:, h * 128:(h + 1) * 128],
                             Bst[:, h * 128:(h + 1) * 128],
                             crep[:, h * 62 + 61:h * 62 + 62],
                             pcheb[:, s0 + 128:s0 + 256], ALU.mult, ALU.add),
                         r=[Bst, crep, pk], w=[Bst])

                if DEBUG and n == 0:
                    dH1 = dbg_out("Hst1_dbg", [128, H * 128])
                    dB1 = dbg_out("Bst1_dbg", [128, H * 128])
                    dY = dbg_out("xch0_dbg", [128, 1024], BF16)
                    dCr = dbg_out("crep0_dbg", [128, 16 * 62])
                    S.dma("sync", dH1[:], Hst[:], r=[Hst])
                    S.dma("sync", dB1[:], Bst[:], r=[Bst])
                    S.dma("sync", dY[:], xchbf[:], r=[xchbf])
                    S.dma("sync", dCr[:], crep[:], r=[crep])
            omms(1)
            while deferred:
                deferred.pop(0)()
        S.barrier()

        _st = ExitStack()
        with _st:
            wosb = _st.enter_context(
                nc.sbuf_tensor("wosb", [128, KC * 2048], BF16))
            wof2 = [_st.enter_context(nc.sbuf_tensor(f"wof{i}", [128, 1024],
                                                     F32)) for i in range(2)]
            ofm2 = [_st.enter_context(nc.sbuf_tensor(f"ofmt{i}", [128, 2048],
                                                     BF16)) for i in range(2)]
            xt42 = [_st.enter_context(nc.sbuf_tensor("xt4_0", [128, 2048],
                                                     F32))] * 2
            ax2 = [_st.enter_context(nc.sbuf_tensor("axsb0", [128, 2048],
                                                    F32))] * 2
            pW2 = [_st.enter_context(nc.psum_tensor(f"pW{i}", [128, 512],
                                                    F32)) for i in range(2)]
            for kc in range(KC):
                for hv in range(2):
                    wof = wof2[hv]
                    S.dma("sync" if hv == 0 else "scalar", wof[:],
                          Wo_d[kc * 128:(kc + 1) * 128,
                               hv * 1024:(hv + 1) * 1024], r=[wof], w=[wof])
                    S.op("scalar", lambda e, kc=kc, hv=hv, wof=wof:
                         e.activation(
                             wosb[:, kc * 2048 + hv * 1024:
                                  kc * 2048 + (hv + 1) * 1024],
                             wof[:], AF.Copy),
                         r=[wof, f"wosb{kc}"], w=[f"wosb{kc}"])
            for tp in range(TP):
                ofmt, xt4, axsb = ofm2[tp % 2], xt42[tp % 2], ax2[tp % 2]
                S.dma("sync", ofmt[:], ofmspill[tp], r=[ofmspill], w=[ofmt])
                S.dma("sync", xt4[:], x_d[tp * 128:(tp + 1) * 128, :], w=[xt4])
                for dm in range(4):
                    pW = pW2[dm % 2]
                    for h in range(H):
                        S.op("tensor", lambda e, dm=dm, h=h, ofmt=ofmt, pW=pW:
                             e.matmul(
                                 pW[:], ofmt[:, h * 128:(h + 1) * 128],
                                 wosb[:, h * 2048 + dm * 512:
                                      h * 2048 + (dm + 1) * 512],
                                 start=(h == 0), stop=(h == H - 1)),
                             r=[ofmt, f"wosb{h}"], w=[pW])
                    S.op("vector", lambda e, dm=dm, xt4=xt4, axsb=axsb, pW=pW:
                         e.tensor_tensor(
                             axsb[:, dm * 512:(dm + 1) * 512], pW[:],
                             xt4[:, dm * 512:(dm + 1) * 512], ALU.add),
                         r=[pW, xt4, axsb], w=[axsb])
                S.dma("sync", attnxspill[tp], axsb[:], r=[axsb],
                      w=[attnxspill])
        S.barrier()

        _st = ExitStack()
        with _st:
            w1t2 = [_st.enter_context(nc.sbuf_tensor(f"w1t{i}", [128, 2048],
                                                     F32)) for i in range(3)]
            w1qb2 = [_st.enter_context(nc.sbuf_tensor(f"w1qb{i}", [128, 2048],
                                                      BF16)) for i in range(2)]
            w1qT2 = [_st.enter_context(nc.sbuf_tensor(f"w1qT{i}", [128, 2048],
                                                      BF16)) for i in range(2)]
            w2t2 = [_st.enter_context(nc.sbuf_tensor(f"w2t{i}", [128, 2048],
                                                     F32)) for i in range(2)]
            w2qb2 = [_st.enter_context(nc.sbuf_tensor(f"w2qb{i}", [128, 2048],
                                                      BF16)) for i in range(2)]
            w2st = _st.enter_context(nc.sbuf_tensor("w2st", [128, 2048], BF16))
            qm1 = _st.enter_context(nc.sbuf_tensor("qm1", [128, 2048], F32))
            qs1 = _st.enter_context(nc.sbuf_tensor("qs1", [128, 2048], BF16))
            qm2 = _st.enter_context(nc.sbuf_tensor("qm2", [128, 2048], F32))
            qs2 = _st.enter_context(nc.sbuf_tensor("qs2", [128, 2048], BF16))
            b1sb = _st.enter_context(nc.sbuf_tensor("b1sb", [128, 64], F32))
            hbf2 = [_st.enter_context(nc.sbuf_tensor(f"hbf{i}", [128, 512],
                                                     BF16)) for i in range(4)]
            ph52 = [_st.enter_context(nc.psum_tensor(f"ph5_{i}", [128, 512],
                                                     F32)) for i in range(4)]
            pwt2 = [_st.enter_context(nc.psum_tensor(f"pwt{i}", [128, 512],
                                                     BF16)) for i in range(2)]
            pb1 = _st.enter_context(nc.psum_tensor("pb1", [128, 64], F32))

            S.dma("sync", SA[0:64, 0:128],
                  b1_d.ap().rearrange("(a b) -> a b", b=128), r=[SA], w=[SA])
            S.op("tensor", lambda e: e.transpose(pb1[:], SA[0:64, 0:128],
                                                 id64f[:]),
                 r=[SA, id64f], w=[pb1])
            S.op("vector", lambda e: e.tensor_copy(b1sb[:], pb1[:]),
                 r=[pb1], w=[b1sb])

            def quant(wt, wqb, tcol, qm, qs):
                S.op("scalar", lambda e: e.activation(qm[:], wt[:], AF.Abs),
                     r=[wt, qm], w=[qm])
                S.op("scalar", lambda e: e.activation(qs[:], wt[:], AF.Sign),
                     r=[wt, qs], w=[qs])
                S.op("vector", lambda e: e.scalar_tensor_tensor(
                    wqb[:], qm[:], tcol, qs[:], ALU.is_ge, ALU.mult),
                    r=[qm, qs, vecs, wqb], w=[wqb])

            def transpose16(wqb, dst, dstkey):
                for g in range(4):
                    pwt = pwt2[g % 2]
                    for j in range(4):
                        S.op("tensor", lambda e, g=g, j=j, wqb=wqb, pwt=pwt:
                             e.transpose(
                                 pwt[:, j * 128:(j + 1) * 128],
                                 wqb[:, (g * 4 + j) * 128:
                                     (g * 4 + j + 1) * 128], id128b[:]),
                             r=[wqb, id128b], w=[pwt])
                    S.op("vector", lambda e, g=g, dst=dst, pwt=pwt:
                         e.tensor_copy(dst[:, g * 512:(g + 1) * 512], pwt[:]),
                         r=[pwt, dstkey], w=[dstkey])

            def w1_mms(fb):
                w1qT = w1qT2[fb % 2]
                for t5 in range(NT):
                    ph5 = ph52[(fb * NT + t5) % 4]
                    hbf = hbf2[(fb * NT + t5) % 4]
                    for kc in range(KC):
                        S.op("tensor", lambda e, kc=kc, t5=t5, w1qT=w1qT,
                             ph5=ph5: e.matmul(
                                 ph5[:], w1qT[:, kc * 128:(kc + 1) * 128],
                                 xn[:, kc * L + t5 * 512:
                                    kc * L + (t5 + 1) * 512],
                                 start=(kc == 0), stop=(kc == KC - 1)),
                             r=[w1qT, "xn"], w=[ph5])
                    S.op("scalar", lambda e, fb=fb, ph5=ph5, hbf=hbf:
                         e.activation(hbf[:], ph5[:], AF.Gelu,
                                      bias=b1sb[:, fb:fb + 1]),
                         r=[ph5, b1sb, hbf], w=[hbf])
                    dstv = hspill.ap()[t5 * 4:(t5 + 1) * 4, :, fb, :] \
                        .transpose([1, 0, 2])
                    S.dma("gpsimd", dstv,
                          hbf[:].rearrange("p (a c) -> p a c", a=4),
                          r=[hbf], w=[hspill])

            for fb in range(FB):
                w1t, w1qb, w1qT = w1t2[fb % 3], w1qb2[fb % 2], w1qT2[fb % 2]
                S.dma("sync", w1t[:], w1_d[fb * 128:(fb + 1) * 128, :],
                      w=[w1t])
                quant(w1t, w1qb, vecs[:, 4:5], qm1, qs1)
                r_, cc = fb // 4, fb % 4
                qd, dmb = r_ // 4, r_ % 4
                w2t, w2qb = w2t2[fb % 2], w2qb2[fb % 2]
                S.dma("scalar", w2t[:],
                      w2_d[r_ * 128:(r_ + 1) * 128,
                           cc * 2048:(cc + 1) * 2048], w=[w2t])
                if fb > 0:
                    w1_mms(fb - 1)
                quant(w2t, w2qb, vecs[:, 5:6], qm2, qs2)
                transpose16(w1qb, w1qT, w1qT)
                transpose16(w2qb, w2st, w2st)
                dstv2 = w2qT_dram.ap()[qd].rearrange(
                    "p (f c) -> p f c", f=64)[:, cc * 16:(cc + 1) * 16,
                                              dmb * 128:(dmb + 1) * 128]
                S.dma("scalar", dstv2,
                      w2st[:].rearrange("p (f c) -> p f c", f=16),
                      r=[w2st], w=[w2qT_dram])
            w1_mms(FB - 1)
        S.barrier()
        es_xn.close()

        _st = ExitStack()
        with _st:
            w2qT = _st.enter_context(
                nc.sbuf_tensor("w2qT", [128, 32 * 1024], BF16))
            ht2 = [_st.enter_context(nc.sbuf_tensor(f"htile{i}", [128, 8192],
                                                    BF16)) for i in range(2)]
            axt2 = [_st.enter_context(nc.sbuf_tensor(f"axt{i}", [128, 512],
                                                     F32)) for i in range(2)]
            b2rep = _st.enter_context(nc.sbuf_tensor("b2rep", [128, 2048], F32))
            outs2 = [_st.enter_context(nc.sbuf_tensor(f"outsb{i}", [128, 512],
                                                      F32)) for i in range(2)]
            po2 = [_st.enter_context(nc.psum_tensor(f"pout{i}", [128, 512],
                                                    F32)) for i in range(2)]
            S.dma("gpsimd", b2rep[:],
                  b2_d.ap().unsqueeze(0).partition_broadcast(128).squeeze(1),
                  w=[b2rep])
            for qd in range(4):
                for j in range(4):
                    S.dma("sync",
                          w2qT[:, j * 8192:(j + 1) * 8192].rearrange(
                              "p (a c) -> p a c", a=8),
                          w2qT_dram.ap()[qd, :, j * 8192:(j + 1) * 8192]
                          .rearrange("p (a c) -> p a c", a=8),
                          r=[w2qT_dram, f"w2q{j}"], w=[f"w2q{j}"])
                for tp in range(TP):
                    htile, axt = ht2[tp % 2], axt2[tp % 2]
                    pout, outsb = po2[tp % 2], outs2[tp % 2]
                    S.dma("scalar", htile[:].rearrange("p (a c) -> p a c",
                                                       a=FB),
                          hspill.ap()[tp], r=[hspill], w=[htile])
                    S.dma("sync", axt[:],
                          attnxspill.ap()[tp, :, qd * 512:(qd + 1) * 512],
                          r=[attnxspill], w=[axt])
                    for fb in range(FB):
                        S.op("tensor", lambda e, fb=fb, htile=htile, pout=pout:
                             e.matmul(
                                 pout[:],
                                 htile[:, fb * 128:(fb + 1) * 128],
                                 w2qT[:, fb * 512:(fb + 1) * 512],
                                 start=(fb == 0), stop=(fb == FB - 1)),
                             r=[htile, f"w2q{fb // 16}"], w=[pout])
                    S.op("vector", lambda e, pout=pout, axt=axt, outsb=outsb:
                         e.tensor_tensor(outsb[:], pout[:], axt[:], ALU.add),
                         r=[pout, axt, outsb], w=[outsb])
                    S.op("vector", lambda e, qd=qd, outsb=outsb:
                         e.tensor_tensor(
                             outsb[:], outsb[:],
                             b2rep[:, qd * 512:(qd + 1) * 512], ALU.add),
                         r=[outsb, b2rep], w=[outsb])
                    S.dma("sync",
                          out_d[tp * 128:(tp + 1) * 128,
                                qd * 512:(qd + 1) * 512], outsb[:],
                          r=[outsb], w=[out_d])

        guards = S.emit()
        for g in guards:
            pass

    return nc


def _consts():
    j = np.arange(64)
    c = np.arange(64)
    maskb = np.where(j[:, None] > c[None, :], np.float32(-1e30),
                     np.float32(0.0)).astype(np.float32)
    triu = (j[:, None] <= c[None, :]).astype(np.float32)
    return dict(
        id128b=np.eye(128, dtype=ml_dtypes.bfloat16),
        negid128b=(-np.eye(128)).astype(ml_dtypes.bfloat16),
        id64f=np.eye(64, dtype=np.float32),
        id128f=np.eye(128, dtype=np.float32),
        triu64=triu,
        ones64=np.ones((64, 64), np.float32),
        onesrow=np.ones((1, 128), np.float32),
        ones128c=np.ones((128, 1), np.float32),
        maskb64=maskb,
        ridgeid=(RIDGE * np.eye(128)).astype(np.float32),
    )


def kernel(L0_attn_hidden_states, L0_attn_Wq, L0_attn_Wk, L0_attn_Wv,
           L0_attn_conv_q, L0_attn_conv_k, L0_attn_conv_v, L0_attn_Wa,
           L0_attn_A_log, L0_attn_dt_bias, L0_attn_Wb, L0_attn_bb,
           L0_attn_Walpha, L0_attn_balpha, L0_attn_Wg, L0_attn_norm_w,
           L0_attn_Wo, L0_ffn_w1, L0_ffn_b1, L0_ffn_w2, L0_ffn_b2):
    nc = build_program()
    consts = _consts()
    normwrep = np.tile(np.asarray(L0_attn_norm_w, np.float32)[None, :],
                       (64, 8)).astype(np.float32)
    x = np.ascontiguousarray(np.asarray(L0_attn_hidden_states, np.float32))
    shared = dict(
        Wq=np.asarray(L0_attn_Wq, np.float32),
        Wk=np.asarray(L0_attn_Wk, np.float32),
        Wv=np.asarray(L0_attn_Wv, np.float32),
        conv_q=np.asarray(L0_attn_conv_q, np.float32),
        conv_k=np.asarray(L0_attn_conv_k, np.float32),
        conv_v=np.asarray(L0_attn_conv_v, np.float32),
        Wa=np.asarray(L0_attn_Wa, np.float32),
        A_log=np.asarray(L0_attn_A_log, np.float32),
        dt_bias=np.asarray(L0_attn_dt_bias, np.float32),
        Wb=np.asarray(L0_attn_Wb, np.float32),
        bb=np.asarray(L0_attn_bb, np.float32),
        Walpha=np.asarray(L0_attn_Walpha, np.float32),
        balpha=np.asarray(L0_attn_balpha, np.float32),
        Wg=np.asarray(L0_attn_Wg, np.float32),
        Wo=np.asarray(L0_attn_Wo, np.float32),
        w1=np.asarray(L0_ffn_w1, np.float32),
        b1=np.asarray(L0_ffn_b1, np.float32),
        w2=np.asarray(L0_ffn_w2, np.float32),
        b2=np.asarray(L0_ffn_b2, np.float32),
        normwrep=normwrep,
        **consts,
    )
    in_maps = []
    for core in range(NCORES):
        m = dict(shared)
        m["x"] = x[core]
        m["w1sl"] = np.ascontiguousarray(
            shared["w1"][core * (DFF // NCORES):(core + 1) * (DFF // NCORES), :])
        m["w2sl"] = np.ascontiguousarray(
            shared["w2"][:, core * (DFF // NCORES):(core + 1) * (DFF // NCORES)])
        in_maps.append(m)
    res = run_bass_kernel_spmd(nc, in_maps, core_ids=list(range(NCORES)))
    global LAST_RES
    LAST_RES = res
    out = np.stack([r["out"] for r in res.results], axis=0)
    return out


LAST_RES = None
